# version 08
# baseline (speedup 1.0000x reference)
import contextlib
import math
import numpy as np
import ml_dtypes
import concourse.bass as bass
import concourse.mybir as mybir
from concourse.bass_utils import run_bass_kernel_spmd

F32 = mybir.dt.float32
BF16 = mybir.dt.bfloat16
AF = mybir.ActivationFunctionType
OP = mybir.AluOpType
AX = mybir.AxisListType

T_TOK = 4096
S_LEN = 2048
N_TILE = 32
S_TILE = 16
E = 1024
NCH = 8
BF = ml_dtypes.bfloat16


def host_constants():
    c = {}
    c["idn"] = np.eye(128).astype(BF)
    C64 = (np.eye(64, dtype=np.float64) - 1.0 / 64.0) / 8.0
    Cb = np.zeros((128, 128), dtype=np.float64)
    Cb[:64, :64] = C64
    Cb[64:, 64:] = C64
    c["cblk"] = Cb.astype(BF)
    ind = np.zeros((128, 2), dtype=np.float32)
    ind[:64, 0] = 1.0
    ind[64:, 1] = 1.0
    c["ind2"] = ind.astype(BF)
    c["ones_row"] = np.ones((1, 128), dtype=np.float32).astype(BF)
    c["ones_col"] = np.ones((128, 1), dtype=np.float32).astype(BF)
    seln = np.zeros((64, 32 * 128), dtype=np.float32)
    for p in range(8):
        for tc in range(4):
            j = p * 4 + tc
            seln[(2 * p) * 4 + tc, j * 128:j * 128 + 64] = 1.0
            seln[(2 * p + 1) * 4 + tc, j * 128 + 64:j * 128 + 128] = 1.0
    c["seln"] = seln.astype(BF)
    self_ = np.zeros((64, 32 * 128), dtype=np.float32)
    for br in range(2):
        for p in range(4):
            for tc in range(4):
                j = br * 16 + p * 4 + tc
                self_[br * 32 + (2 * p) * 4 + tc, j * 128:j * 128 + 64] = 1.0
                self_[br * 32 + (2 * p + 1) * 4 + tc, j * 128 + 64:j * 128 + 128] = 1.0
    c["self"] = self_.astype(BF)
    half = 16
    inv = 10000.0 ** (-np.arange(half, dtype=np.float64) / half)
    pos = np.arange(S_LEN, dtype=np.float64)
    ang = pos[:, None] * inv[None, :]
    c["cos_t"] = np.ascontiguousarray(
        np.cos(ang).astype(np.float32).reshape(16, 128, 16).transpose(1, 0, 2).reshape(128, 256))
    c["sin_t"] = np.ascontiguousarray(
        np.sin(ang).astype(np.float32).reshape(16, 128, 16).transpose(1, 0, 2).reshape(128, 256))
    for k, off in enumerate((0, 128, 256, 384)):
        n = np.arange(128)[:, None]
        t = np.arange(512)[None, :]
        c[f"caus{k}"] = (t >= n + off).astype(np.float32).astype(BF)
    return c


def host_l1f_factors(G1, G2):
    G1 = np.asarray(G1, dtype=np.float32)
    G2 = np.asarray(G2, dtype=np.float32)
    UT = np.zeros((4, 512, 512), dtype=np.float32)
    V = np.zeros((4, 512, 512), dtype=np.float32)
    b_i, d_i, g_i, e_i = np.arange(32), np.arange(32), np.arange(16), np.arange(16)
    for c in range(4):
        A = G1[0, :, c, :]
        B = G2[c, :, 0, :]
        bb, dd, gg = np.meshgrid(b_i, d_i, g_i, indexing="ij")
        UT[c, bb * 16 + gg, dd * 16 + gg] = A[bb, dd]
        bb, gg, ee = np.meshgrid(b_i, g_i, e_i, indexing="ij")
        V[c, bb * 16 + gg, bb * 16 + ee] = B[ee, gg]
    return UT, V


class PSem:
    def __init__(self, handle):
        self.h = handle
        self.base = 0
        self.planned = 0

    def wait(self, e, v):
        e.wait_ge(self.h, self.base + v)

    def inc(self, ins, n=1):
        ins.then_inc(self.h, n)
        self.planned += n


class Ctx:
    def __init__(self, nc, stack):
        self.nc = nc
        self.stack = stack
        self._i = 0
        self.b = {}
        self.cb = {}
        self.tap_names = ()
        self._sem_pool = {}

    def sem(self, name=None, scope=None, kls="c"):
        pool = self._sem_pool.setdefault(kls, [])
        if scope is not None and pool:
            ps = pool.pop()
        else:
            self._i += 1
            ps = PSem(self.stack.enter_context(
                self.nc.semaphore(name or f"s{self._i}")))
        if scope is not None:
            scope.callback(self._ret_sem, ps, kls)
        return ps

    def _ret_sem(self, ps, kls="c"):
        ps.base += ps.planned
        ps.planned = 0
        self._sem_pool.setdefault(kls, []).append(ps)

    def sbuf(self, name, shape, dtype, scope=None):
        h = (scope or self.stack).enter_context(self.nc.sbuf_tensor(name, shape, dtype))
        self.b[name] = h
        return h

    def psum(self, name, shape, dtype, scope=None):
        h = (scope or self.stack).enter_context(self.nc.psum_tensor(name, shape, dtype))
        self.b[name] = h
        return h

    def tap(self, name, ap, rows, cols):
        if name not in self.tap_names:
            return
        nc = self.nc
        td = nc.dram_tensor("tap_" + name, [rows, cols], F32, kind="ExternalOutput")
        sc = contextlib.ExitStack()
        stg = self.sbuf("tapstg_" + name, [rows, cols], F32, sc)
        s = self.sem(scope=sc)
        with nc.Block() as blk:
            @blk.vector
            def _(e):
                prime(e, s)
                s.inc(e.tensor_copy(stg[:, :], ap), 1)

            @blk.sync
            def _(e):
                s.wait(e, 1)
                s.inc(e.dma_start(td.ap(), stg[:, :]), 16)
                s.wait(e, 17)
        sc.close()


def tile(x, i, w=1024):
    return x[:, i * w:(i + 1) * w]


def prime(e, *sems):
    for ps in sems:
        for x in (ps if isinstance(ps, (list, tuple)) else [ps]):
            x.wait(e, 0)


def emit_ln(ctx, blk, name, src_f, dst_f, n, src_gate=None, apply_gate=None,
            done=None, scope=None):
    st = ctx.sbuf(f"lnst_{name}", [128, 4 * n], F32, scope)
    scr = ctx.b["g_lnscr"]
    m2 = ctx.sbuf(f"lnm2_{name}", [128, n], F32, scope)
    C_SUM, C_SS, C_R, C_B = 0, n, 2 * n, 3 * n
    sf = ctx.sem(scope=scope)
    vf = ctx.sem(scope=scope)

    @blk.scalar
    def _(e):
        prime(e, sf, vf, *( [done] if done is not None else [] ))
        for j in range(n):
            if src_gate is not None:
                g = src_gate(j)
                if g is not None:
                    g[0].wait(e, g[1])
            if j >= 2:
                sf.wait(e, j - 1)
            sf.inc(e.activation(scr[:, (j % 2) * 1024:(j % 2 + 1) * 1024],
                                src_f(j), AF.Square,
                                accum_out=st[:, C_SS + j:C_SS + j + 1]), 1)
        sf.wait(e, n)
        vf.wait(e, n + 4)
        sf.inc(e.activation(st[:, C_R:C_R + n], st[:, C_SS:C_SS + n], AF.Sqrt), 1)
        vf.wait(e, n + 6)
        for j in range(n):
            if apply_gate is not None:
                g = apply_gate(j)
                if g is not None:
                    g[0].wait(e, g[1])
            ins = e.activation(dst_f(j), src_f(j), AF.Identity,
                               scale=m2[:, j:j + 1],
                               bias=st[:, C_B + j:C_B + j + 1])
            if done is not None:
                done.inc(ins, 1)

    @blk.vector
    def _(e):
        prime(e, vf, sf)
        for j in range(n):
            if src_gate is not None:
                g = src_gate(j)
                if g is not None:
                    g[0].wait(e, g[1])
            vf.inc(e.reduce_sum(st[:, C_SUM + j:C_SUM + j + 1], src_f(j),
                                axis=AX.X), 1)
        vf.wait(e, n)
        vf.inc(e.tensor_scalar(st[:, C_SUM:C_SUM + n], st[:, C_SUM:C_SUM + n],
                               1.0 / 1024.0, None, OP.mult), 1)
        sf.wait(e, n)
        vf.wait(e, n + 1)
        vf.inc(e.tensor_mul(m2[:, 0:n], st[:, C_SUM:C_SUM + n],
                            st[:, C_SUM:C_SUM + n]), 1)
        vf.wait(e, n + 2)
        vf.inc(e.scalar_tensor_tensor(st[:, C_R:C_R + n], st[:, C_SS:C_SS + n],
                                      1.0 / 1024.0, m2[:, 0:n],
                                      OP.mult, OP.subtract), 1)
        vf.wait(e, n + 3)
        vf.inc(e.tensor_scalar(st[:, C_SS:C_SS + n], st[:, C_R:C_R + n],
                               1e-5, None, OP.add), 1)
        sf.wait(e, n + 1)
        vf.wait(e, n + 4)
        vf.inc(e.reciprocal(m2[:, 0:n], st[:, C_R:C_R + n]), 1)
        vf.wait(e, n + 5)
        vf.inc(e.scalar_tensor_tensor(st[:, C_B:C_B + n], st[:, C_SUM:C_SUM + n],
                                      -1.0, m2[:, 0:n],
                                      OP.mult, OP.mult), 1)


def emit_ln_transpose(ctx, blk, name, tiles, dst, tcols, coloff,
                      src_gate=None, scope=None, sem_scope=None):
    n = len(tiles)
    nring = min(8, ((n + 3) // 4) * 4)
    ring = ctx.sbuf(f"ring_{name}", [128, nring * 1024], BF16, scope)
    pa = ctx.psum(f"pta_{name}", [128, 512], BF16, scope)
    pb = ctx.psum(f"ptb_{name}", [128, 512], BF16, scope)
    idn = ctx.b["idn"]
    s_app = ctx.sem(scope=scope)
    s_tr = ctx.sem(scope=scope)
    s_ev = ctx.sem(scope=sem_scope or scope)
    nsup = (n + 3) // 4

    def ring_gate(j):
        sup_prev = j // 4 - (nring // 4)
        if sup_prev < 0:
            return None
        return (s_ev, NCH * (sup_prev + 1))

    emit_ln(ctx, blk, name, lambda j: tile(ctx.b["xres"], tiles[j]),
            lambda j: tile(ring, j % nring), n, src_gate=src_gate,
            apply_gate=ring_gate, done=s_app, scope=scope)

    @blk.tensor
    def _(e):
        prime(e, s_app, s_tr, s_ev)
        k = 0
        for sup in range(nsup):
            j0 = sup * 4
            cnt = min(4, n - j0)
            s_app.wait(e, j0 + cnt)
            for c in range(NCH):
                buf = pa if (k % 2 == 0) else pb
                if k >= 2:
                    s_ev.wait(e, k - 1)
                for j in range(cnt):
                    ins = e.transpose(buf[:, j * 128:(j + 1) * 128],
                                      tile(ring, (j0 + j) % nring)[:, c * 128:(c + 1) * 128],
                                      idn[:, :])
                s_tr.inc(ins, 1)
                k += 1

    @blk.vector
    def _(e):
        prime(e, s_tr, s_ev)
        k = 0
        for sup in range(nsup):
            j0 = sup * 4
            cnt = min(4, n - j0)
            for c in range(NCH):
                buf = pa if (k % 2 == 0) else pb
                s_tr.wait(e, k + 1)
                s_ev.inc(e.tensor_copy(
                    dst[:, c * tcols + coloff + j0 * 128:
                        c * tcols + coloff + j0 * 128 + cnt * 128],
                    buf[:, 0:cnt * 128]), 1)
                k += 1
    return NCH * nsup, s_ev


def emit_out_ln_add(ctx, name, src, tcols, tok0, src_ready=None):
    nc = ctx.nc
    n = tcols // 128
    nring = min(8, n)
    sc = contextlib.ExitStack()
    with nc.Block() as blk:
        ring = ctx.sbuf(f"oring_{name}", [128, nring * 1024], BF16, sc)
        lnout = ctx.sbuf(f"olnout_{name}", [128, 2 * 1024], BF16, sc)
        pa = ctx.psum(f"opta_{name}", [128, 512], BF16, sc)
        pb = ctx.psum(f"optb_{name}", [128, 512], BF16, sc)
        idn = ctx.b["idn"]
        s_tr = ctx.sem(scope=sc)
        s_ev = ctx.sem(scope=sc)
        s_app = ctx.sem(scope=sc)
        s_add = ctx.sem(scope=sc)

        @blk.tensor
        def _(e):
            prime(e, s_tr, s_ev, s_app, s_add)
            if src_ready is not None:
                src_ready[0].wait(e, src_ready[1])
            k = 0
            for j in range(n):
                for cp in range(2):
                    buf = pa if (k % 2 == 0) else pb
                    if k >= 2:
                        s_ev.wait(e, k - 1)
                    if j >= nring and cp == 0:
                        s_app.wait(e, j - nring + 1)
                    for cc in range(4):
                        c = cp * 4 + cc
                        ins = e.transpose(
                            buf[:, cc * 128:(cc + 1) * 128],
                            src[:, c * tcols + j * 128: c * tcols + (j + 1) * 128],
                            idn[:, :])
                    s_tr.inc(ins, 1)
                    k += 1

        @blk.vector
        def _(e):
            prime(e, s_tr, s_ev, s_app, s_add)
            for k in range(2 * n):
                j, cp = divmod(k, 2)
                buf = pa if (k % 2 == 0) else pb
                s_tr.wait(e, k + 1)
                s_ev.inc(e.tensor_copy(tile(ring, j % nring)[:, cp * 512:(cp + 1) * 512],
                                       buf[:, 0:512]), 1)

        def src_gate(j):
            return (s_ev, 2 * (j + 1))

        def apply_gate(j):
            if j < 2:
                return None
            return (s_add, j - 1)

        emit_ln(ctx, blk, f"ln2_{name}", lambda j: tile(ring, j % nring),
                lambda j: tile(lnout, j % 2), n,
                src_gate=src_gate, apply_gate=apply_gate, done=s_app, scope=sc)

        @blk.vector
        def _(e):
            prime(e, s_add)
            for j in range(n):
                s_app.wait(e, j + 1)
                s_add.inc(e.tensor_add(tile(ctx.b["xres"], tok0 // 128 + j),
                                       tile(ctx.b["xres"], tok0 // 128 + j),
                                       tile(lnout, j % 2)), 1)
    ctx.tap(f"lnst_{name}", ctx.b[f"lnst_ln2_{name}"][:, :], 128, 4 * n)
    ctx.tap(f"ring_{name}", ctx.b[f"oring_{name}"][:, 0:1024], 128, 1024)
    sc.close()


def emit_l1_ffn(ctx):
    nc = ctx.nc
    ut_in = nc.dram_tensor("l1f_ut", [2, 4, 512, 512], F32, kind="ExternalInput")
    v_in = nc.dram_tensor("l1f_v", [2, 4, 512, 512], F32, kind="ExternalInput")
    br_in = nc.dram_tensor("l1f_br", [512], F32, kind="ExternalInput")
    bi_in = nc.dram_tensor("l1f_bi", [512], F32, kind="ExternalInput")

    sc_all = contextlib.ExitStack()
    sc_w = contextlib.ExitStack()
    wT = ctx.sbuf("l1f_wT", [128, 12 * 512], BF16, sc_all)
    bias = ctx.sbuf("l1f_bias", [128, 8], F32, sc_all)
    s_db = ctx.sem(scope=sc_all)

    with nc.Block() as blk:
        uv = ctx.sbuf("l1f_uv", [128, 4 * 512], F32, sc_w)
        uvb = ctx.sbuf("l1f_uvb", [128, 64 * 512], BF16, sc_w)
        pw = ctx.psum("l1f_pw", [128, 512], F32, sc_w)
        pw2 = ctx.psum("l1f_pw2", [128, 512], F32, sc_w)
        s_d = [ctx.sem(scope=sc_w) for _ in range(4)]
        s_c = ctx.sem(scope=sc_w)
        s_m = ctx.sem(scope=sc_w)
        s_w = ctx.sem(scope=sc_w)

        def ucol(uvsel, ri, c, kch):
            return (uvsel * 32 + ri * 16 + c * 4 + kch) * 512

        @blk.sync
        def _(e):
            prime(e, s_d, s_db, s_c)
            for uvsel in range(2):
                src = ut_in if uvsel == 0 else v_in
                for ri in range(2):
                    for c in range(4):
                        for kch in range(4):
                            k = uvsel * 32 + ri * 16 + c * 4 + kch
                            if k >= 4:
                                s_c.wait(e, k - 3)
                            s_d[k % 4].inc(e.dma_start(
                                uv[:, (k % 4) * 512:(k % 4 + 1) * 512],
                                src.ap()[ri, c, kch * 128:(kch + 1) * 128, :]), 16)
            s_db.inc(e.dma_start(bias[:, 0:4],
                                 bass.AP(br_in, 0, [[1, 128], [128, 4]]),
                                 allow_slow_non_contiguous=True), 16)
            s_db.inc(e.dma_start(bias[:, 4:8],
                                 bass.AP(bi_in, 0, [[1, 128], [128, 4]]),
                                 allow_slow_non_contiguous=True), 16)

        @blk.vector
        def _(e):
            prime(e, s_d, s_c)
            for k in range(64):
                s_d[k % 4].wait(e, 16 * (k // 4 + 1))
                s_c.inc(e.tensor_copy(uvb[:, k * 512:(k + 1) * 512],
                                      uv[:, (k % 4) * 512:(k % 4 + 1) * 512]), 1)

        @blk.tensor
        def _(e):
            prime(e, s_c, s_m, s_w)
            s_c.wait(e, 64)
            k = 0
            for ri in range(2):
                for mq in range(4):
                    buf = pw if (k % 2 == 0) else pw2
                    if k >= 2:
                        s_w.wait(e, k - 1)
                    for c in range(4):
                        for kch in range(4):
                            ins = e.matmul(
                                buf[:, :],
                                uvb[:, ucol(0, ri, c, kch) + mq * 128:
                                    ucol(0, ri, c, kch) + (mq + 1) * 128],
                                uvb[:, ucol(1, ri, c, kch):
                                    ucol(1, ri, c, kch) + 512],
                                start=(c == 0 and kch == 0),
                                stop=(c == 3 and kch == 3))
                    s_m.inc(ins, 1)
                    k += 1

        @blk.scalar
        def _(e):
            prime(e, s_m, s_w)
            for k in range(8):
                buf = pw if (k % 2 == 0) else pw2
                s_m.wait(e, k + 1)
                s_w.inc(e.activation(wT[:, k * 512:(k + 1) * 512],
                                     buf[:, :], AF.Copy), 1)
            s_w.wait(e, 8)
            for mq in range(4):
                e.activation(wT[:, (8 + mq) * 512:(9 + mq) * 512],
                             wT[:, (4 + mq) * 512:(5 + mq) * 512],
                             AF.Copy, scale=-1.0)
    ctx.tap("uvb_ut00", uvb[:, 0:512], 128, 512)
    ctx.tap("uvb_v00", uvb[:, 32 * 512:33 * 512], 128, 512)
    ctx.tap("wTr0", wT[:, 0:512], 128, 512)
    ctx.tap("wTi0", wT[:, 4 * 512:5 * 512], 128, 512)
    sc_w.close()

    sc_x = contextlib.ExitStack()
    sc_ln = contextlib.ExitStack()
    xlnT = ctx.sbuf("l1f_xlnT", [128, NCH * T_TOK], BF16, sc_x)
    with nc.Block() as blk:
        nev, s_ln = emit_ln_transpose(ctx, blk, "l1f_ln1", list(range(N_TILE)),
                                      xlnT, T_TOK, 0, scope=sc_ln, sem_scope=sc_x)
    sc_ln.close()
    ctx.tap("xlnT_c0", xlnT[:, 0:512], 128, 512)
    ctx.tap("xlnT_c4", xlnT[:, 4 * T_TOK:4 * T_TOK + 512], 128, 512)

    for g in range(T_TOK // 512):
        gsc = contextlib.ExitStack()
        psc = contextlib.ExitStack()
        outbuf = ctx.sbuf(f"l1f_out{g}", [128, NCH * 512], BF16, gsc)
        s_mm = ctx.sem(scope=gsc)
        s_oev = ctx.sem(scope=gsc)
        with nc.Block() as blk:
            pts = [ctx.psum(f"l1f_pg{g}_{o}", [128, 512], F32, psc) for o in range(8)]

            @blk.tensor
            def _(e):
                prime(e, s_mm, s_ln)
                s_ln.wait(e, nev)
                for q in range(8):
                    ri, be = divmod(q, 4)
                    for part in range(2):
                        if ri == 0:
                            wreg = 0 if part == 0 else 8
                        else:
                            wreg = 4 if part == 0 else 0
                        for cq in range(4):
                            ins = e.matmul(
                                pts[q][:, :],
                                wT[:, (wreg + cq) * 512 + be * 128:
                                    (wreg + cq) * 512 + (be + 1) * 128],
                                xlnT[:, (part * 4 + cq) * T_TOK + g * 512:
                                     (part * 4 + cq) * T_TOK + (g + 1) * 512],
                                start=(part == 0 and cq == 0),
                                stop=(part == 1 and cq == 3))
                    s_mm.inc(ins, 1)

            @blk.scalar
            def _(e):
                prime(e, s_oev, s_mm)
                s_db.wait(e, 32)
                for q in range(8):
                    ri, be = divmod(q, 4)
                    s_mm.wait(e, q + 1)
                    s_oev.inc(e.activation(outbuf[:, q * 512:(q + 1) * 512],
                                           pts[q][:, :], AF.Identity,
                                           bias=bias[:, ri * 4 + be:ri * 4 + be + 1]), 1)

        psc.close()
        if g == 0:
            ctx.tap("outbuf0", outbuf[:, 0:8 * 512], 128, 8 * 512)
        emit_out_ln_add(ctx, f"l1f_g{g}", outbuf, 512, g * 512,
                        src_ready=(s_oev, 8))
        gsc.close()
    sc_x.close()
    sc_all.close()


def emit_l1_attn(ctx):
    nc = ctx.nc
    k_in = nc.dram_tensor("l1a_k", [8192, 1024], F32, kind="ExternalInput")
    v_in = nc.dram_tensor("l1a_v", [8192, 1024], F32, kind="ExternalInput")
    kT_h = nc.dram_tensor("l1a_kT", [1024, 8192], BF16)
    kb_h = nc.dram_tensor("l1a_kb", [8192, 1024], BF16)
    vb_h = nc.dram_tensor("l1a_vb", [8192, 1024], BF16)
    idn = ctx.b["idn"]

    sc_all = contextlib.ExitStack()
    mb = ctx.sbuf("l1a_mb", [128, 8 * 1024], BF16, sc_all)

    sc_p = contextlib.ExitStack()
    with nc.Block() as blk:
        stg = ctx.sbuf("l1a_stg", [128, 2 * 1024], F32, sc_p)
        kbb = ctx.sbuf("l1a_kbb", [128, 2 * 1024], BF16, sc_p)
        ktst = ctx.sbuf("l1a_ktst", [128, 2 * 1024], BF16, sc_p)
        pt = ctx.psum("l1a_pt", [128, 1024], BF16, sc_p)
        pt2 = ctx.psum("l1a_pt2", [128, 1024], BF16, sc_p)
        s_l = [ctx.sem(scope=sc_p) for _ in range(2)]
        s_cst = ctx.sem(scope=sc_p)
        s_tr = ctx.sem(scope=sc_p)
        s_te = ctx.sem(scope=sc_p)
        s_out = [ctx.sem(scope=sc_p, kls="gdma") for _ in range(4)]

        def outs_done(e, nk):
            for sl in range(4):
                cnt = (nk + 3 - sl) // 4
                if cnt > 0:
                    s_out[sl].wait(e, 16 * cnt)

        @blk.sync
        def _(e):
            prime(e, s_l, s_cst)
            for i in range(128):
                if i >= 2:
                    s_cst.wait(e, i - 1)
                src = k_in if i % 2 == 0 else v_in
                ch = i // 2
                s_l[i % 2].inc(e.dma_start(
                    stg[:, (i % 2) * 1024:(i % 2 + 1) * 1024],
                    src.ap()[ch * 128:(ch + 1) * 128, :]), 16)

        def _T(j):
            if j < 0:
                return 0
            return 10 * (j // 2) + (9 if j % 2 == 0 else 10)

        @blk.vector
        def _(e):
            prime(e, s_l, s_cst, s_out, s_tr)
            for i in range(128):
                s_l[i % 2].wait(e, 16 * (i // 2 + 1))
                if i >= 2:
                    outs_done(e, _T(i - 2))
                    if i % 2 == 0:
                        s_tr.wait(e, i // 2)
                s_cst.inc(e.tensor_copy(
                    kbb[:, (i % 2) * 1024:(i % 2 + 1) * 1024],
                    stg[:, (i % 2) * 1024:(i % 2 + 1) * 1024]), 1)

        @blk.tensor
        def _(e):
            prime(e, s_cst, s_tr, s_te)
            for ch in range(64):
                buf = pt if ch % 2 == 0 else pt2
                s_cst.wait(e, 2 * ch + 1)
                if ch >= 2:
                    s_te.wait(e, ch - 1)
                for dc in range(8):
                    ins = e.transpose(buf[:, dc * 128:(dc + 1) * 128],
                                      kbb[:, dc * 128:(dc + 1) * 128],
                                      idn[:, :])
                s_tr.inc(ins, 1)

        @blk.scalar
        def _(e):
            prime(e, s_tr, s_te, s_out)
            for ch in range(64):
                buf = pt if ch % 2 == 0 else pt2
                s_tr.wait(e, ch + 1)
                if ch >= 2:
                    outs_done(e, _T(2 * ch - 4))
                s_te.inc(e.activation(ktst[:, (ch % 2) * 1024:(ch % 2 + 1) * 1024],
                                      buf[:, :], AF.Copy), 1)

        @blk.gpsimd
        def _(e):
            prime(e, s_out, s_cst, s_te)
            kdma = 0

            def issue(out_ap, in_ap):
                nonlocal kdma
                sl = kdma % 4
                if kdma >= 4:
                    s_out[sl].wait(e, 16 * (kdma // 4))
                s_out[sl].inc(e.dma_start(out_ap, in_ap), 16)
                kdma += 1

            for i in range(128):
                if i % 2 == 0:
                    ch = i // 2
                    s_cst.wait(e, i + 1)
                    issue(kb_h.ap()[ch * 128:(ch + 1) * 128, :], kbb[:, 0:1024])
                    s_te.wait(e, ch + 1)
                    for dc in range(8):
                        issue(kT_h.ap()[dc * 128:(dc + 1) * 128, ch * 128:(ch + 1) * 128],
                              ktst[:, (ch % 2) * 1024 + dc * 128:(ch % 2) * 1024 + (dc + 1) * 128])
                else:
                    ch = i // 2
                    s_cst.wait(e, i + 1)
                    issue(vb_h.ap()[ch * 128:(ch + 1) * 128, :], kbb[:, 1024:2048])
            for sl in range(4):
                s_out[sl].wait(e, 16 * 160)
    sc_p.close()

    sc_g = contextlib.ExitStack()
    with nc.Block() as blk:
        gstg = ctx.sbuf("l1a_gstg", [128, 4 * 1024], BF16, sc_g)
        s_l = [ctx.sem(scope=sc_g) for _ in range(4)]
        s_mm = ctx.sem(scope=sc_g)
        s_me = ctx.sem(scope=sc_g)
        pg = [ctx.psum(f"l1a_pg{o}", [128, 512], F32, sc_g) for o in range(8)]

        @blk.sync
        def _(e):
            prime(e, s_l, s_mm)
            for half in range(2):
                for ch in range(64):
                    i = half * 64 + ch
                    if i >= 4:
                        s_mm.wait(e, i - 3)
                    s_l[i % 4].inc(e.dma_start(
                        gstg[:, (i % 4) * 1024:(i % 4 + 1) * 1024],
                        kb_h.ap()[ch * 128:(ch + 1) * 128, :]), 16)

        @blk.tensor
        def _(e):
            prime(e, s_l, s_mm, s_me)
            for half in range(2):
                for ch in range(64):
                    i = half * 64 + ch
                    s_l[i % 4].wait(e, 16 * (i // 4 + 1))
                    if half == 1 and ch == 0:
                        s_me.wait(e, 8)
                    for dc in range(8):
                        ins = e.matmul(
                            pg[dc][:, :],
                            gstg[:, (i % 4) * 1024 + dc * 128:(i % 4) * 1024 + (dc + 1) * 128],
                            gstg[:, (i % 4) * 1024 + half * 512:(i % 4) * 1024 + half * 512 + 512],
                            start=(ch == 0), stop=(ch == 63))
                    s_mm.inc(ins, 1)

        @blk.scalar
        def _(e):
            prime(e, s_mm, s_me)
            for half in range(2):
                s_mm.wait(e, 64 * (half + 1))
                for dc in range(8):
                    s_me.inc(e.activation(
                        mb[:, dc * 1024 + half * 512: dc * 1024 + half * 512 + 512],
                        pg[dc][:, :], AF.Copy), 1)

    sc_g.close()

    sc_b = contextlib.ExitStack()
    qT = ctx.sbuf("l1a_qT", [128, 8 * 512], BF16, sc_b)
    gT = ctx.sbuf("l1a_gT", [128, 64 * 512], BF16, sc_b)
    ktb = ctx.sbuf("l1a_ktb", [128, 2 * 8 * 512], BF16, sc_b)
    vbuf = ctx.sbuf("l1a_vbuf", [128, 4 * 1024], BF16, sc_b)
    outb = ctx.sbuf("l1a_outb", [128, 8 * 512], BF16, sc_b)

    for tb in range(8):
        bsc = contextlib.ExitStack()
        mqp = ctx.sbuf(f"l1a_mqp{tb}", [128, 2 * 512], BF16, bsc)
        sqb = ctx.sbuf(f"l1a_sqb{tb}", [128, 512], F32, bsc)
        cb = ctx.sbuf(f"l1a_cb{tb}", [128, 512], F32, bsc)
        ssb = sqb[0:1, 0:512]
        with nc.Block() as blk:
            pmq = ctx.psum(f"l1a_pmq_{tb}", [128, 512], F32, bsc)
            pmq2 = ctx.psum(f"l1a_pmq2_{tb}", [128, 512], F32, bsc)
            pss = ctx.psum(f"l1a_pss_{tb}", [1, 512], F32, bsc)
            pbc = ctx.psum(f"l1a_pbc_{tb}", [128, 512], F32, bsc)
            psc = [ctx.psum(f"l1a_psc{tb}_{o}", [128, 512], F32, bsc) for o in range(2)]
            s_kl = [ctx.sem(scope=bsc) for _ in range(2)]
            s_mq = ctx.sem(scope=bsc)
            s_mqv = ctx.sem(scope=bsc)
            s_ssm = ctx.sem(scope=bsc)
            s_cchain = ctx.sem(scope=bsc)
            s_qs = ctx.sem(scope=bsc)
            s_sc = ctx.sem(scope=bsc)
            s_ge = ctx.sem(scope=bsc)

            ssbb = mqp[0:1, 0:512]
            nev, s_qe = emit_ln_transpose(ctx, blk, f"l1a_q{tb}",
                                          list(range(4 * tb, 4 * tb + 4)),
                                          qT, 512, 0, scope=bsc)

            @blk.tensor
            def _(e):
                prime(e, s_mq, s_ssm, s_cchain, s_sc, s_kl, s_mqv, s_qs, s_ge)
                s_qe.wait(e, nev)
                for dc in range(8):
                    buf = pmq if dc % 2 == 0 else pmq2
                    if dc >= 2:
                        s_mqv.wait(e, dc - 1)
                    for dcp in range(8):
                        ins = e.matmul(
                            buf[:, :],
                            mb[:, dcp * 1024 + dc * 128: dcp * 1024 + (dc + 1) * 128],
                            qT[:, dcp * 512:(dcp + 1) * 512],
                            start=(dcp == 0), stop=(dcp == 7))
                    s_mq.inc(ins, 1)
                    if dc >= 1:
                        s_mqv.wait(e, dc)
                        s_ssm.inc(e.matmul(pss[:, :], ctx.cb["ones_col"][:, :],
                                           mqp[:, ((dc - 1) % 2) * 512:((dc - 1) % 2 + 1) * 512],
                                           start=(dc == 1), stop=False,
                                           skip_group_check=True), 1)
                s_mqv.wait(e, 8)
                s_ssm.inc(e.matmul(pss[:, :], ctx.cb["ones_col"][:, :],
                                   mqp[:, 1 * 512:2 * 512],
                                   start=False, stop=True,
                                   skip_group_check=True), 1)
                s_cchain.wait(e, 2)
                s_cchain.inc(e.matmul(pbc[:, :], ctx.cb["ones_row"][:, :],
                                      ssbb[:, :], start=True, stop=True), 1)
                s_qs.wait(e, 8)
                k = 0
                for sc in range(16):
                    s_kl[sc % 2].wait(e, 16 * 8 * (sc // 2 + 1))
                    for nt in range(4):
                        buf = psc[k % 2]
                        if k >= 2:
                            s_ge.wait(e, k - 1)
                        for dc in range(8):
                            ins = e.matmul(
                                buf[:, :],
                                ktb[:, (sc % 2) * 4096 + dc * 512 + nt * 128:
                                    (sc % 2) * 4096 + dc * 512 + (nt + 1) * 128],
                                qT[:, dc * 512:(dc + 1) * 512],
                                start=(dc == 0), stop=(dc == 7))
                        s_sc.inc(ins, 1)
                        k += 1

            @blk.vector
            def _(e):
                prime(e, s_mq, s_mqv, s_cchain, s_qs, s_ssm)
                for dc in range(8):
                    s_mq.wait(e, dc + 1)
                    if dc >= 2:
                        s_ssm.wait(e, dc - 1)
                    s_mqv.inc(e.tensor_mul(mqp[:, (dc % 2) * 512:(dc % 2 + 1) * 512],
                                           pmq[:, :] if dc % 2 == 0 else pmq2[:, :],
                                           qT[:, dc * 512:(dc + 1) * 512]), 1)
                s_ssm.wait(e, 8)
                s_cchain.inc(e.tensor_copy(ssb, pss[:, :]), 1)
                s_cchain.wait(e, 1)
                s_cchain.inc(e.tensor_copy(ssbb[:, :], ssb), 1)
                s_cchain.wait(e, 4)
                s_cchain.inc(e.reciprocal(cb[:, :], sqb[:, :]), 1)
                s_cchain.wait(e, 5)
                s_cchain.inc(e.tensor_scalar(cb[:, :], cb[:, :],
                                             90.50966799187809, None, OP.mult), 1)
                s_cchain.wait(e, 6)
                for dc in range(8):
                    s_qs.inc(e.tensor_tensor(qT[:, dc * 512:(dc + 1) * 512],
                                             qT[:, dc * 512:(dc + 1) * 512],
                                             cb[:, :], OP.mult), 1)

            @blk.scalar
            def _(e):
                prime(e, s_cchain, s_ge, s_sc)
                s_cchain.wait(e, 3)
                s_cchain.inc(e.activation(sqb[:, :], pbc[:, :], AF.Sqrt), 1)
                for k in range(64):
                    s_sc.wait(e, k + 1)
                    s_ge.inc(e.activation(gT[:, k * 512:(k + 1) * 512],
                                          psc[k % 2][:, :], AF.Gelu), 1)

            @blk.sync
            def _(e):
                prime(e, s_kl, s_sc)
                for sc in range(16):
                    if sc >= 2:
                        s_sc.wait(e, 4 * (sc - 1))
                    for dc in range(8):
                        s_kl[sc % 2].inc(e.dma_start(
                            ktb[:, (sc % 2) * 4096 + dc * 512:(sc % 2) * 4096 + (dc + 1) * 512],
                            kT_h.ap()[dc * 128:(dc + 1) * 128,
                                      sc * 512:(sc + 1) * 512]), 16)

        bsc.close()

        bsc2 = contextlib.ExitStack()
        with nc.Block() as blk:
            po = [ctx.psum(f"l1a_po{tb}_{o}", [128, 512], F32, bsc2) for o in range(8)]
            s_vl = [ctx.sem(scope=bsc2) for _ in range(2)]
            s_av = ctx.sem(scope=bsc2)
            s_oe = ctx.sem(scope=bsc2)

            @blk.sync
            def _(e):
                prime(e, s_vl, s_av)
                for grp in range(32):
                    if grp >= 2:
                        s_av.wait(e, 2 * (grp - 1))
                    for c4 in range(2):
                        ch = grp * 2 + c4
                        s_vl[grp % 2].inc(e.dma_start(
                            vbuf[:, ((grp % 2) * 2 + c4) * 1024:
                                 ((grp % 2) * 2 + c4 + 1) * 1024],
                            vb_h.ap()[ch * 128:(ch + 1) * 128, :]), 16)

            @blk.tensor
            def _(e):
                prime(e, s_vl, s_av)
                for grp in range(32):
                    s_vl[grp % 2].wait(e, 16 * 2 * (grp // 2 + 1))
                    for c4 in range(2):
                        ch = grp * 2 + c4
                        for oc in range(8):
                            ins = e.matmul(
                                po[oc][:, :],
                                vbuf[:, ((grp % 2) * 2 + c4) * 1024 + oc * 128:
                                     ((grp % 2) * 2 + c4) * 1024 + (oc + 1) * 128],
                                gT[:, ch * 512:(ch + 1) * 512],
                                start=(ch == 0), stop=(ch == 63))
                        s_av.inc(ins, 1)

            @blk.scalar
            def _(e):
                prime(e, s_oe, s_av)
                s_av.wait(e, 64)
                for oc in range(8):
                    s_oe.inc(e.activation(outb[:, oc * 512:(oc + 1) * 512],
                                          po[oc][:, :], AF.Copy), 1)
        bsc2.close()

        emit_out_ln_add(ctx, f"l1a_o{tb}", outb, 512, tb * 512)
    sc_b.close()
    sc_all.close()


def emit_l0_attn(ctx):
    nc = ctx.nc
    idn = ctx.b["idn"]
    w_in = {w: nc.dram_tensor(f"l0a_{w}", [1024, 1024], F32, kind="ExternalInput")
            for w in ("wq", "wk", "wv", "wo")}
    qt_h = nc.dram_tensor("l0a_qt", [2, 16, 65, 2048], BF16)
    rs_h = nc.dram_tensor("l0a_rs", [2, 64, 512], BF16)
    kt_h = nc.dram_tensor("l0a_kt", [2, 16, 65, 2048], BF16)
    va_h = nc.dram_tensor("l0a_va", [2, 16, 128, 1040], BF16)
    VAR_F = 64.0 / 3969.0

    for sq in range(2):
        t0 = sq * 16
        sc_seq = contextlib.ExitStack()
        avT = ctx.sbuf(f"l0a_avT{sq}", [128, 8 * 2048], BF16, sc_seq)

        sc_x = contextlib.ExitStack()
        sc_ln = contextlib.ExitStack()
        xlnT = ctx.sbuf(f"l0a_xlnT{sq}", [128, NCH * 2048], BF16, sc_x)
        with nc.Block() as blk:
            nev, s_ln = emit_ln_transpose(ctx, blk, f"l0a_ln1_{sq}",
                                          list(range(t0, t0 + 16)),
                                          xlnT, 2048, 0, scope=sc_ln, sem_scope=sc_x)
        sc_ln.close()

        sc_w = contextlib.ExitStack()
        wb = ctx.sbuf(f"l0a_wb{sq}", [128, 8 * 1024], BF16, sc_w)
        wstg = ctx.sbuf(f"l0a_wstg{sq}", [128, 2 * 1024], F32, sc_w)
        ev1 = ctx.sbuf(f"l0a_ev1_{sq}", [128, 3 * 512], BF16, sc_w)
        ev2 = ctx.sbuf(f"l0a_ev2_{sq}", [128, 3 * 512], BF16, sc_w)
        sqv = ctx.sbuf(f"l0a_sqv{sq}", [128, 3 * 512], BF16, sc_w)
        vrow = ctx.sbuf(f"l0a_vrow{sq}", [2, 3 * 512], BF16, sc_w)

        for wi, wname in enumerate(("wq", "wk")):
            dst_h = qt_h if wname == "wq" else kt_h
            wsc = contextlib.ExitStack()
            with nc.Block() as blk:
                pj = [ctx.psum(f"l0a_pj{sq}{wi}{j}", [128, 512], F32, wsc) for j in range(3)]
                pc = [ctx.psum(f"l0a_pc{sq}{wi}{j}", [128, 512], F32, wsc) for j in range(3)]
                pv = ctx.psum(f"l0a_pv{sq}{wi}", [2, 512], F32, wsc)
                s_wl = [ctx.sem(scope=wsc) for _ in range(2)]
                s_wc = ctx.sem(scope=wsc)
                s_pj = ctx.sem(scope=wsc)
                s_e1 = ctx.sem(scope=wsc)
                s_cm = ctx.sem(scope=wsc)
                s_e2 = ctx.sem(scope=wsc)
                s_sq = ctx.sem(scope=wsc)
                s_iv = ctx.sem(scope=wsc)
                s_vr = ctx.sem(scope=wsc)
                s_od = [ctx.sem(scope=wsc, kls="gdma") for _ in range(2)]

                @blk.sync
                def _(e):
                    prime(e, s_wl, s_wc)
                    for ec in range(8):
                        if ec >= 2:
                            s_wc.wait(e, ec - 1)
                        s_wl[ec % 2].inc(e.dma_start(
                            wstg[:, (ec % 2) * 1024:(ec % 2 + 1) * 1024],
                            w_in[wname].ap()[ec * 128:(ec + 1) * 128, :]), 16)

                @blk.vector
                def _(e):
                    prime(e, s_wl, s_wc)
                    for ec in range(8):
                        s_wl[ec % 2].wait(e, 16 * (ec // 2 + 1))
                        s_wc.inc(e.tensor_copy(wb[:, ec * 1024:(ec + 1) * 1024],
                                               wstg[:, (ec % 2) * 1024:(ec % 2 + 1) * 1024]), 1)

                @blk.tensor
                def _(e):
                    prime(e, s_pj, s_cm, s_iv, s_wc, s_e1, s_e2, s_sq, s_vr, s_ln)
                    s_wc.wait(e, 8)
                    s_ln.wait(e, nev)
                    k = 0
                    for dc in range(8):
                        for tc in range(4):
                            if k >= 3:
                                s_e1.wait(e, k - 2)
                            for ec in range(8):
                                ins = e.matmul(
                                    pj[k % 3][:, :],
                                    wb[:, ec * 1024 + dc * 128: ec * 1024 + (dc + 1) * 128],
                                    xlnT[:, ec * 2048 + tc * 512: ec * 2048 + (tc + 1) * 512],
                                    start=(ec == 0), stop=(ec == 7))
                            s_pj.inc(ins, 1)
                            s_e1.wait(e, k + 1)
                            if k >= 3:
                                s_e2.wait(e, k - 2)
                            s_cm.inc(e.matmul(pc[k % 3][:, :], ctx.cb["cblk"][:, :],
                                              ev1[:, (k % 3) * 512:(k % 3 + 1) * 512],
                                              start=True, stop=True), 1)
                            s_sq.wait(e, k + 1)
                            if k >= 1:
                                s_vr.wait(e, k)
                            s_iv.inc(e.matmul(pv[:, :], ctx.cb["ind2"][:, :],
                                              sqv[:, (k % 3) * 512:(k % 3 + 1) * 512],
                                              start=True, stop=True), 1)
                            k += 1

                @blk.scalar
                def _(e):
                    prime(e, s_pj, s_e1, s_cm, s_e2, s_sq, s_iv, s_vr, *s_od)
                    for k in range(32):
                        s_pj.wait(e, k + 1)
                        if k >= 3:
                            s_cm.wait(e, k - 2)
                            s_sq.wait(e, k - 2)
                        if k >= 3:
                            s_od[(k - 3) % 2].wait(e, 16 * 4 * ((k - 3) // 2 + 1))
                        s_e1.inc(e.activation(ev1[:, (k % 3) * 512:(k % 3 + 1) * 512],
                                              pj[k % 3][:, :], AF.Copy), 1)
                        s_cm.wait(e, k + 1)
                        s_e2.inc(e.activation(ev2[:, (k % 3) * 512:(k % 3 + 1) * 512],
                                              pc[k % 3][:, :], AF.Copy), 1)
                        s_e2.wait(e, k + 1)
                        s_sq.inc(e.activation(sqv[:, (k % 3) * 512:(k % 3 + 1) * 512],
                                              ev2[:, (k % 3) * 512:(k % 3 + 1) * 512],
                                              AF.Square), 1)
                        s_iv.wait(e, k + 1)
                        s_vr.inc(e.activation(vrow[:, (k % 3) * 512:(k % 3 + 1) * 512],
                                              pv[:, :], AF.Identity,
                                              scale=(VAR_F if wname == "wq" else 1.0)), 1)

                @blk.gpsimd
                def _(e):
                    prime(e, s_od, s_e2, s_vr)
                    for k in range(32):
                        dc, tc = divmod(k, 4)
                        if k >= 2:
                            s_od[k % 2].wait(e, 16 * 4 * (k // 2))
                        s_e2.wait(e, k + 1)
                        for hh in range(2):
                            s_od[k % 2].inc(e.dma_start(
                                dst_h.ap()[sq, 2 * dc + hh, 0:64, tc * 512:(tc + 1) * 512],
                                ev2[(hh * 64):(hh * 64 + 64),
                                    (k % 3) * 512:(k % 3 + 1) * 512]), 16)
                        s_vr.wait(e, k + 1)
                        for hh in range(2):
                            s_od[k % 2].inc(e.dma_start(
                                dst_h.ap()[sq, 2 * dc + hh, 64:65, tc * 512:(tc + 1) * 512],
                                vrow[hh:hh + 1, (k % 3) * 512:(k % 3 + 1) * 512]), 16)
                    for sl in range(2):
                        s_od[sl].wait(e, 16 * 4 * 16)
            wsc.close()

        wsc = contextlib.ExitStack()
        with nc.Block() as blk:
            vstg = ctx.sbuf(f"l0a_vstg{sq}", [128, 1040], BF16, wsc)
            pj = [ctx.psum(f"l0a_pvj{sq}{j}", [128, 512], F32, wsc) for j in range(2)]
            s_wl = [ctx.sem(scope=wsc) for _ in range(2)]
            s_wc = ctx.sem(scope=wsc)
            s_pj = ctx.sem(scope=wsc)
            s_ev = ctx.sem(scope=wsc)
            s_od = ctx.sem(scope=wsc, kls="gdma")

            @blk.sync
            def _(e):
                prime(e, s_wl, s_wc)
                for ec in range(8):
                    if ec >= 2:
                        s_wc.wait(e, ec - 1)
                    s_wl[ec % 2].inc(e.dma_start(
                        wstg[:, (ec % 2) * 1024:(ec % 2 + 1) * 1024],
                        w_in["wv"].ap()[ec * 128:(ec + 1) * 128, :]), 16)

            @blk.vector
            def _(e):
                prime(e, s_wl, s_wc, s_ev, s_pj, s_od)
                for ec in range(8):
                    s_wl[ec % 2].wait(e, 16 * (ec // 2 + 1))
                    s_wc.inc(e.tensor_copy(wb[:, ec * 1024:(ec + 1) * 1024],
                                           wstg[:, (ec % 2) * 1024:(ec % 2 + 1) * 1024]), 1)
                s_ev.inc(e.memset(bass.AP(vstg, 64, [[1040, 128], [65, 16], [1, 1]]), 1.0), 1)
                for k in range(32):
                    nt, hf = divmod(k, 2)
                    s_pj.wait(e, k + 1)
                    if nt >= 1:
                        s_od.wait(e, 16 * nt)
                    s_ev.inc(e.tensor_copy(
                        bass.AP(vstg, hf * 8 * 65, [[1040, 128], [65, 8], [1, 64]]),
                        pj[k % 2][:, :]), 1)

            @blk.tensor
            def _(e):
                prime(e, s_pj, s_wc, s_ev, s_ln)
                s_wc.wait(e, 8)
                s_ln.wait(e, nev)
                k = 0
                for nt in range(16):
                    for hf in range(2):
                        if k >= 2:
                            s_ev.wait(e, k)
                        for ec in range(8):
                            ins = e.matmul(
                                pj[k % 2][:, :],
                                xlnT[:, ec * 2048 + nt * 128: ec * 2048 + (nt + 1) * 128],
                                wb[:, ec * 1024 + hf * 512: ec * 1024 + (hf + 1) * 512],
                                start=(ec == 0), stop=(ec == 7))
                        s_pj.inc(ins, 1)
                        k += 1

            @blk.gpsimd
            def _(e):
                prime(e, s_od, s_ev)
                for nt in range(16):
                    s_ev.wait(e, 2 * nt + 3)
                    s_od.inc(e.dma_start(va_h.ap()[sq, nt, :, :], vstg[:, :]), 16)
                s_od.wait(e, 16 * 16)
        wsc.close()
        sc_w.close()
        sc_x.close()

        sc_a = contextlib.ExitStack()
        rstage = ctx.sbuf(f"l0a_rstage{sq}", [1, 2 * 512], BF16, sc_a)
        qkt = ctx.sbuf(f"l0a_qkt{sq}", [65, 4 * 2048], BF16, sc_a)
        vab = ctx.sbuf(f"l0a_vab{sq}", [128, 2 * 1040], BF16, sc_a)
        ptb = ctx.sbuf(f"l0a_ptb{sq}", [128, 16 * 512], BF16, sc_a)
        with nc.Block() as blk:
            psc = [ctx.psum(f"l0a_psc{sq}{j}", [128, 1024], F32, sc_a) for j in range(2)]
            pav = ctx.psum(f"l0a_pav{sq}", [65, 512], F32, sc_a)
            s_hl = [ctx.sem(scope=sc_a) for _ in range(2)]
            s_sc = ctx.sem(scope=sc_a)
            s_ex = ctx.sem(scope=sc_a)
            s_av = ctx.sem(scope=sc_a)
            s_ae = ctx.sem(scope=sc_a)
            s_rst = ctx.sem(scope=sc_a)
            s_rod = [ctx.sem(scope=sc_a, kls="gdma") for _ in range(2)]

            @blk.sync
            def _(e):
                prime(e, s_hl, s_av)
                for h in range(16):
                    if h >= 2:
                        s_av.wait(e, 4 * (h - 1))
                    s_hl[h % 2].inc(e.dma_start(
                        qkt[:, (h % 2) * 2048:(h % 2 + 1) * 2048],
                        qt_h.ap()[sq, h, :, :]), 16)
                    s_hl[h % 2].inc(e.dma_start(
                        qkt[:, (2 + h % 2) * 2048:(3 + h % 2) * 2048],
                        kt_h.ap()[sq, h, :, :]), 16)
                    s_hl[h % 2].inc(e.dma_start(
                        vab[:, (h % 2) * 1040:(h % 2 + 1) * 1040],
                        bass.AP(va_h, (sq * 16 * 128 * 1040) + h * 65,
                                [[1040, 128], [128 * 1040, 16], [1, 65]])), 16)

            @blk.tensor
            def _(e):
                prime(e, s_sc, s_av, s_hl, s_ex, s_ae)
                for h in range(16):
                    s_hl[h % 2].wait(e, 48 * (h // 2 + 1))
                    for tc in range(4):
                        k = h * 4 + tc
                        for ng in range(8):
                            kk = k * 8 + ng
                            if kk >= 2:
                                s_ex.wait(e, kk - 1)
                            for j in range(2):
                                nt = 2 * ng + j
                                ins = e.matmul(
                                    psc[kk % 2][:, j * 512:(j + 1) * 512],
                                    qkt[:, (2 + h % 2) * 2048 + nt * 128:
                                        (2 + h % 2) * 2048 + (nt + 1) * 128],
                                    qkt[:, (h % 2) * 2048 + tc * 512:
                                        (h % 2) * 2048 + (tc + 1) * 512],
                                    start=True, stop=True)
                            s_sc.inc(ins, 1)
                        s_ae.wait(e, k)
                        for nt in range(16):
                            s_ex.wait(e, k * 8 + nt // 2 + 1)
                            ins = e.matmul(
                                pav[:, :],
                                vab[:, (h % 2) * 1040 + nt * 65:
                                    (h % 2) * 1040 + (nt + 1) * 65],
                                ptb[:, nt * 512:(nt + 1) * 512],
                                start=(nt == 0), stop=(nt == 15))
                        s_av.inc(ins, 1)

            @blk.scalar
            def _(e):
                prime(e, s_ex, s_sc)
                for kk in range(16 * 4 * 8):
                    s_sc.wait(e, kk + 1)
                    ng = kk % 8
                    s_ex.inc(e.activation(ptb[:, ng * 1024:(ng + 1) * 1024],
                                          psc[kk % 2][:, :], AF.Exp), 1)

            @blk.vector
            def _(e):
                prime(e, s_av, s_ae, s_rst, s_rod)
                for h in range(16):
                    for tc in range(4):
                        k = h * 4 + tc
                        s_av.wait(e, k + 1)
                        if k >= 2:
                            s_rod[k % 2].wait(e, 16 * (k // 2))
                        s_rst.inc(e.tensor_copy(
                            rstage[0:1, (k % 2) * 512:(k % 2 + 1) * 512],
                            pav[64:65, :]), 1)
                        s_ae.inc(e.tensor_copy(
                            avT[(h % 2) * 64:(h % 2) * 64 + 64,
                                (h // 2) * 2048 + tc * 512:(h // 2) * 2048 + (tc + 1) * 512],
                            pav[0:64, :]), 1)

            @blk.gpsimd
            def _(e):
                prime(e, s_rod, s_rst)
                for k in range(64):
                    s_rst.wait(e, k + 1)
                    s_rod[k % 2].inc(e.dma_start(
                        rs_h.ap()[sq, k:k + 1, :],
                        rstage[0:1, (k % 2) * 512:(k % 2 + 1) * 512]), 16)
                for sl in range(2):
                    s_rod[sl].wait(e, 16 * 32)
        sc_a.close()

        sc_p4 = contextlib.ExitStack()
        wob = ctx.sbuf(f"l0a_wob{sq}", [128, 8 * 1024], BF16, sc_p4)
        wstg4 = ctx.sbuf(f"l0a_wstg4{sq}", [128, 2 * 1024], F32, sc_p4)
        recb = ctx.sbuf(f"l0a_recb{sq}", [64, 512], BF16, sc_p4)
        recf = ctx.sbuf(f"l0a_recf{sq}", [64, 512], F32, sc_p4)
        rows = ctx.sbuf(f"l0a_rows{sq}", [64, 512], BF16, sc_p4)
        selb = ctx.sbuf(f"l0a_selb{sq}", [64, 32 * 128], BF16, sc_p4)
        outb = ctx.sbuf(f"l0a_outb{sq}", [128, 8 * 512], BF16, sc_p4)
        s_oev = ctx.sem(scope=sc_seq)
        sc_nrm = contextlib.ExitStack()
        with nc.Block() as blk:
            pbc = [ctx.psum(f"l0a_pbc{sq}{j}", [128, 512], F32, sc_nrm) for j in range(2)]
            s_wl = [ctx.sem(scope=sc_p4) for _ in range(2)]
            s_wc = ctx.sem(scope=sc_p4)
            s_rc = ctx.sem(scope=sc_p4)
            s_bc = ctx.sem(scope=sc_p4)
            s_nm = ctx.sem(scope=sc_p4)

            s_rl = ctx.sem()

            @blk.sync
            def _(e):
                prime(e, s_wl, s_wc, s_rl)
                s_rl.inc(e.dma_start(rows[:, :], rs_h.ap()[sq]), 16)
                s_rl.wait(e, 16)
                s_rl.inc(e.dma_start(selb[:, :], ctx.cdram["seln"].ap()), 16)
                for ec in range(8):
                    if ec >= 2:
                        s_wc.wait(e, ec - 1)
                    s_wl[ec % 2].inc(e.dma_start(
                        wstg4[:, (ec % 2) * 1024:(ec % 2 + 1) * 1024],
                        w_in["wo"].ap()[ec * 128:(ec + 1) * 128, :]), 16)

            @blk.vector
            def _(e):
                prime(e, s_wl, s_wc, s_rc, s_bc, s_nm)
                for ec in range(8):
                    s_wl[ec % 2].wait(e, 16 * (ec // 2 + 1))
                    s_wc.inc(e.tensor_copy(wob[:, ec * 1024:(ec + 1) * 1024],
                                           wstg4[:, (ec % 2) * 1024:(ec % 2 + 1) * 1024]), 1)
                s_rl.wait(e, 16)
                s_rc.inc(e.reciprocal(recf[:, :], rows[:, :]), 1)
                s_rc.wait(e, 1)
                s_rc.inc(e.tensor_copy(recb[:, :], recf[:, :]), 1)
                for j in range(32):
                    p, tc = divmod(j, 4)
                    s_bc.wait(e, j + 1)
                    s_nm.inc(e.tensor_tensor(
                        avT[:, p * 2048 + tc * 512: p * 2048 + (tc + 1) * 512],
                        avT[:, p * 2048 + tc * 512: p * 2048 + (tc + 1) * 512],
                        pbc[j % 2][:, :], OP.mult), 1)

            @blk.tensor
            def _(e):
                prime(e, s_rc, s_bc, s_nm)
                s_rc.wait(e, 2)
                s_rl.wait(e, 32)
                for j in range(32):
                    if j >= 2:
                        s_nm.wait(e, j - 1)
                    s_bc.inc(e.matmul(pbc[j % 2][:, :],
                                      selb[:, j * 128:(j + 1) * 128],
                                      recb[:, :],
                                      start=True, stop=True), 1)
        sc_nrm.close()
        for tc in range(4):
            wsc2 = contextlib.ExitStack()
            with nc.Block() as blk:
                pw = [ctx.psum(f"l0a_pw{sq}{tc}{j}", [128, 512], F32, wsc2) for j in range(8)]
                s_wm = ctx.sem(scope=wsc2)

                @blk.tensor
                def _(e):
                    prime(e, s_wm)
                    for eci in range(8):
                        for hdc in range(8):
                            ins = e.matmul(
                                pw[eci][:, :],
                                wob[:, hdc * 1024 + eci * 128: hdc * 1024 + (eci + 1) * 128],
                                avT[:, hdc * 2048 + tc * 512: hdc * 2048 + (tc + 1) * 512],
                                start=(hdc == 0), stop=(hdc == 7))
                        s_wm.inc(ins, 1)

                @blk.scalar
                def _(e):
                    prime(e, s_oev, s_wm)
                    for eci in range(8):
                        s_wm.wait(e, eci + 1)
                        s_oev.inc(e.activation(outb[:, eci * 512:(eci + 1) * 512],
                                               pw[eci][:, :], AF.Copy), 1)
            wsc2.close()
            emit_out_ln_add(ctx, f"l0a_o{sq}_{tc}", outb, 512, sq * 2048 + tc * 512)
        sc_p4.close()
        sc_seq.close()


def emit_l0_ffn(ctx):
    nc = ctx.nc
    idn = ctx.b["idn"]
    rw_in = nc.dram_tensor("l0f_rw", [1024, 8], F32, kind="ExternalInput")
    wqkv_in = [nc.dram_tensor("l0f_mqkv", [1024, 1536], F32, kind="ExternalInput"),
               nc.dram_tensor("l0f_dqkv", [1024, 1536], F32, kind="ExternalInput")]
    wo_in = [nc.dram_tensor("l0f_mwo", [512, 1024], F32, kind="ExternalInput"),
             nc.dram_tensor("l0f_dwo", [512, 1024], F32, kind="ExternalInput")]
    qt_h = nc.dram_tensor("l0f_qt", [2, 2, 8, 64, 2048], BF16)
    kt_h = nc.dram_tensor("l0f_kt", [2, 2, 8, 64, 2048], BF16)
    va_h = nc.dram_tensor("l0f_va", [2, 2, 16, 128, 520], BF16)
    rs_h = nc.dram_tensor("l0f_rs", [2, 64, 512], BF16)
    gm_h = nc.dram_tensor("l0f_gm", [2, 2, 8, 2048], F32)

    for sq in range(2):
        t0 = sq * 16
        sc_seq = contextlib.ExitStack()
        avT = ctx.sbuf(f"l0f_avT{sq}", [128, 8 * 2048], BF16, sc_seq)
        cmt = ctx.sbuf(f"l0f_cmt{sq}", [128, 128], F32, sc_seq)

        sc_x = contextlib.ExitStack()
        sc_ln = contextlib.ExitStack()
        xlnT = ctx.sbuf(f"l0f_xlnT{sq}", [128, NCH * 2048], BF16, sc_x)
        with nc.Block() as blk:
            nev, s_ln = emit_ln_transpose(ctx, blk, f"l0f_ln1_{sq}",
                                          list(range(t0, t0 + 16)),
                                          xlnT, 2048, 0, scope=sc_ln, sem_scope=sc_x)
        sc_ln.close()

        sc_r = contextlib.ExitStack()
        with nc.Block() as blk:
            rsc = ctx.sbuf(f"l0f_rsc{sq}", [8, 2048], F32, sc_r)
            rwf = ctx.sbuf(f"l0f_rwf{sq}", [128, 64], F32, sc_r)
            rwb = ctx.sbuf(f"l0f_rwb{sq}", [128, 64], BF16, sc_r)
            bis = ctx.sbuf(f"l0f_bis{sq}", [8, 8], F32, sc_r)
            cmp = ctx.sbuf(f"l0f_cmp{sq}", [8, 2048], BF16, sc_r)
            sg = ctx.sbuf(f"l0f_sg{sq}", [8, 2048], F32, sc_r)
            msk = ctx.sbuf(f"l0f_msk{sq}", [8, 2048], F32, sc_r)
            gmb = ctx.sbuf(f"l0f_gmbf{sq}", [8, 2048], F32, sc_r)
            mskb = ctx.sbuf(f"l0f_mskb{sq}", [8, 2048], BF16, sc_r)
            prc = [ctx.psum(f"l0f_prc{sq}{j}", [8, 512], F32, sc_r) for j in range(2)]
            ptc = ctx.psum(f"l0f_ptc{sq}", [128, 128], BF16, sc_r)
            s_rl = ctx.sem()
            s_cst = ctx.sem(scope=sc_r)
            s_mm = ctx.sem(scope=sc_r)
            s_ev = ctx.sem(scope=sc_r)
            vf = ctx.sem(scope=sc_r)
            s_sg = ctx.sem(scope=sc_r)
            s_tm = ctx.sem(scope=sc_r)
            s_cme = ctx.sem(scope=sc_r)
            s_go = ctx.sem(scope=sc_r, kls="gdma")

            @blk.sync
            def _(e):
                prime(e, s_rl)
                for half in range(2):
                    s_rl.inc(e.dma_start(
                        rwf[:, half * 32:(half + 1) * 32],
                        bass.AP(rw_in, half * 4 * 128 * 8,
                                [[8, 128], [1024, 4], [1, 8]])), 16)

            @blk.vector
            def _(e):
                prime(e, vf, s_cst, s_mm, s_ev, s_sg, s_tm, s_go, s_cme)
                s_rl.wait(e, 32)
                s_cst.inc(e.tensor_copy(rwb[:, :], rwf[:, :]), 1)
                for tc in range(4):
                    s_mm.wait(e, tc + 1)
                    vf.inc(e.tensor_copy(rsc[:, tc * 512:(tc + 1) * 512],
                                         prc[tc % 2][:, :]), 1)
                v = 4
                vf.wait(e, v)
                vf.inc(e.memset(bis[:, 0:1], -16.0), 1); v += 1
                vf.wait(e, v)
                vf.inc(e.memset(bis[:, 1:2], 16.0), 1); v += 1
                for it in range(24):
                    vf.wait(e, v)
                    vf.inc(e.scalar_tensor_tensor(bis[:, 2:3], bis[:, 0:1], 1.0,
                                                  bis[:, 1:2], OP.bypass, OP.add), 1); v += 1
                    vf.wait(e, v)
                    vf.inc(e.tensor_scalar(bis[:, 2:3], bis[:, 2:3], 0.5, None,
                                           OP.mult), 1); v += 1
                    vf.wait(e, v)
                    vf.inc(e.tensor_scalar(cmp[:, :], rsc[:, :], bis[:, 2:3], None,
                                           OP.is_gt, OP.add, accum_out=bis[:, 3:4]), 1); v += 1
                    vf.wait(e, v)
                    vf.inc(e.tensor_scalar(bis[:, 4:5], bis[:, 3:4], 255.5, None,
                                           OP.is_gt), 1); v += 1
                    vf.wait(e, v)
                    vf.inc(e.tensor_sub(bis[:, 5:6], bis[:, 2:3], bis[:, 0:1]), 1); v += 1
                    vf.wait(e, v)
                    vf.inc(e.tensor_mul(bis[:, 6:7], bis[:, 5:6], bis[:, 4:5]), 1); v += 1
                    vf.wait(e, v)
                    vf.inc(e.tensor_add(bis[:, 0:1], bis[:, 0:1], bis[:, 6:7]), 1); v += 1
                    vf.wait(e, v)
                    vf.inc(e.tensor_sub(bis[:, 5:6], bis[:, 2:3], bis[:, 1:2]), 1); v += 1
                    vf.wait(e, v)
                    vf.inc(e.tensor_scalar(bis[:, 7:8], bis[:, 4:5], -1.0, 1.0,
                                           OP.mult, OP.add), 1); v += 1
                    vf.wait(e, v)
                    vf.inc(e.tensor_mul(bis[:, 6:7], bis[:, 5:6], bis[:, 7:8]), 1); v += 1
                    vf.wait(e, v)
                    vf.inc(e.tensor_add(bis[:, 1:2], bis[:, 1:2], bis[:, 6:7]), 1); v += 1
                vf.wait(e, v)
                vf.inc(e.tensor_scalar(msk[:, :], rsc[:, :], bis[:, 0:1], None,
                                       OP.is_gt), 1); v += 1
                vf.wait(e, v)
                vf.inc(e.tensor_copy(mskb[:, :], msk[:, :]), 1); v += 1
                s_sg.wait(e, 1)
                vf.wait(e, v)
                vf.inc(e.tensor_mul(gmb[:, :], sg[:, :], msk[:, :]), 1); v += 1
                s_tm.wait(e, 16)
                s_cme.inc(e.tensor_copy(cmt[:, :], ptc[:, :]), 1)

            @blk.scalar
            def _(e):
                prime(e, s_sg, vf)
                vf.wait(e, 4)
                s_sg.inc(e.activation(sg[:, :], rsc[:, :], AF.Sigmoid), 1)

            @blk.tensor
            def _(e):
                prime(e, s_mm, s_tm, s_cst, vf, s_ln, s_ev)
                s_ln.wait(e, nev)
                s_cst.wait(e, 1)
                for tc in range(4):
                    if tc >= 2:
                        vf.wait(e, tc - 1)
                    for ec in range(8):
                        ins = e.matmul(prc[tc % 2][:, :],
                                       rwb[:, ec * 8:(ec + 1) * 8],
                                       xlnT[:, ec * 2048 + tc * 512: ec * 2048 + (tc + 1) * 512],
                                       start=(ec == 0), stop=(ec == 7))
                    s_mm.inc(ins, 1)
                vf.wait(e, 6 + 24 * 11 + 2)
                for i in range(16):
                    ins = e.transpose(ptc[:, i * 8:(i + 1) * 8],
                                      mskb[0:8, i * 128:(i + 1) * 128], idn[0:8, 0:8])
                    s_tm.inc(ins, 1)

            @blk.gpsimd
            def _(e):
                prime(e, s_go, vf)
                vf.wait(e, 6 + 24 * 11 + 3)
                s_go.inc(e.dma_start(gm_h.ap()[sq, 0], gmb[:, :]), 16)
                s_go.wait(e, 16)
                s_go.inc(e.dma_start(gm_h.ap()[sq, 1], msk[:, :]), 16)
                s_go.wait(e, 32)
        sc_r.close()

        for br in range(2):
            fsc = contextlib.ExitStack()
            with nc.Block() as blk:
                wqb = ctx.sbuf(f"l0f_wqb{sq}{br}", [128, 8 * 1536], BF16, fsc)
                wstg = ctx.sbuf(f"l0f_wstg{sq}{br}", [128, 2 * 1536], F32, fsc)
                qkv = ctx.sbuf(f"l0f_qkv{sq}{br}", [128, 2 * 1536], BF16, fsc)
                rtmp = ctx.sbuf(f"l0f_rtmp{sq}{br}", [128, 4 * 128], BF16, fsc)
                tstg = ctx.sbuf(f"l0f_tstg{sq}{br}", [128, 2 * 2 * 512], BF16, fsc)
                vstg = ctx.sbuf(f"l0f_vstg{sq}{br}", [128, 2 * 520], BF16, fsc)
                pj = [ctx.psum(f"l0f_pj{sq}{br}{j}", [128, 512], F32, fsc) for j in range(2)]
                ptr = [ctx.psum(f"l0f_ptr{sq}{br}{j}", [128, 512], BF16, fsc) for j in range(2)]
                s_wl = [ctx.sem(scope=fsc) for _ in range(2)]
                s_wc = ctx.sem(scope=fsc)
                s_pj = ctx.sem(scope=fsc)
                s_qe = ctx.sem(scope=fsc)
                s_rp = ctx.sem(scope=fsc)
                s_tr = ctx.sem(scope=fsc)
                s_te = ctx.sem(scope=fsc)
                s_va = ctx.sem(scope=fsc)
                s_od = [ctx.sem(scope=fsc, kls="gdma") for _ in range(2)]
                s_vo = [ctx.sem(scope=fsc, kls="gdma") for _ in range(2)]

                @blk.sync
                def _(e):
                    prime(e, s_wl, s_wc)
                    for ec in range(8):
                        if ec >= 2:
                            s_wc.wait(e, ec - 1)
                        s_wl[ec % 2].inc(e.dma_start(
                            wstg[:, (ec % 2) * 1536:(ec % 2 + 1) * 1536],
                            wqkv_in[br].ap()[ec * 128:(ec + 1) * 128, :]), 16)

                @blk.vector
                def _(e):
                    prime(e, s_wl, s_wc, s_pj, s_qe, s_rp, s_od, s_va, *s_vo)
                    for ec in range(8):
                        s_wl[ec % 2].wait(e, 16 * (ec // 2 + 1))
                        s_wc.inc(e.tensor_copy(wqb[:, ec * 1536:(ec + 1) * 1536],
                                               wstg[:, (ec % 2) * 1536:(ec % 2 + 1) * 1536]), 1)
                    for i in range(16):
                        sl = i % 2
                        base = sl * 1536
                        s_qe.wait(e, 3 * (i + 1))
                        for qk in range(2):
                            off = base + qk * 64
                            cs = bass.AP(ctx.cb["cos_t"], i * 16, [[256, 128], [0, 8], [1, 16]])
                            sn = bass.AP(ctx.cb["sin_t"], i * 16, [[256, 128], [0, 8], [1, 16]])
                            x1 = bass.AP(qkv, off, [[3072, 128], [192, 8], [1, 16]])
                            x2 = bass.AP(qkv, off + 16, [[3072, 128], [192, 8], [1, 16]])
                            t1 = bass.AP(rtmp, 0, [[512, 128], [16, 8], [1, 16]])
                            t2 = bass.AP(rtmp, 128, [[512, 128], [16, 8], [1, 16]])
                            t3 = bass.AP(rtmp, 256, [[512, 128], [16, 8], [1, 16]])
                            t4 = bass.AP(rtmp, 384, [[512, 128], [16, 8], [1, 16]])
                            rb = 12 * i + 6 * qk
                            s_rp.wait(e, rb)
                            s_rp.inc(e.tensor_tensor(t1, x1, cs, OP.mult), 1)
                            s_rp.wait(e, rb + 1)
                            s_rp.inc(e.tensor_tensor(t2, x1, sn, OP.mult), 1)
                            s_rp.wait(e, rb + 2)
                            s_rp.inc(e.tensor_tensor(t3, x2, sn, OP.mult), 1)
                            s_rp.wait(e, rb + 3)
                            s_rp.inc(e.tensor_tensor(t4, x2, cs, OP.mult), 1)
                            s_rp.wait(e, rb + 4)
                            s_rp.inc(e.tensor_tensor(x1, t1, t3, OP.subtract), 1)
                            s_rp.wait(e, rb + 5)
                            s_rp.inc(e.tensor_tensor(x2, t2, t4, OP.add), 1)
                        if i >= 2:
                            s_vo[i % 2].wait(e, 16 * (i // 2))
                        for h in range(8):
                            vsl = bass.AP(vstg, (i % 2) * 520 + h * 65, [[1040, 128], [1, 64]])
                            vsrc = qkv[:, base + h * 192 + 128: base + h * 192 + 192]
                            if br == 0:
                                s_va.inc(e.tensor_scalar(vsl, vsrc,
                                                         cmt[:, i * 8 + h:i * 8 + h + 1],
                                                         None, OP.mult), 1)
                                s_va.inc(e.tensor_copy(
                                    bass.AP(vstg, (i % 2) * 520 + h * 65 + 64,
                                            [[1040, 128], [1, 1]]),
                                    cmt[:, i * 8 + h:i * 8 + h + 1]), 1)
                            else:
                                s_va.inc(e.tensor_copy(vsl, vsrc), 1)
                                s_va.inc(e.memset(
                                    bass.AP(vstg, (i % 2) * 520 + h * 65 + 64,
                                            [[1040, 128], [1, 1]]), 1.0), 1)

                @blk.tensor
                def _(e):
                    prime(e, s_pj, s_tr, s_wc, s_qe, s_rp, s_te, s_ln)
                    s_wc.wait(e, 8)
                    s_ln.wait(e, nev)

                    def transposes(i):
                        for qk in range(2):
                            kt = 2 * i + qk
                            buf = ptr[kt % 2]
                            if kt >= 2:
                                s_te.wait(e, kt - 1)
                            for hp in range(4):
                                for par in range(2):
                                    h = 2 * hp + par
                                    ins = e.transpose(
                                        buf[par * 64:par * 64 + 64, hp * 128:(hp + 1) * 128],
                                        qkv[:, (i % 2) * 1536 + h * 192 + qk * 64:
                                            (i % 2) * 1536 + h * 192 + qk * 64 + 64],
                                        idn[:, :])
                            s_tr.inc(ins, 1)

                    k = 0
                    for i in range(16):
                        for cc in range(3):
                            if k >= 2:
                                s_qe.wait(e, k - 1)
                            for ec in range(8):
                                ins = e.matmul(
                                    pj[k % 2][:, :],
                                    xlnT[:, ec * 2048 + i * 128: ec * 2048 + (i + 1) * 128],
                                    wqb[:, ec * 1536 + cc * 512: ec * 1536 + (cc + 1) * 512],
                                    start=(ec == 0), stop=(ec == 7))
                            s_pj.inc(ins, 1)
                            k += 1
                        if i >= 1:
                            s_rp.wait(e, 12 * i)
                            transposes(i - 1)
                    s_rp.wait(e, 12 * 16)
                    transposes(15)

                @blk.scalar
                def _(e):
                    prime(e, s_pj, s_qe, s_tr, s_te, s_od, s_rp, s_va)

                    def tstg_evac(kt):
                        s_tr.wait(e, kt + 1)
                        if kt >= 2:
                            s_od[kt % 2].wait(e, 16 * 4 * (kt // 2))
                        s_te.inc(e.activation(tstg[:, (kt % 2) * 512:(kt % 2 + 1) * 512],
                                              ptr[kt % 2][:, :], AF.Copy), 1)

                    for i in range(16):
                        for cc in range(3):
                            k = 3 * i + cc
                            s_pj.wait(e, k + 1)
                            if i >= 2 and cc == 0:
                                s_rp.wait(e, 12 * (i - 1))
                                s_va.wait(e, 16 * (i - 1))
                                s_tr.wait(e, 2 * (i - 1))
                            s_qe.inc(e.activation(
                                qkv[:, (i % 2) * 1536 + cc * 512:
                                    (i % 2) * 1536 + (cc + 1) * 512],
                                pj[k % 2][:, :], AF.Copy), 1)
                        if i >= 1:
                            tstg_evac(2 * (i - 1))
                            tstg_evac(2 * (i - 1) + 1)
                    tstg_evac(30)
                    tstg_evac(31)

                @blk.gpsimd
                def _(e):
                    prime(e, s_od, s_te, s_va, *s_vo)
                    for i in range(16):
                        for qk in range(2):
                            kt = 2 * i + qk
                            dst = qt_h if qk == 0 else kt_h
                            s_te.wait(e, kt + 1)
                            if kt >= 2:
                                s_od[kt % 2].wait(e, 16 * 4 * (kt // 2))
                            for hp in range(4):
                                s_od[kt % 2].inc(e.dma_start(
                                    bass.AP(dst, ((sq * 2 + br) * 8 + 2 * hp) * 64 * 2048 + i * 128,
                                            [[2048, 128], [1, 128]]),
                                    tstg[:, (kt % 2) * 512 + hp * 128:
                                         (kt % 2) * 512 + (hp + 1) * 128]), 16)
                        s_va.wait(e, 16 * (i + 1))
                        if i >= 2:
                            s_vo[i % 2].wait(e, 16 * (i // 2))
                        s_vo[i % 2].inc(e.dma_start(
                            va_h.ap()[sq, br, i],
                            vstg[:, (i % 2) * 520:(i % 2 + 1) * 520]), 16)
                    for sl in range(2):
                        s_od[sl].wait(e, 16 * 64)
                        s_vo[sl].wait(e, 16 * 8)
            fsc.close()

        sc_x.close()

        sc_a = contextlib.ExitStack()
        rstage = ctx.sbuf(f"l0f_rstage{sq}", [1, 2 * 512], BF16, sc_a)
        causb = ctx.sbuf(f"l0f_causb{sq}", [128, 4 * 512], BF16, sc_a)
        qs = ctx.sbuf(f"l0f_qs{sq}", [64, 2 * 2048], BF16, sc_a)
        ks = ctx.sbuf(f"l0f_ks{sq}", [64, 2 * 2048], BF16, sc_a)
        vab = ctx.sbuf(f"l0f_vab{sq}", [128, 2 * 1040], BF16, sc_a)
        ptb = ctx.sbuf(f"l0f_ptb{sq}", [128, 16 * 512], BF16, sc_a)
        with nc.Block() as blk:
            psc = [ctx.psum(f"l0f_psc{sq}{j}", [128, 1024], F32, sc_a) for j in range(2)]
            pav = ctx.psum(f"l0f_pav{sq}", [65, 512], F32, sc_a)
            s_hl = [ctx.sem(scope=sc_a) for _ in range(2)]
            s_sc = ctx.sem(scope=sc_a)
            s_ex = ctx.sem(scope=sc_a)
            s_mk = ctx.sem(scope=sc_a)
            s_av = ctx.sem(scope=sc_a)
            s_ae = ctx.sem(scope=sc_a)
            s_rst = ctx.sem(scope=sc_a)
            s_rod = [ctx.sem(scope=sc_a, kls="gdma") for _ in range(2)]

            def sc_count(hh):
                return 40 * hh

            s_cl = ctx.sem()

            @blk.sync
            def _(e):
                prime(e, s_hl, s_av, s_cl)
                for ci in range(4):
                    s_cl.inc(e.dma_start(causb[:, ci * 512:(ci + 1) * 512],
                                         ctx.cdram[f"caus{ci}"].ap()), 16)
                for hh in range(16):
                    br, h = divmod(hh, 8)
                    if hh >= 2:
                        s_av.wait(e, 4 * (hh - 1))
                    s_hl[hh % 2].inc(e.dma_start(
                        qs[:, (hh % 2) * 2048:(hh % 2 + 1) * 2048],
                        qt_h.ap()[sq, br, h]), 16)
                    s_hl[hh % 2].inc(e.dma_start(
                        ks[:, (hh % 2) * 2048:(hh % 2 + 1) * 2048],
                        kt_h.ap()[sq, br, h]), 16)
                    s_hl[hh % 2].inc(e.dma_start(
                        vab[:, (hh % 2) * 1040:(hh % 2 + 1) * 1040],
                        bass.AP(va_h, ((sq * 2 + br) * 16) * 128 * 520 + h * 65,
                                [[520, 128], [128 * 520, 16], [1, 65]])), 16)

            @blk.tensor
            def _(e):
                prime(e, s_sc, s_av, s_hl, s_ex, s_ae, s_mk)
                kk = 0
                for hh in range(16):
                    br, h = divmod(hh, 8)
                    s_hl[hh % 2].wait(e, 48 * (hh // 2 + 1))
                    for tc in range(4):
                        g = hh * 4 + tc
                        nts = 4 * tc + 4
                        for ng in range(nts // 2):
                            if kk >= 2:
                                s_ex.wait(e, kk - 1)
                            for j in range(2):
                                nt = 2 * ng + j
                                ins = e.matmul(
                                    psc[kk % 2][:, j * 512:(j + 1) * 512],
                                    ks[:, (hh % 2) * 2048 + nt * 128:
                                       (hh % 2) * 2048 + (nt + 1) * 128],
                                    qs[:, (hh % 2) * 2048 + tc * 512:
                                       (hh % 2) * 2048 + (tc + 1) * 512],
                                    start=True, stop=True)
                            s_sc.inc(ins, 1)
                            kk += 1
                        s_ae.wait(e, g)
                        kk0 = kk - nts // 2
                        for nt in range(nts):
                            if nt >= 4 * tc:
                                s_mk.wait(e, 4 * g + (nt - 4 * tc) + 1)
                            else:
                                s_ex.wait(e, kk0 + nt // 2 + 1)
                            ins = e.matmul(
                                pav[:, :],
                                vab[:, (hh % 2) * 1040 + nt * 65:
                                    (hh % 2) * 1040 + (nt + 1) * 65],
                                ptb[:, nt * 512:(nt + 1) * 512],
                                start=(nt == 0), stop=(nt == nts - 1))
                        s_av.inc(ins, 1)

            @blk.scalar
            def _(e):
                prime(e, s_ex, s_sc)
                kk = 0
                for hh in range(16):
                    for tc in range(4):
                        nts = 4 * tc + 4
                        for ng in range(nts // 2):
                            s_sc.wait(e, kk + 1)
                            s_ex.inc(e.activation(ptb[:, ng * 1024:(ng + 1) * 1024],
                                                  psc[kk % 2][:, :], AF.Exp,
                                                  scale=0.125), 1)
                            kk += 1

            @blk.vector
            def _(e):
                prime(e, s_av, s_ae, s_rst, s_rod, s_ex, s_mk)
                s_cl.wait(e, 64)
                kk = 0
                for hh in range(16):
                    for tc in range(4):
                        g = hh * 4 + tc
                        nts = 4 * tc + 4
                        kk += nts // 2
                        for dd in range(4):
                            nt = 4 * tc + dd
                            s_ex.wait(e, kk - nts // 2 + nt // 2 + 1)
                            s_mk.inc(e.tensor_tensor(
                                ptb[:, nt * 512:(nt + 1) * 512],
                                ptb[:, nt * 512:(nt + 1) * 512],
                                causb[:, dd * 512:(dd + 1) * 512], OP.mult), 1)
                        s_av.wait(e, g + 1)
                        if g >= 2:
                            s_rod[g % 2].wait(e, 16 * (g // 2))
                        s_rst.inc(e.tensor_copy(
                            rstage[0:1, (g % 2) * 512:(g % 2 + 1) * 512],
                            pav[64:65, :]), 1)
                        s_ae.inc(e.tensor_copy(
                            avT[(hh % 2) * 64:(hh % 2) * 64 + 64,
                                (hh // 2) * 2048 + tc * 512:(hh // 2) * 2048 + (tc + 1) * 512],
                            pav[0:64, :]), 1)

            @blk.gpsimd
            def _(e):
                prime(e, s_rod, s_rst)
                for g in range(64):
                    s_rst.wait(e, g + 1)
                    s_rod[g % 2].inc(e.dma_start(
                        rs_h.ap()[sq, g:g + 1, :],
                        rstage[0:1, (g % 2) * 512:(g % 2 + 1) * 512]), 16)
                for sl in range(2):
                    s_rod[sl].wait(e, 16 * 32)
        sc_a.close()

        sc_p4 = contextlib.ExitStack()
        wob = ctx.sbuf(f"l0f_wob{sq}", [128, 8 * 1024], BF16, sc_p4)
        wstg4 = ctx.sbuf(f"l0f_wstg4{sq}", [128, 2 * 1024], F32, sc_p4)
        rows = ctx.sbuf(f"l0f_rows{sq}", [64, 512], BF16, sc_p4)
        gm32 = ctx.sbuf(f"l0f_gm32{sq}", [32, 512], F32, sc_p4)
        mk32 = ctx.sbuf(f"l0f_mk32{sq}", [32, 512], F32, sc_p4)
        scl = ctx.sbuf(f"l0f_scl{sq}", [64, 512], F32, sc_p4)
        mrec = ctx.sbuf(f"l0f_mrec{sq}", [32, 512], F32, sc_p4)
        selb2 = ctx.sbuf(f"l0f_selb{sq}", [64, 32 * 128], BF16, sc_p4)
        sclb = ctx.sbuf(f"l0f_sclb{sq}", [64, 512], BF16, sc_p4)
        outb = ctx.sbuf(f"l0f_outb{sq}", [128, 8 * 512], BF16, sc_p4)
        s_oev = ctx.sem(scope=sc_seq)
        sc_nrm = contextlib.ExitStack()
        with nc.Block() as blk:
            pbc = [ctx.psum(f"l0f_pbc{sq}{j}", [128, 512], F32, sc_nrm) for j in range(2)]
            s_rl = ctx.sem()
            s_rc = ctx.sem(scope=sc_nrm)
            s_bc = ctx.sem(scope=sc_nrm)
            s_nm = ctx.sem(scope=sc_nrm)
            s_wl = [ctx.sem(scope=sc_p4) for _ in range(2)]
            s_wc = ctx.sem(scope=sc_p4)

            @blk.sync
            def _(e):
                prime(e, s_wl, s_wc, s_rl)
                s_rl.inc(e.dma_start(rows[:, :], rs_h.ap()[sq]), 16)
                s_rl.wait(e, 16)
                s_rl.inc(e.dma_start(gm32[:, :],
                                     bass.AP(gm_h, sq * 2 * 8 * 2048,
                                             [[2048, 8], [512, 4], [1, 512]])), 16)
                s_rl.wait(e, 32)
                s_rl.inc(e.dma_start(mk32[:, :],
                                     bass.AP(gm_h, sq * 2 * 8 * 2048 + 8 * 2048,
                                             [[2048, 8], [512, 4], [1, 512]])), 16)
                s_rl.wait(e, 48)
                s_rl.inc(e.dma_start(selb2[:, :], ctx.cdram["self"].ap()), 16)
                for br in range(2):
                    for ec4 in range(4):
                        ec = br * 4 + ec4
                        if ec >= 2:
                            s_wc.wait(e, ec - 1)
                        s_wl[ec % 2].inc(e.dma_start(
                            wstg4[:, (ec % 2) * 1024:(ec % 2 + 1) * 1024],
                            wo_in[br].ap()[ec4 * 128:(ec4 + 1) * 128, :]), 16)

            @blk.vector
            def _(e):
                prime(e, s_wl, s_wc, s_rc, s_bc, s_nm)
                for ec in range(8):
                    s_wl[ec % 2].wait(e, 16 * (ec // 2 + 1))
                    s_wc.inc(e.tensor_copy(wob[:, ec * 1024:(ec + 1) * 1024],
                                           wstg4[:, (ec % 2) * 1024:(ec % 2 + 1) * 1024]), 1)
                s_rl.wait(e, 48)
                v = 0
                s_rc.inc(e.tensor_scalar(scl[0:32, :], rows[0:32, :], 1.0, None,
                                         OP.add), 1); v += 1
                s_rc.wait(e, v)
                s_rc.inc(e.tensor_sub(scl[0:32, :], scl[0:32, :], mk32[:, :]), 1); v += 1
                s_rc.wait(e, v)
                s_rc.inc(e.reciprocal(mrec[:, :], scl[0:32, :]), 1); v += 1
                s_rc.wait(e, v)
                s_rc.inc(e.tensor_mul(scl[0:32, :], gm32[:, :], mrec[:, :]), 1); v += 1
                s_rc.wait(e, v)
                s_rc.inc(e.reciprocal(scl[32:64, :], rows[32:64, :]), 1); v += 1
                s_rc.wait(e, v)
                s_rc.inc(e.tensor_copy(sclb[:, :], scl[:, :]), 1); v += 1
                for j in range(32):
                    br, rem = divmod(j, 16)
                    p, tc = divmod(rem, 4)
                    s_bc.wait(e, j + 1)
                    s_nm.inc(e.tensor_tensor(
                        avT[:, (br * 4 + p) * 2048 + tc * 512:
                            (br * 4 + p) * 2048 + (tc + 1) * 512],
                        avT[:, (br * 4 + p) * 2048 + tc * 512:
                            (br * 4 + p) * 2048 + (tc + 1) * 512],
                        pbc[j % 2][:, :], OP.mult), 1)

            @blk.tensor
            def _(e):
                prime(e, s_rc, s_bc, s_nm)
                s_rc.wait(e, 6)
                s_rl.wait(e, 64)
                for j in range(32):
                    if j >= 2:
                        s_nm.wait(e, j - 1)
                    s_bc.inc(e.matmul(pbc[j % 2][:, :],
                                      selb2[:, j * 128:(j + 1) * 128],
                                      sclb[:, :], start=True, stop=True), 1)
        sc_nrm.close()
        for tc in range(4):
            wsc2 = contextlib.ExitStack()
            with nc.Block() as blk:
                pw = [ctx.psum(f"l0f_pw{sq}{tc}{j}", [128, 512], F32, wsc2) for j in range(8)]
                s_wm = ctx.sem(scope=wsc2)

                @blk.tensor
                def _(e):
                    prime(e, s_wm, s_wc)
                    s_wc.wait(e, 8)
                    for eci in range(8):
                        for hd8 in range(8):
                            br, hdc = divmod(hd8, 4)
                            ins = e.matmul(
                                pw[eci][:, :],
                                wob[:, (br * 4 + hdc) * 1024 + eci * 128:
                                    (br * 4 + hdc) * 1024 + (eci + 1) * 128],
                                avT[:, (br * 4 + hdc) * 2048 + tc * 512:
                                    (br * 4 + hdc) * 2048 + (tc + 1) * 512],
                                start=(hd8 == 0), stop=(hd8 == 7))
                        s_wm.inc(ins, 1)

                @blk.scalar
                def _(e):
                    prime(e, s_oev, s_wm)
                    for eci in range(8):
                        s_wm.wait(e, eci + 1)
                        s_oev.inc(e.activation(outb[:, eci * 512:(eci + 1) * 512],
                                               pw[eci][:, :], AF.Copy), 1)
            wsc2.close()
            emit_out_ln_add(ctx, f"l0f_o{sq}_{tc}", outb, 512, sq * 2048 + tc * 512)
        sc_p4.close()
        sc_seq.close()


def build_program(upto="all", taps=()):
    stack = contextlib.ExitStack()
    nc = bass.Bass("TRN2", target_bir_lowering=False)
    ctx = Ctx(nc, stack)
    ctx.tap_names = tuple(taps)

    x_in = nc.dram_tensor("x", [T_TOK, E], F32, kind="ExternalInput")
    y_out = nc.dram_tensor("y", [T_TOK, E], F32, kind="ExternalOutput")
    cdefs = host_constants()
    cdram = {}
    for k, v in cdefs.items():
        dt = BF16 if v.dtype == BF else F32
        cdram[k] = nc.dram_tensor(k, list(v.shape), dt, kind="ExternalInput")

    ctx.cdram = cdram
    for k, v in cdefs.items():
        if k in ("seln", "self", "caus0", "caus1", "caus2", "caus3"):
            continue
        dt = BF16 if v.dtype == BF else F32
        ctx.cb[k] = ctx.sbuf("c_" + k, list(v.shape), dt)
    ctx.b["idn"] = ctx.cb["idn"]
    xres = ctx.sbuf("xres", [128, N_TILE * 1024], BF16)
    ctx.sbuf("g_lnscr", [128, 2 * 1024], BF16)
    sc_ph0 = contextlib.ExitStack()
    stage = ctx.sbuf("stage", [128, 4 * 1024], F32, sc_ph0)

    s_xd = [ctx.sem(f"xd{j}") for j in range(4)]
    s_xc = ctx.sem("xc")
    s_cd = ctx.sem("cd")
    ncst = len(cdefs)
    with nc.Block() as blk:
        @blk.sync
        def _(e):
            for i in range(N_TILE):
                if i >= 4:
                    s_xc.wait(e, i - 3)
                s_xd[i % 4].inc(e.dma_start(
                    stage[:, (i % 4) * 1024:(i % 4 + 1) * 1024],
                    x_in.ap()[i * 128:(i + 1) * 128, :]), 16)

        @blk.gpsimd
        def _(e):
            for k in cdefs:
                if k in ("seln", "self", "caus0", "caus1", "caus2", "caus3"):
                    continue
                s_cd.inc(e.dma_start(ctx.cb[k][:, :], cdram[k].ap()), 16)
            s_cd.wait(e, 16 * (ncst - 6))

        @blk.vector
        def _(e):
            for i in range(N_TILE):
                s_xd[i % 4].wait(e, 16 * (i // 4 + 1))
                s_xc.inc(e.tensor_copy(
                    tile(xres, i),
                    stage[:, (i % 4) * 1024:(i % 4 + 1) * 1024]), 1)

    sc_ph0.close()
    if upto != "copy":
        order = ["l0a", "l0f", "l1a", "l1f"]
        emits = {"l0a": emit_l0_attn, "l0f": emit_l0_ffn,
                 "l1a": emit_l1_attn, "l1f": emit_l1_ffn}
        sel = order if upto == "all" else upto.split(",")
        for s in sel:
            emits[s](ctx)

    s_oc = ctx.sem("oc")
    s_od = [ctx.sem("od0"), ctx.sem("od1")]
    sc_out = contextlib.ExitStack()
    stage = ctx.sbuf("ostage", [128, 2 * 1024], F32, sc_out)
    with nc.Block() as blk:
        @blk.scalar
        def _(e):
            for i in range(N_TILE):
                if i >= 2:
                    s_od[i % 2].wait(e, 16 * (i // 2))
                s_oc.inc(e.activation(stage[:, (i % 2) * 1024:(i % 2 + 1) * 1024],
                                      tile(xres, i), AF.Copy), 1)

        @blk.sync
        def _(e):
            for i in range(N_TILE):
                s_oc.wait(e, i + 1)
                s_od[i % 2].inc(e.dma_start(
                    y_out.ap()[i * 128:(i + 1) * 128, :],
                    stage[:, (i % 2) * 1024:(i % 2 + 1) * 1024]), 16)
            s_od[0].wait(e, 16 * 16)
            s_od[1].wait(e, 16 * 16)
    sc_out.close()

    nc._ctx_keepalive = stack
    return nc, ctx


def make_in_maps(inputs, upto="all"):
    cdefs = host_constants()
    x = np.ascontiguousarray(np.asarray(inputs["L0_attn_x"], dtype=np.float32))
    sel = ["l0a", "l0f", "l1a", "l1f"] if upto == "all" else upto.split(",")
    shared = dict(cdefs)
    if "l0a" in sel:
        shared["l0a_wq"] = np.ascontiguousarray(np.asarray(inputs["L0_attn_Wq"], dtype=np.float32))
        shared["l0a_wk"] = np.ascontiguousarray(np.asarray(inputs["L0_attn_Wk"], dtype=np.float32))
        shared["l0a_wv"] = np.ascontiguousarray(np.asarray(inputs["L0_attn_Wv"], dtype=np.float32))
        shared["l0a_wo"] = np.ascontiguousarray(np.asarray(inputs["L0_attn_Wo"], dtype=np.float32))
    if "l0f" in sel:
        shared["l0f_rw"] = np.ascontiguousarray(np.asarray(inputs["L0_ffn_mosa_router_w"], dtype=np.float32))
        shared["l0f_mqkv"] = np.ascontiguousarray(np.asarray(inputs["L0_ffn_mosa_wqkv"], dtype=np.float32).reshape(1024, 1536))
        shared["l0f_dqkv"] = np.ascontiguousarray(np.asarray(inputs["L0_ffn_dense_wqkv"], dtype=np.float32).reshape(1024, 1536))
        shared["l0f_mwo"] = np.ascontiguousarray(np.asarray(inputs["L0_ffn_mosa_wo"], dtype=np.float32))
        shared["l0f_dwo"] = np.ascontiguousarray(np.asarray(inputs["L0_ffn_dense_wo"], dtype=np.float32))
    if "l1a" in sel:
        shared["l1a_k"] = np.ascontiguousarray(np.asarray(inputs["L1_attn_k"], dtype=np.float32))
        shared["l1a_v"] = np.ascontiguousarray(np.asarray(inputs["L1_attn_v"], dtype=np.float32))
    if "l1f" in sel:
        UTr, Vr = host_l1f_factors(inputs["L1_ffn_G1_R"], inputs["L1_ffn_G2_R"])
        UTi, Vi = host_l1f_factors(inputs["L1_ffn_G1_I"], inputs["L1_ffn_G2_I"])
        shared["l1f_ut"] = np.ascontiguousarray(np.stack([UTr, UTi]))
        shared["l1f_v"] = np.ascontiguousarray(np.stack([Vr, Vi]))
        shared["l1f_br"] = np.ascontiguousarray(np.asarray(inputs["L1_ffn_bias_real"], dtype=np.float32))
        shared["l1f_bi"] = np.ascontiguousarray(np.asarray(inputs["L1_ffn_bias_imag"], dtype=np.float32))
    in_maps = []
    for c in range(8):
        m = {"x": np.ascontiguousarray(x[2 * c:2 * c + 2].reshape(T_TOK, E))}
        m.update(shared)
        in_maps.append(m)
    return in_maps


def run(inputs, upto="all", taps=()):
    nc, ctx = build_program(upto=upto, taps=taps)
    in_maps = make_in_maps(inputs, upto=upto)
    res = run_bass_kernel_spmd(nc, in_maps, core_ids=list(range(8)))
    return res


def kernel(**inputs):
    res = run(inputs, upto="all")
    out = np.concatenate([r["y"].reshape(2, S_LEN, E) for r in res.results], axis=0)
    return out.astype(np.float32)


# version 09
# speedup vs baseline: 1.0020x; 1.0020x over previous
import contextlib
import math
import numpy as np
import ml_dtypes
import concourse.bass as bass
import concourse.mybir as mybir
from concourse.bass_utils import run_bass_kernel_spmd

F32 = mybir.dt.float32
BF16 = mybir.dt.bfloat16
AF = mybir.ActivationFunctionType
OP = mybir.AluOpType
AX = mybir.AxisListType

T_TOK = 4096
S_LEN = 2048
N_TILE = 32
S_TILE = 16
E = 1024
NCH = 8
BF = ml_dtypes.bfloat16


def host_constants():
    c = {}
    c["idn"] = np.eye(128).astype(BF)
    C64 = (np.eye(64, dtype=np.float64) - 1.0 / 64.0) / 8.0
    Cb = np.zeros((128, 128), dtype=np.float64)
    Cb[:64, :64] = C64
    Cb[64:, 64:] = C64
    c["cblk"] = Cb.astype(BF)
    ind = np.zeros((128, 2), dtype=np.float32)
    ind[:64, 0] = 1.0
    ind[64:, 1] = 1.0
    c["ind2"] = ind.astype(BF)
    c["ones_row"] = np.ones((1, 128), dtype=np.float32).astype(BF)
    c["ones_col"] = np.ones((128, 1), dtype=np.float32).astype(BF)
    seln = np.zeros((64, 32 * 128), dtype=np.float32)
    for p in range(8):
        for tc in range(4):
            j = p * 4 + tc
            seln[(2 * p) * 4 + tc, j * 128:j * 128 + 64] = 1.0
            seln[(2 * p + 1) * 4 + tc, j * 128 + 64:j * 128 + 128] = 1.0
    c["seln"] = seln.astype(BF)
    self_ = np.zeros((64, 32 * 128), dtype=np.float32)
    for br in range(2):
        for p in range(4):
            for tc in range(4):
                j = br * 16 + p * 4 + tc
                self_[br * 32 + (2 * p) * 4 + tc, j * 128:j * 128 + 64] = 1.0
                self_[br * 32 + (2 * p + 1) * 4 + tc, j * 128 + 64:j * 128 + 128] = 1.0
    c["self"] = self_.astype(BF)
    half = 16
    inv = 10000.0 ** (-np.arange(half, dtype=np.float64) / half)
    pos = np.arange(S_LEN, dtype=np.float64)
    ang = pos[:, None] * inv[None, :]
    def _tiled(a):
        return np.ascontiguousarray(
            a.astype(np.float32).reshape(16, 128, 16).transpose(1, 0, 2).reshape(128, 256))
    cosv, sinv = np.cos(ang), np.sin(ang)
    chi, shi = cosv.astype(np.float32).astype(np.float64), sinv.astype(np.float32).astype(np.float64)
    c["cos_t"] = _tiled(cosv)
    c["sin_t"] = _tiled(sinv)
    c["cos_lo"] = _tiled(cosv - chi)
    c["sin_lo"] = _tiled(sinv - shi)
    for k, off in enumerate((0, 128, 256, 384)):
        n = np.arange(128)[:, None]
        t = np.arange(512)[None, :]
        c[f"caus{k}"] = (t >= n + off).astype(np.float32).astype(BF)
    return c


def host_l1f_factors(G1, G2):
    G1 = np.asarray(G1, dtype=np.float32)
    G2 = np.asarray(G2, dtype=np.float32)
    UT = np.zeros((4, 512, 512), dtype=np.float32)
    V = np.zeros((4, 512, 512), dtype=np.float32)
    b_i, d_i, g_i, e_i = np.arange(32), np.arange(32), np.arange(16), np.arange(16)
    for c in range(4):
        A = G1[0, :, c, :]
        B = G2[c, :, 0, :]
        bb, dd, gg = np.meshgrid(b_i, d_i, g_i, indexing="ij")
        UT[c, bb * 16 + gg, dd * 16 + gg] = A[bb, dd]
        bb, gg, ee = np.meshgrid(b_i, g_i, e_i, indexing="ij")
        V[c, bb * 16 + gg, bb * 16 + ee] = B[ee, gg]
    return UT, V


class PSem:
    def __init__(self, handle):
        self.h = handle
        self.base = 0
        self.planned = 0

    def wait(self, e, v):
        e.wait_ge(self.h, self.base + v)

    def inc(self, ins, n=1):
        ins.then_inc(self.h, n)
        self.planned += n


class Ctx:
    def __init__(self, nc, stack):
        self.nc = nc
        self.stack = stack
        self._i = 0
        self.b = {}
        self.cb = {}
        self.tap_names = ()
        self._sem_pool = {}

    def sem(self, name=None, scope=None, kls="c"):
        pool = self._sem_pool.setdefault(kls, [])
        if scope is not None and pool:
            ps = pool.pop()
        else:
            self._i += 1
            ps = PSem(self.stack.enter_context(
                self.nc.semaphore(name or f"s{self._i}")))
        if scope is not None:
            scope.callback(self._ret_sem, ps, kls)
        return ps

    def _ret_sem(self, ps, kls="c"):
        ps.base += ps.planned
        ps.planned = 0
        self._sem_pool.setdefault(kls, []).append(ps)

    def sbuf(self, name, shape, dtype, scope=None):
        h = (scope or self.stack).enter_context(self.nc.sbuf_tensor(name, shape, dtype))
        self.b[name] = h
        return h

    def psum(self, name, shape, dtype, scope=None):
        h = (scope or self.stack).enter_context(self.nc.psum_tensor(name, shape, dtype))
        self.b[name] = h
        return h

    def tap(self, name, ap, rows, cols):
        if name not in self.tap_names:
            return
        nc = self.nc
        td = nc.dram_tensor("tap_" + name, [rows, cols], F32, kind="ExternalOutput")
        sc = contextlib.ExitStack()
        stg = self.sbuf("tapstg_" + name, [rows, cols], F32, sc)
        s = self.sem(scope=sc)
        with nc.Block() as blk:
            @blk.vector
            def _(e):
                prime(e, s)
                s.inc(e.tensor_copy(stg[:, :], ap), 1)

            @blk.sync
            def _(e):
                s.wait(e, 1)
                s.inc(e.dma_start(td.ap(), stg[:, :]), 16)
                s.wait(e, 17)
        sc.close()


def tile(x, i, w=1024):
    return x[:, i * w:(i + 1) * w]


def prime(e, *sems):
    for ps in sems:
        for x in (ps if isinstance(ps, (list, tuple)) else [ps]):
            x.wait(e, 0)


def emit_ln(ctx, blk, name, src_f, dst_f, n, src_gate=None, apply_gate=None,
            done=None, scope=None):
    st = ctx.sbuf(f"lnst_{name}", [128, 4 * n], F32, scope)
    scr = ctx.b["g_lnscr"]
    m2 = ctx.sbuf(f"lnm2_{name}", [128, n], F32, scope)
    C_SUM, C_SS, C_R, C_B = 0, n, 2 * n, 3 * n
    sf = ctx.sem(scope=scope)
    vf = ctx.sem(scope=scope)

    @blk.scalar
    def _(e):
        prime(e, sf, vf, *( [done] if done is not None else [] ))
        for j in range(n):
            if src_gate is not None:
                g = src_gate(j)
                if g is not None:
                    g[0].wait(e, g[1])
            if j >= 2:
                sf.wait(e, j - 1)
            sf.inc(e.activation(scr[:, (j % 2) * 1024:(j % 2 + 1) * 1024],
                                src_f(j), AF.Square,
                                accum_out=st[:, C_SS + j:C_SS + j + 1]), 1)
        sf.wait(e, n)
        vf.wait(e, n + 4)
        sf.inc(e.activation(st[:, C_R:C_R + n], st[:, C_SS:C_SS + n], AF.Sqrt), 1)
        vf.wait(e, n + 6)
        for j in range(n):
            if apply_gate is not None:
                g = apply_gate(j)
                if g is not None:
                    g[0].wait(e, g[1])
            ins = e.activation(dst_f(j), src_f(j), AF.Identity,
                               scale=m2[:, j:j + 1],
                               bias=st[:, C_B + j:C_B + j + 1])
            if done is not None:
                done.inc(ins, 1)

    @blk.vector
    def _(e):
        prime(e, vf, sf)
        for j in range(n):
            if src_gate is not None:
                g = src_gate(j)
                if g is not None:
                    g[0].wait(e, g[1])
            vf.inc(e.reduce_sum(st[:, C_SUM + j:C_SUM + j + 1], src_f(j),
                                axis=AX.X), 1)
        vf.wait(e, n)
        vf.inc(e.tensor_scalar(st[:, C_SUM:C_SUM + n], st[:, C_SUM:C_SUM + n],
                               1.0 / 1024.0, None, OP.mult), 1)
        sf.wait(e, n)
        vf.wait(e, n + 1)
        vf.inc(e.tensor_mul(m2[:, 0:n], st[:, C_SUM:C_SUM + n],
                            st[:, C_SUM:C_SUM + n]), 1)
        vf.wait(e, n + 2)
        vf.inc(e.scalar_tensor_tensor(st[:, C_R:C_R + n], st[:, C_SS:C_SS + n],
                                      1.0 / 1024.0, m2[:, 0:n],
                                      OP.mult, OP.subtract), 1)
        vf.wait(e, n + 3)
        vf.inc(e.tensor_scalar(st[:, C_SS:C_SS + n], st[:, C_R:C_R + n],
                               1e-5, None, OP.add), 1)
        sf.wait(e, n + 1)
        vf.wait(e, n + 4)
        vf.inc(e.reciprocal(m2[:, 0:n], st[:, C_R:C_R + n]), 1)
        vf.wait(e, n + 5)
        vf.inc(e.scalar_tensor_tensor(st[:, C_B:C_B + n], st[:, C_SUM:C_SUM + n],
                                      -1.0, m2[:, 0:n],
                                      OP.mult, OP.mult), 1)


def emit_ln_transpose(ctx, blk, name, tiles, dst, tcols, coloff,
                      src_gate=None, scope=None, sem_scope=None):
    n = len(tiles)
    nring = min(8, ((n + 3) // 4) * 4)
    ring = ctx.sbuf(f"ring_{name}", [128, nring * 1024], BF16, scope)
    pa = ctx.psum(f"pta_{name}", [128, 512], BF16, scope)
    pb = ctx.psum(f"ptb_{name}", [128, 512], BF16, scope)
    idn = ctx.b["idn"]
    s_app = ctx.sem(scope=scope)
    s_tr = ctx.sem(scope=scope)
    s_ev = ctx.sem(scope=sem_scope or scope)
    nsup = (n + 3) // 4

    def ring_gate(j):
        sup_prev = j // 4 - (nring // 4)
        if sup_prev < 0:
            return None
        return (s_ev, NCH * (sup_prev + 1))

    emit_ln(ctx, blk, name, lambda j: tile(ctx.b["xres"], tiles[j]),
            lambda j: tile(ring, j % nring), n, src_gate=src_gate,
            apply_gate=ring_gate, done=s_app, scope=scope)

    @blk.tensor
    def _(e):
        prime(e, s_app, s_tr, s_ev)
        k = 0
        for sup in range(nsup):
            j0 = sup * 4
            cnt = min(4, n - j0)
            s_app.wait(e, j0 + cnt)
            for c in range(NCH):
                buf = pa if (k % 2 == 0) else pb
                if k >= 2:
                    s_ev.wait(e, k - 1)
                for j in range(cnt):
                    ins = e.transpose(buf[:, j * 128:(j + 1) * 128],
                                      tile(ring, (j0 + j) % nring)[:, c * 128:(c + 1) * 128],
                                      idn[:, :])
                s_tr.inc(ins, 1)
                k += 1

    @blk.vector
    def _(e):
        prime(e, s_tr, s_ev)
        k = 0
        for sup in range(nsup):
            j0 = sup * 4
            cnt = min(4, n - j0)
            for c in range(NCH):
                buf = pa if (k % 2 == 0) else pb
                s_tr.wait(e, k + 1)
                s_ev.inc(e.tensor_copy(
                    dst[:, c * tcols + coloff + j0 * 128:
                        c * tcols + coloff + j0 * 128 + cnt * 128],
                    buf[:, 0:cnt * 128]), 1)
                k += 1
    return NCH * nsup, s_ev


def emit_out_ln_add(ctx, name, src, tcols, tok0, src_ready=None):
    nc = ctx.nc
    n = tcols // 128
    nring = min(8, n)
    sc = contextlib.ExitStack()
    with nc.Block() as blk:
        ring = ctx.sbuf(f"oring_{name}", [128, nring * 1024], BF16, sc)
        lnout = ctx.sbuf(f"olnout_{name}", [128, 2 * 1024], BF16, sc)
        pa = ctx.psum(f"opta_{name}", [128, 512], BF16, sc)
        pb = ctx.psum(f"optb_{name}", [128, 512], BF16, sc)
        idn = ctx.b["idn"]
        s_tr = ctx.sem(scope=sc)
        s_ev = ctx.sem(scope=sc)
        s_app = ctx.sem(scope=sc)
        s_add = ctx.sem(scope=sc)

        @blk.tensor
        def _(e):
            prime(e, s_tr, s_ev, s_app, s_add)
            if src_ready is not None:
                src_ready[0].wait(e, src_ready[1])
            k = 0
            for j in range(n):
                for cp in range(2):
                    buf = pa if (k % 2 == 0) else pb
                    if k >= 2:
                        s_ev.wait(e, k - 1)
                    if j >= nring and cp == 0:
                        s_app.wait(e, j - nring + 1)
                    for cc in range(4):
                        c = cp * 4 + cc
                        ins = e.transpose(
                            buf[:, cc * 128:(cc + 1) * 128],
                            src[:, c * tcols + j * 128: c * tcols + (j + 1) * 128],
                            idn[:, :])
                    s_tr.inc(ins, 1)
                    k += 1

        @blk.vector
        def _(e):
            prime(e, s_tr, s_ev, s_app, s_add)
            for k in range(2 * n):
                j, cp = divmod(k, 2)
                buf = pa if (k % 2 == 0) else pb
                s_tr.wait(e, k + 1)
                s_ev.inc(e.tensor_copy(tile(ring, j % nring)[:, cp * 512:(cp + 1) * 512],
                                       buf[:, 0:512]), 1)

        def src_gate(j):
            return (s_ev, 2 * (j + 1))

        def apply_gate(j):
            if j < 2:
                return None
            return (s_add, j - 1)

        emit_ln(ctx, blk, f"ln2_{name}", lambda j: tile(ring, j % nring),
                lambda j: tile(lnout, j % 2), n,
                src_gate=src_gate, apply_gate=apply_gate, done=s_app, scope=sc)

        @blk.vector
        def _(e):
            prime(e, s_add)
            for j in range(n):
                s_app.wait(e, j + 1)
                s_add.inc(e.tensor_add(tile(ctx.b["xres"], tok0 // 128 + j),
                                       tile(ctx.b["xres"], tok0 // 128 + j),
                                       tile(lnout, j % 2)), 1)
    ctx.tap(f"lnst_{name}", ctx.b[f"lnst_ln2_{name}"][:, :], 128, 4 * n)
    ctx.tap(f"ring_{name}", ctx.b[f"oring_{name}"][:, 0:1024], 128, 1024)
    sc.close()


def emit_l1_ffn(ctx):
    nc = ctx.nc
    ut_in = nc.dram_tensor("l1f_ut", [2, 4, 512, 512], F32, kind="ExternalInput")
    v_in = nc.dram_tensor("l1f_v", [2, 4, 512, 512], F32, kind="ExternalInput")
    br_in = nc.dram_tensor("l1f_br", [512], F32, kind="ExternalInput")
    bi_in = nc.dram_tensor("l1f_bi", [512], F32, kind="ExternalInput")

    sc_all = contextlib.ExitStack()
    sc_w = contextlib.ExitStack()
    wT = ctx.sbuf("l1f_wT", [128, 12 * 512], BF16, sc_all)
    bias = ctx.sbuf("l1f_bias", [128, 8], F32, sc_all)
    s_db = ctx.sem(scope=sc_all)

    with nc.Block() as blk:
        uv = ctx.sbuf("l1f_uv", [128, 4 * 512], F32, sc_w)
        uvb = ctx.sbuf("l1f_uvb", [128, 64 * 512], BF16, sc_w)
        pw = ctx.psum("l1f_pw", [128, 512], F32, sc_w)
        pw2 = ctx.psum("l1f_pw2", [128, 512], F32, sc_w)
        s_d = [ctx.sem(scope=sc_w) for _ in range(4)]
        s_c = ctx.sem(scope=sc_w)
        s_m = ctx.sem(scope=sc_w)
        s_w = ctx.sem(scope=sc_w)

        def ucol(uvsel, ri, c, kch):
            return (uvsel * 32 + ri * 16 + c * 4 + kch) * 512

        @blk.sync
        def _(e):
            prime(e, s_d, s_db, s_c)
            for uvsel in range(2):
                src = ut_in if uvsel == 0 else v_in
                for ri in range(2):
                    for c in range(4):
                        for kch in range(4):
                            k = uvsel * 32 + ri * 16 + c * 4 + kch
                            if k >= 4:
                                s_c.wait(e, k - 3)
                            s_d[k % 4].inc(e.dma_start(
                                uv[:, (k % 4) * 512:(k % 4 + 1) * 512],
                                src.ap()[ri, c, kch * 128:(kch + 1) * 128, :]), 16)
            s_db.inc(e.dma_start(bias[:, 0:4],
                                 bass.AP(br_in, 0, [[1, 128], [128, 4]]),
                                 allow_slow_non_contiguous=True), 16)
            s_db.inc(e.dma_start(bias[:, 4:8],
                                 bass.AP(bi_in, 0, [[1, 128], [128, 4]]),
                                 allow_slow_non_contiguous=True), 16)

        @blk.vector
        def _(e):
            prime(e, s_d, s_c)
            for k in range(64):
                s_d[k % 4].wait(e, 16 * (k // 4 + 1))
                s_c.inc(e.tensor_copy(uvb[:, k * 512:(k + 1) * 512],
                                      uv[:, (k % 4) * 512:(k % 4 + 1) * 512]), 1)

        @blk.tensor
        def _(e):
            prime(e, s_c, s_m, s_w)
            s_c.wait(e, 64)
            k = 0
            for ri in range(2):
                for mq in range(4):
                    buf = pw if (k % 2 == 0) else pw2
                    if k >= 2:
                        s_w.wait(e, k - 1)
                    for c in range(4):
                        for kch in range(4):
                            ins = e.matmul(
                                buf[:, :],
                                uvb[:, ucol(0, ri, c, kch) + mq * 128:
                                    ucol(0, ri, c, kch) + (mq + 1) * 128],
                                uvb[:, ucol(1, ri, c, kch):
                                    ucol(1, ri, c, kch) + 512],
                                start=(c == 0 and kch == 0),
                                stop=(c == 3 and kch == 3))
                    s_m.inc(ins, 1)
                    k += 1

        @blk.scalar
        def _(e):
            prime(e, s_m, s_w)
            for k in range(8):
                buf = pw if (k % 2 == 0) else pw2
                s_m.wait(e, k + 1)
                s_w.inc(e.activation(wT[:, k * 512:(k + 1) * 512],
                                     buf[:, :], AF.Copy), 1)
            s_w.wait(e, 8)
            for mq in range(4):
                e.activation(wT[:, (8 + mq) * 512:(9 + mq) * 512],
                             wT[:, (4 + mq) * 512:(5 + mq) * 512],
                             AF.Copy, scale=-1.0)
    ctx.tap("uvb_ut00", uvb[:, 0:512], 128, 512)
    ctx.tap("uvb_v00", uvb[:, 32 * 512:33 * 512], 128, 512)
    ctx.tap("wTr0", wT[:, 0:512], 128, 512)
    ctx.tap("wTi0", wT[:, 4 * 512:5 * 512], 128, 512)
    sc_w.close()

    sc_x = contextlib.ExitStack()
    sc_ln = contextlib.ExitStack()
    xlnT = ctx.sbuf("l1f_xlnT", [128, NCH * T_TOK], BF16, sc_x)
    with nc.Block() as blk:
        nev, s_ln = emit_ln_transpose(ctx, blk, "l1f_ln1", list(range(N_TILE)),
                                      xlnT, T_TOK, 0, scope=sc_ln, sem_scope=sc_x)
    sc_ln.close()
    ctx.tap("xlnT_c0", xlnT[:, 0:512], 128, 512)
    ctx.tap("xlnT_c4", xlnT[:, 4 * T_TOK:4 * T_TOK + 512], 128, 512)

    for g in range(T_TOK // 512):
        gsc = contextlib.ExitStack()
        psc = contextlib.ExitStack()
        outbuf = ctx.sbuf(f"l1f_out{g}", [128, NCH * 512], BF16, gsc)
        s_mm = ctx.sem(scope=gsc)
        s_oev = ctx.sem(scope=gsc)
        with nc.Block() as blk:
            pts = [ctx.psum(f"l1f_pg{g}_{o}", [128, 512], F32, psc) for o in range(8)]

            @blk.tensor
            def _(e):
                prime(e, s_mm, s_ln)
                s_ln.wait(e, nev)
                for q in range(8):
                    ri, be = divmod(q, 4)
                    for part in range(2):
                        if ri == 0:
                            wreg = 0 if part == 0 else 8
                        else:
                            wreg = 4 if part == 0 else 0
                        for cq in range(4):
                            ins = e.matmul(
                                pts[q][:, :],
                                wT[:, (wreg + cq) * 512 + be * 128:
                                    (wreg + cq) * 512 + (be + 1) * 128],
                                xlnT[:, (part * 4 + cq) * T_TOK + g * 512:
                                     (part * 4 + cq) * T_TOK + (g + 1) * 512],
                                start=(part == 0 and cq == 0),
                                stop=(part == 1 and cq == 3))
                    s_mm.inc(ins, 1)

            @blk.scalar
            def _(e):
                prime(e, s_oev, s_mm)
                s_db.wait(e, 32)
                for q in range(8):
                    ri, be = divmod(q, 4)
                    s_mm.wait(e, q + 1)
                    s_oev.inc(e.activation(outbuf[:, q * 512:(q + 1) * 512],
                                           pts[q][:, :], AF.Identity,
                                           bias=bias[:, ri * 4 + be:ri * 4 + be + 1]), 1)

        psc.close()
        if g == 0:
            ctx.tap("outbuf0", outbuf[:, 0:8 * 512], 128, 8 * 512)
        emit_out_ln_add(ctx, f"l1f_g{g}", outbuf, 512, g * 512,
                        src_ready=(s_oev, 8))
        gsc.close()
    sc_x.close()
    sc_all.close()


def emit_l1_attn(ctx):
    nc = ctx.nc
    k_in = nc.dram_tensor("l1a_k", [8192, 1024], F32, kind="ExternalInput")
    v_in = nc.dram_tensor("l1a_v", [8192, 1024], F32, kind="ExternalInput")
    kT_h = nc.dram_tensor("l1a_kT", [1024, 8192], BF16)
    kb_h = nc.dram_tensor("l1a_kb", [8192, 1024], BF16)
    vb_h = nc.dram_tensor("l1a_vb", [8192, 1024], BF16)
    idn = ctx.b["idn"]

    sc_all = contextlib.ExitStack()
    mb = ctx.sbuf("l1a_mb", [128, 8 * 1024], BF16, sc_all)

    sc_p = contextlib.ExitStack()
    with nc.Block() as blk:
        stg = ctx.sbuf("l1a_stg", [128, 2 * 1024], F32, sc_p)
        kbb = ctx.sbuf("l1a_kbb", [128, 2 * 1024], BF16, sc_p)
        ktst = ctx.sbuf("l1a_ktst", [128, 2 * 1024], BF16, sc_p)
        pt = ctx.psum("l1a_pt", [128, 1024], BF16, sc_p)
        pt2 = ctx.psum("l1a_pt2", [128, 1024], BF16, sc_p)
        s_l = [ctx.sem(scope=sc_p) for _ in range(2)]
        s_cst = ctx.sem(scope=sc_p)
        s_tr = ctx.sem(scope=sc_p)
        s_te = ctx.sem(scope=sc_p)
        s_out = [ctx.sem(scope=sc_p, kls="gdma") for _ in range(4)]

        def outs_done(e, nk):
            for sl in range(4):
                cnt = (nk + 3 - sl) // 4
                if cnt > 0:
                    s_out[sl].wait(e, 16 * cnt)

        @blk.sync
        def _(e):
            prime(e, s_l, s_cst)
            for i in range(128):
                if i >= 2:
                    s_cst.wait(e, i - 1)
                src = k_in if i % 2 == 0 else v_in
                ch = i // 2
                s_l[i % 2].inc(e.dma_start(
                    stg[:, (i % 2) * 1024:(i % 2 + 1) * 1024],
                    src.ap()[ch * 128:(ch + 1) * 128, :]), 16)

        def _T(j):
            if j < 0:
                return 0
            return 10 * (j // 2) + (9 if j % 2 == 0 else 10)

        @blk.vector
        def _(e):
            prime(e, s_l, s_cst, s_out, s_tr)
            for i in range(128):
                s_l[i % 2].wait(e, 16 * (i // 2 + 1))
                if i >= 2:
                    outs_done(e, _T(i - 2))
                    if i % 2 == 0:
                        s_tr.wait(e, i // 2)
                s_cst.inc(e.tensor_copy(
                    kbb[:, (i % 2) * 1024:(i % 2 + 1) * 1024],
                    stg[:, (i % 2) * 1024:(i % 2 + 1) * 1024]), 1)

        @blk.tensor
        def _(e):
            prime(e, s_cst, s_tr, s_te)
            for ch in range(64):
                buf = pt if ch % 2 == 0 else pt2
                s_cst.wait(e, 2 * ch + 1)
                if ch >= 2:
                    s_te.wait(e, ch - 1)
                for dc in range(8):
                    ins = e.transpose(buf[:, dc * 128:(dc + 1) * 128],
                                      kbb[:, dc * 128:(dc + 1) * 128],
                                      idn[:, :])
                s_tr.inc(ins, 1)

        @blk.scalar
        def _(e):
            prime(e, s_tr, s_te, s_out)
            for ch in range(64):
                buf = pt if ch % 2 == 0 else pt2
                s_tr.wait(e, ch + 1)
                if ch >= 2:
                    outs_done(e, _T(2 * ch - 4))
                s_te.inc(e.activation(ktst[:, (ch % 2) * 1024:(ch % 2 + 1) * 1024],
                                      buf[:, :], AF.Copy), 1)

        @blk.gpsimd
        def _(e):
            prime(e, s_out, s_cst, s_te)
            kdma = 0

            def issue(out_ap, in_ap):
                nonlocal kdma
                sl = kdma % 4
                if kdma >= 4:
                    s_out[sl].wait(e, 16 * (kdma // 4))
                s_out[sl].inc(e.dma_start(out_ap, in_ap), 16)
                kdma += 1

            for i in range(128):
                if i % 2 == 0:
                    ch = i // 2
                    s_cst.wait(e, i + 1)
                    issue(kb_h.ap()[ch * 128:(ch + 1) * 128, :], kbb[:, 0:1024])
                    s_te.wait(e, ch + 1)
                    for dc in range(8):
                        issue(kT_h.ap()[dc * 128:(dc + 1) * 128, ch * 128:(ch + 1) * 128],
                              ktst[:, (ch % 2) * 1024 + dc * 128:(ch % 2) * 1024 + (dc + 1) * 128])
                else:
                    ch = i // 2
                    s_cst.wait(e, i + 1)
                    issue(vb_h.ap()[ch * 128:(ch + 1) * 128, :], kbb[:, 1024:2048])
            for sl in range(4):
                s_out[sl].wait(e, 16 * 160)
    sc_p.close()

    sc_g = contextlib.ExitStack()
    with nc.Block() as blk:
        gstg = ctx.sbuf("l1a_gstg", [128, 4 * 1024], BF16, sc_g)
        s_l = [ctx.sem(scope=sc_g) for _ in range(4)]
        s_mm = ctx.sem(scope=sc_g)
        s_me = ctx.sem(scope=sc_g)
        pg = [ctx.psum(f"l1a_pg{o}", [128, 512], F32, sc_g) for o in range(8)]

        @blk.sync
        def _(e):
            prime(e, s_l, s_mm)
            for half in range(2):
                for ch in range(64):
                    i = half * 64 + ch
                    if i >= 4:
                        s_mm.wait(e, i - 3)
                    s_l[i % 4].inc(e.dma_start(
                        gstg[:, (i % 4) * 1024:(i % 4 + 1) * 1024],
                        kb_h.ap()[ch * 128:(ch + 1) * 128, :]), 16)

        @blk.tensor
        def _(e):
            prime(e, s_l, s_mm, s_me)
            for half in range(2):
                for ch in range(64):
                    i = half * 64 + ch
                    s_l[i % 4].wait(e, 16 * (i // 4 + 1))
                    if half == 1 and ch == 0:
                        s_me.wait(e, 8)
                    for dc in range(8):
                        ins = e.matmul(
                            pg[dc][:, :],
                            gstg[:, (i % 4) * 1024 + dc * 128:(i % 4) * 1024 + (dc + 1) * 128],
                            gstg[:, (i % 4) * 1024 + half * 512:(i % 4) * 1024 + half * 512 + 512],
                            start=(ch == 0), stop=(ch == 63))
                    s_mm.inc(ins, 1)

        @blk.scalar
        def _(e):
            prime(e, s_mm, s_me)
            for half in range(2):
                s_mm.wait(e, 64 * (half + 1))
                for dc in range(8):
                    s_me.inc(e.activation(
                        mb[:, dc * 1024 + half * 512: dc * 1024 + half * 512 + 512],
                        pg[dc][:, :], AF.Copy), 1)

    sc_g.close()

    sc_b = contextlib.ExitStack()
    qT = ctx.sbuf("l1a_qT", [128, 8 * 512], BF16, sc_b)
    gT = ctx.sbuf("l1a_gT", [128, 64 * 512], BF16, sc_b)
    ktb = ctx.sbuf("l1a_ktb", [128, 2 * 8 * 512], BF16, sc_b)
    vbuf = ctx.sbuf("l1a_vbuf", [128, 4 * 1024], BF16, sc_b)
    outb = ctx.sbuf("l1a_outb", [128, 8 * 512], BF16, sc_b)

    for tb in range(8):
        bsc = contextlib.ExitStack()
        mqp = ctx.sbuf(f"l1a_mqp{tb}", [128, 2 * 512], BF16, bsc)
        sqb = ctx.sbuf(f"l1a_sqb{tb}", [128, 512], F32, bsc)
        cb = ctx.sbuf(f"l1a_cb{tb}", [128, 512], F32, bsc)
        ssb = sqb[0:1, 0:512]
        with nc.Block() as blk:
            pmq = ctx.psum(f"l1a_pmq_{tb}", [128, 512], F32, bsc)
            pmq2 = ctx.psum(f"l1a_pmq2_{tb}", [128, 512], F32, bsc)
            pss = ctx.psum(f"l1a_pss_{tb}", [1, 512], F32, bsc)
            pbc = ctx.psum(f"l1a_pbc_{tb}", [128, 512], F32, bsc)
            psc = [ctx.psum(f"l1a_psc{tb}_{o}", [128, 512], F32, bsc) for o in range(2)]
            s_kl = [ctx.sem(scope=bsc) for _ in range(2)]
            s_mq = ctx.sem(scope=bsc)
            s_mqv = ctx.sem(scope=bsc)
            s_ssm = ctx.sem(scope=bsc)
            s_cchain = ctx.sem(scope=bsc)
            s_qs = ctx.sem(scope=bsc)
            s_sc = ctx.sem(scope=bsc)
            s_ge = ctx.sem(scope=bsc)

            ssbb = mqp[0:1, 0:512]
            nev, s_qe = emit_ln_transpose(ctx, blk, f"l1a_q{tb}",
                                          list(range(4 * tb, 4 * tb + 4)),
                                          qT, 512, 0, scope=bsc)

            @blk.tensor
            def _(e):
                prime(e, s_mq, s_ssm, s_cchain, s_sc, s_kl, s_mqv, s_qs, s_ge)
                s_qe.wait(e, nev)
                for dc in range(8):
                    buf = pmq if dc % 2 == 0 else pmq2
                    if dc >= 2:
                        s_mqv.wait(e, dc - 1)
                    for dcp in range(8):
                        ins = e.matmul(
                            buf[:, :],
                            mb[:, dcp * 1024 + dc * 128: dcp * 1024 + (dc + 1) * 128],
                            qT[:, dcp * 512:(dcp + 1) * 512],
                            start=(dcp == 0), stop=(dcp == 7))
                    s_mq.inc(ins, 1)
                    if dc >= 1:
                        s_mqv.wait(e, dc)
                        s_ssm.inc(e.matmul(pss[:, :], ctx.cb["ones_col"][:, :],
                                           mqp[:, ((dc - 1) % 2) * 512:((dc - 1) % 2 + 1) * 512],
                                           start=(dc == 1), stop=False,
                                           skip_group_check=True), 1)
                s_mqv.wait(e, 8)
                s_ssm.inc(e.matmul(pss[:, :], ctx.cb["ones_col"][:, :],
                                   mqp[:, 1 * 512:2 * 512],
                                   start=False, stop=True,
                                   skip_group_check=True), 1)
                s_cchain.wait(e, 2)
                s_cchain.inc(e.matmul(pbc[:, :], ctx.cb["ones_row"][:, :],
                                      ssbb[:, :], start=True, stop=True), 1)
                s_qs.wait(e, 8)
                k = 0
                for sc in range(16):
                    s_kl[sc % 2].wait(e, 16 * 8 * (sc // 2 + 1))
                    for nt in range(4):
                        buf = psc[k % 2]
                        if k >= 2:
                            s_ge.wait(e, k - 1)
                        for dc in range(8):
                            ins = e.matmul(
                                buf[:, :],
                                ktb[:, (sc % 2) * 4096 + dc * 512 + nt * 128:
                                    (sc % 2) * 4096 + dc * 512 + (nt + 1) * 128],
                                qT[:, dc * 512:(dc + 1) * 512],
                                start=(dc == 0), stop=(dc == 7))
                        s_sc.inc(ins, 1)
                        k += 1

            @blk.vector
            def _(e):
                prime(e, s_mq, s_mqv, s_cchain, s_qs, s_ssm)
                for dc in range(8):
                    s_mq.wait(e, dc + 1)
                    if dc >= 2:
                        s_ssm.wait(e, dc - 1)
                    s_mqv.inc(e.tensor_mul(mqp[:, (dc % 2) * 512:(dc % 2 + 1) * 512],
                                           pmq[:, :] if dc % 2 == 0 else pmq2[:, :],
                                           qT[:, dc * 512:(dc + 1) * 512]), 1)
                s_ssm.wait(e, 8)
                s_cchain.inc(e.tensor_copy(ssb, pss[:, :]), 1)
                s_cchain.wait(e, 1)
                s_cchain.inc(e.tensor_copy(ssbb[:, :], ssb), 1)
                s_cchain.wait(e, 4)
                s_cchain.inc(e.reciprocal(cb[:, :], sqb[:, :]), 1)
                s_cchain.wait(e, 5)
                s_cchain.inc(e.tensor_scalar(cb[:, :], cb[:, :],
                                             90.50966799187809, None, OP.mult), 1)
                s_cchain.wait(e, 6)
                for dc in range(8):
                    s_qs.inc(e.tensor_tensor(qT[:, dc * 512:(dc + 1) * 512],
                                             qT[:, dc * 512:(dc + 1) * 512],
                                             cb[:, :], OP.mult), 1)

            @blk.scalar
            def _(e):
                prime(e, s_cchain, s_ge, s_sc)
                s_cchain.wait(e, 3)
                s_cchain.inc(e.activation(sqb[:, :], pbc[:, :], AF.Sqrt), 1)
                for k in range(64):
                    s_sc.wait(e, k + 1)
                    s_ge.inc(e.activation(gT[:, k * 512:(k + 1) * 512],
                                          psc[k % 2][:, :], AF.Gelu), 1)

            @blk.sync
            def _(e):
                prime(e, s_kl, s_sc)
                for sc in range(16):
                    if sc >= 2:
                        s_sc.wait(e, 4 * (sc - 1))
                    for dc in range(8):
                        s_kl[sc % 2].inc(e.dma_start(
                            ktb[:, (sc % 2) * 4096 + dc * 512:(sc % 2) * 4096 + (dc + 1) * 512],
                            kT_h.ap()[dc * 128:(dc + 1) * 128,
                                      sc * 512:(sc + 1) * 512]), 16)

        bsc.close()

        bsc2 = contextlib.ExitStack()
        with nc.Block() as blk:
            po = [ctx.psum(f"l1a_po{tb}_{o}", [128, 512], F32, bsc2) for o in range(8)]
            s_vl = [ctx.sem(scope=bsc2) for _ in range(2)]
            s_av = ctx.sem(scope=bsc2)
            s_oe = ctx.sem(scope=bsc2)

            @blk.sync
            def _(e):
                prime(e, s_vl, s_av)
                for grp in range(32):
                    if grp >= 2:
                        s_av.wait(e, 2 * (grp - 1))
                    for c4 in range(2):
                        ch = grp * 2 + c4
                        s_vl[grp % 2].inc(e.dma_start(
                            vbuf[:, ((grp % 2) * 2 + c4) * 1024:
                                 ((grp % 2) * 2 + c4 + 1) * 1024],
                            vb_h.ap()[ch * 128:(ch + 1) * 128, :]), 16)

            @blk.tensor
            def _(e):
                prime(e, s_vl, s_av)
                for grp in range(32):
                    s_vl[grp % 2].wait(e, 16 * 2 * (grp // 2 + 1))
                    for c4 in range(2):
                        ch = grp * 2 + c4
                        for oc in range(8):
                            ins = e.matmul(
                                po[oc][:, :],
                                vbuf[:, ((grp % 2) * 2 + c4) * 1024 + oc * 128:
                                     ((grp % 2) * 2 + c4) * 1024 + (oc + 1) * 128],
                                gT[:, ch * 512:(ch + 1) * 512],
                                start=(ch == 0), stop=(ch == 63))
                        s_av.inc(ins, 1)

            @blk.scalar
            def _(e):
                prime(e, s_oe, s_av)
                s_av.wait(e, 64)
                for oc in range(8):
                    s_oe.inc(e.activation(outb[:, oc * 512:(oc + 1) * 512],
                                          po[oc][:, :], AF.Copy), 1)
        bsc2.close()

        emit_out_ln_add(ctx, f"l1a_o{tb}", outb, 512, tb * 512)
    sc_b.close()
    sc_all.close()


def emit_l0_attn(ctx):
    nc = ctx.nc
    idn = ctx.b["idn"]
    w_in = {w: nc.dram_tensor(f"l0a_{w}", [1024, 1024], F32, kind="ExternalInput")
            for w in ("wq", "wk", "wv", "wo")}
    qt_h = nc.dram_tensor("l0a_qt", [2, 16, 65, 2048], BF16)
    rs_h = nc.dram_tensor("l0a_rs", [2, 64, 512], BF16)
    kt_h = nc.dram_tensor("l0a_kt", [2, 16, 65, 2048], BF16)
    va_h = nc.dram_tensor("l0a_va", [2, 16, 128, 1040], BF16)
    VAR_F = 64.0 / 3969.0

    for sq in range(2):
        t0 = sq * 16
        sc_seq = contextlib.ExitStack()
        avT = ctx.sbuf(f"l0a_avT{sq}", [128, 8 * 2048], BF16, sc_seq)

        sc_x = contextlib.ExitStack()
        sc_ln = contextlib.ExitStack()
        xlnT = ctx.sbuf(f"l0a_xlnT{sq}", [128, NCH * 2048], BF16, sc_x)
        with nc.Block() as blk:
            nev, s_ln = emit_ln_transpose(ctx, blk, f"l0a_ln1_{sq}",
                                          list(range(t0, t0 + 16)),
                                          xlnT, 2048, 0, scope=sc_ln, sem_scope=sc_x)
        sc_ln.close()

        sc_w = contextlib.ExitStack()
        wb = ctx.sbuf(f"l0a_wb{sq}", [128, 8 * 1024], BF16, sc_w)
        wstg = ctx.sbuf(f"l0a_wstg{sq}", [128, 2 * 1024], F32, sc_w)
        ev1 = ctx.sbuf(f"l0a_ev1_{sq}", [128, 3 * 512], BF16, sc_w)
        ev2 = ctx.sbuf(f"l0a_ev2_{sq}", [128, 3 * 512], BF16, sc_w)
        sqv = ctx.sbuf(f"l0a_sqv{sq}", [128, 3 * 512], BF16, sc_w)
        vrow = ctx.sbuf(f"l0a_vrow{sq}", [2, 3 * 512], BF16, sc_w)

        for wi, wname in enumerate(("wq", "wk")):
            dst_h = qt_h if wname == "wq" else kt_h
            wsc = contextlib.ExitStack()
            with nc.Block() as blk:
                pj = [ctx.psum(f"l0a_pj{sq}{wi}{j}", [128, 512], F32, wsc) for j in range(3)]
                pc = [ctx.psum(f"l0a_pc{sq}{wi}{j}", [128, 512], F32, wsc) for j in range(3)]
                pv = ctx.psum(f"l0a_pv{sq}{wi}", [2, 512], F32, wsc)
                s_wl = [ctx.sem(scope=wsc) for _ in range(2)]
                s_wc = ctx.sem(scope=wsc)
                s_pj = ctx.sem(scope=wsc)
                s_e1 = ctx.sem(scope=wsc)
                s_cm = ctx.sem(scope=wsc)
                s_e2 = ctx.sem(scope=wsc)
                s_sq = ctx.sem(scope=wsc)
                s_iv = ctx.sem(scope=wsc)
                s_vr = ctx.sem(scope=wsc)
                s_od = [ctx.sem(scope=wsc, kls="gdma") for _ in range(2)]

                @blk.sync
                def _(e):
                    prime(e, s_wl, s_wc)
                    for ec in range(8):
                        if ec >= 2:
                            s_wc.wait(e, ec - 1)
                        s_wl[ec % 2].inc(e.dma_start(
                            wstg[:, (ec % 2) * 1024:(ec % 2 + 1) * 1024],
                            w_in[wname].ap()[ec * 128:(ec + 1) * 128, :]), 16)

                @blk.vector
                def _(e):
                    prime(e, s_wl, s_wc)
                    for ec in range(8):
                        s_wl[ec % 2].wait(e, 16 * (ec // 2 + 1))
                        s_wc.inc(e.tensor_copy(wb[:, ec * 1024:(ec + 1) * 1024],
                                               wstg[:, (ec % 2) * 1024:(ec % 2 + 1) * 1024]), 1)

                @blk.tensor
                def _(e):
                    prime(e, s_pj, s_cm, s_iv, s_wc, s_e1, s_e2, s_sq, s_vr, s_ln)
                    s_wc.wait(e, 8)
                    s_ln.wait(e, nev)
                    k = 0
                    for dc in range(8):
                        for tc in range(4):
                            if k >= 3:
                                s_e1.wait(e, k - 2)
                            for ec in range(8):
                                ins = e.matmul(
                                    pj[k % 3][:, :],
                                    wb[:, ec * 1024 + dc * 128: ec * 1024 + (dc + 1) * 128],
                                    xlnT[:, ec * 2048 + tc * 512: ec * 2048 + (tc + 1) * 512],
                                    start=(ec == 0), stop=(ec == 7))
                            s_pj.inc(ins, 1)
                            s_e1.wait(e, k + 1)
                            if k >= 3:
                                s_e2.wait(e, k - 2)
                            s_cm.inc(e.matmul(pc[k % 3][:, :], ctx.cb["cblk"][:, :],
                                              ev1[:, (k % 3) * 512:(k % 3 + 1) * 512],
                                              start=True, stop=True), 1)
                            s_sq.wait(e, k + 1)
                            if k >= 1:
                                s_vr.wait(e, k)
                            s_iv.inc(e.matmul(pv[:, :], ctx.cb["ind2"][:, :],
                                              sqv[:, (k % 3) * 512:(k % 3 + 1) * 512],
                                              start=True, stop=True), 1)
                            k += 1

                @blk.scalar
                def _(e):
                    prime(e, s_pj, s_e1, s_cm, s_e2, s_sq, s_iv, s_vr, *s_od)
                    for k in range(32):
                        s_pj.wait(e, k + 1)
                        if k >= 3:
                            s_cm.wait(e, k - 2)
                            s_sq.wait(e, k - 2)
                        if k >= 3:
                            s_od[(k - 3) % 2].wait(e, 16 * 4 * ((k - 3) // 2 + 1))
                        s_e1.inc(e.activation(ev1[:, (k % 3) * 512:(k % 3 + 1) * 512],
                                              pj[k % 3][:, :], AF.Copy), 1)
                        s_cm.wait(e, k + 1)
                        s_e2.inc(e.activation(ev2[:, (k % 3) * 512:(k % 3 + 1) * 512],
                                              pc[k % 3][:, :], AF.Copy), 1)
                        s_e2.wait(e, k + 1)
                        s_sq.inc(e.activation(sqv[:, (k % 3) * 512:(k % 3 + 1) * 512],
                                              ev2[:, (k % 3) * 512:(k % 3 + 1) * 512],
                                              AF.Square), 1)
                        s_iv.wait(e, k + 1)
                        s_vr.inc(e.activation(vrow[:, (k % 3) * 512:(k % 3 + 1) * 512],
                                              pv[:, :], AF.Identity,
                                              scale=(VAR_F if wname == "wq" else 1.0)), 1)

                @blk.gpsimd
                def _(e):
                    prime(e, s_od, s_e2, s_vr)
                    for k in range(32):
                        dc, tc = divmod(k, 4)
                        if k >= 2:
                            s_od[k % 2].wait(e, 16 * 4 * (k // 2))
                        s_e2.wait(e, k + 1)
                        for hh in range(2):
                            s_od[k % 2].inc(e.dma_start(
                                dst_h.ap()[sq, 2 * dc + hh, 0:64, tc * 512:(tc + 1) * 512],
                                ev2[(hh * 64):(hh * 64 + 64),
                                    (k % 3) * 512:(k % 3 + 1) * 512]), 16)
                        s_vr.wait(e, k + 1)
                        for hh in range(2):
                            s_od[k % 2].inc(e.dma_start(
                                dst_h.ap()[sq, 2 * dc + hh, 64:65, tc * 512:(tc + 1) * 512],
                                vrow[hh:hh + 1, (k % 3) * 512:(k % 3 + 1) * 512]), 16)
                    for sl in range(2):
                        s_od[sl].wait(e, 16 * 4 * 16)
            wsc.close()

        wsc = contextlib.ExitStack()
        with nc.Block() as blk:
            vstg = ctx.sbuf(f"l0a_vstg{sq}", [128, 1040], BF16, wsc)
            pj = [ctx.psum(f"l0a_pvj{sq}{j}", [128, 512], F32, wsc) for j in range(2)]
            s_wl = [ctx.sem(scope=wsc) for _ in range(2)]
            s_wc = ctx.sem(scope=wsc)
            s_pj = ctx.sem(scope=wsc)
            s_ev = ctx.sem(scope=wsc)
            s_od = ctx.sem(scope=wsc, kls="gdma")

            @blk.sync
            def _(e):
                prime(e, s_wl, s_wc)
                for ec in range(8):
                    if ec >= 2:
                        s_wc.wait(e, ec - 1)
                    s_wl[ec % 2].inc(e.dma_start(
                        wstg[:, (ec % 2) * 1024:(ec % 2 + 1) * 1024],
                        w_in["wv"].ap()[ec * 128:(ec + 1) * 128, :]), 16)

            @blk.vector
            def _(e):
                prime(e, s_wl, s_wc, s_ev, s_pj, s_od)
                for ec in range(8):
                    s_wl[ec % 2].wait(e, 16 * (ec // 2 + 1))
                    s_wc.inc(e.tensor_copy(wb[:, ec * 1024:(ec + 1) * 1024],
                                           wstg[:, (ec % 2) * 1024:(ec % 2 + 1) * 1024]), 1)
                s_ev.inc(e.memset(bass.AP(vstg, 64, [[1040, 128], [65, 16], [1, 1]]), 1.0), 1)
                for k in range(32):
                    nt, hf = divmod(k, 2)
                    s_pj.wait(e, k + 1)
                    if nt >= 1:
                        s_od.wait(e, 16 * nt)
                    s_ev.inc(e.tensor_copy(
                        bass.AP(vstg, hf * 8 * 65, [[1040, 128], [65, 8], [1, 64]]),
                        pj[k % 2][:, :]), 1)

            @blk.tensor
            def _(e):
                prime(e, s_pj, s_wc, s_ev, s_ln)
                s_wc.wait(e, 8)
                s_ln.wait(e, nev)
                k = 0
                for nt in range(16):
                    for hf in range(2):
                        if k >= 2:
                            s_ev.wait(e, k)
                        for ec in range(8):
                            ins = e.matmul(
                                pj[k % 2][:, :],
                                xlnT[:, ec * 2048 + nt * 128: ec * 2048 + (nt + 1) * 128],
                                wb[:, ec * 1024 + hf * 512: ec * 1024 + (hf + 1) * 512],
                                start=(ec == 0), stop=(ec == 7))
                        s_pj.inc(ins, 1)
                        k += 1

            @blk.gpsimd
            def _(e):
                prime(e, s_od, s_ev)
                for nt in range(16):
                    s_ev.wait(e, 2 * nt + 3)
                    s_od.inc(e.dma_start(va_h.ap()[sq, nt, :, :], vstg[:, :]), 16)
                s_od.wait(e, 16 * 16)
        wsc.close()
        sc_w.close()
        sc_x.close()

        sc_a = contextlib.ExitStack()
        rstage = ctx.sbuf(f"l0a_rstage{sq}", [1, 2 * 512], BF16, sc_a)
        qkt = ctx.sbuf(f"l0a_qkt{sq}", [65, 4 * 2048], BF16, sc_a)
        vab = ctx.sbuf(f"l0a_vab{sq}", [128, 2 * 1040], BF16, sc_a)
        ptb = ctx.sbuf(f"l0a_ptb{sq}", [128, 16 * 512], BF16, sc_a)
        with nc.Block() as blk:
            psc = [ctx.psum(f"l0a_psc{sq}{j}", [128, 1024], F32, sc_a) for j in range(2)]
            pav = ctx.psum(f"l0a_pav{sq}", [65, 512], F32, sc_a)
            s_hl = [ctx.sem(scope=sc_a) for _ in range(2)]
            s_sc = ctx.sem(scope=sc_a)
            s_ex = ctx.sem(scope=sc_a)
            s_av = ctx.sem(scope=sc_a)
            s_ae = ctx.sem(scope=sc_a)
            s_rst = ctx.sem(scope=sc_a)
            s_rod = [ctx.sem(scope=sc_a, kls="gdma") for _ in range(2)]

            @blk.sync
            def _(e):
                prime(e, s_hl, s_av)
                for h in range(16):
                    if h >= 2:
                        s_av.wait(e, 4 * (h - 1))
                    s_hl[h % 2].inc(e.dma_start(
                        qkt[:, (h % 2) * 2048:(h % 2 + 1) * 2048],
                        qt_h.ap()[sq, h, :, :]), 16)
                    s_hl[h % 2].inc(e.dma_start(
                        qkt[:, (2 + h % 2) * 2048:(3 + h % 2) * 2048],
                        kt_h.ap()[sq, h, :, :]), 16)
                    s_hl[h % 2].inc(e.dma_start(
                        vab[:, (h % 2) * 1040:(h % 2 + 1) * 1040],
                        bass.AP(va_h, (sq * 16 * 128 * 1040) + h * 65,
                                [[1040, 128], [128 * 1040, 16], [1, 65]])), 16)

            @blk.tensor
            def _(e):
                prime(e, s_sc, s_av, s_hl, s_ex, s_ae)
                for h in range(16):
                    s_hl[h % 2].wait(e, 48 * (h // 2 + 1))
                    for tc in range(4):
                        k = h * 4 + tc
                        for ng in range(8):
                            kk = k * 8 + ng
                            if kk >= 2:
                                s_ex.wait(e, kk - 1)
                            for j in range(2):
                                nt = 2 * ng + j
                                ins = e.matmul(
                                    psc[kk % 2][:, j * 512:(j + 1) * 512],
                                    qkt[:, (2 + h % 2) * 2048 + nt * 128:
                                        (2 + h % 2) * 2048 + (nt + 1) * 128],
                                    qkt[:, (h % 2) * 2048 + tc * 512:
                                        (h % 2) * 2048 + (tc + 1) * 512],
                                    start=True, stop=True)
                            s_sc.inc(ins, 1)
                        s_ae.wait(e, k)
                        for nt in range(16):
                            s_ex.wait(e, k * 8 + nt // 2 + 1)
                            ins = e.matmul(
                                pav[:, :],
                                vab[:, (h % 2) * 1040 + nt * 65:
                                    (h % 2) * 1040 + (nt + 1) * 65],
                                ptb[:, nt * 512:(nt + 1) * 512],
                                start=(nt == 0), stop=(nt == 15))
                        s_av.inc(ins, 1)

            @blk.scalar
            def _(e):
                prime(e, s_ex, s_sc)
                for kk in range(16 * 4 * 8):
                    s_sc.wait(e, kk + 1)
                    ng = kk % 8
                    s_ex.inc(e.activation(ptb[:, ng * 1024:(ng + 1) * 1024],
                                          psc[kk % 2][:, :], AF.Exp), 1)

            @blk.vector
            def _(e):
                prime(e, s_av, s_ae, s_rst, s_rod)
                for h in range(16):
                    for tc in range(4):
                        k = h * 4 + tc
                        s_av.wait(e, k + 1)
                        if k >= 2:
                            s_rod[k % 2].wait(e, 16 * (k // 2))
                        s_rst.inc(e.tensor_copy(
                            rstage[0:1, (k % 2) * 512:(k % 2 + 1) * 512],
                            pav[64:65, :]), 1)
                        s_ae.inc(e.tensor_copy(
                            avT[(h % 2) * 64:(h % 2) * 64 + 64,
                                (h // 2) * 2048 + tc * 512:(h // 2) * 2048 + (tc + 1) * 512],
                            pav[0:64, :]), 1)

            @blk.gpsimd
            def _(e):
                prime(e, s_rod, s_rst)
                for k in range(64):
                    s_rst.wait(e, k + 1)
                    s_rod[k % 2].inc(e.dma_start(
                        rs_h.ap()[sq, k:k + 1, :],
                        rstage[0:1, (k % 2) * 512:(k % 2 + 1) * 512]), 16)
                for sl in range(2):
                    s_rod[sl].wait(e, 16 * 32)
        sc_a.close()

        sc_p4 = contextlib.ExitStack()
        wob = ctx.sbuf(f"l0a_wob{sq}", [128, 8 * 1024], BF16, sc_p4)
        wstg4 = ctx.sbuf(f"l0a_wstg4{sq}", [128, 2 * 1024], F32, sc_p4)
        recb = ctx.sbuf(f"l0a_recb{sq}", [64, 512], BF16, sc_p4)
        recf = ctx.sbuf(f"l0a_recf{sq}", [64, 512], F32, sc_p4)
        rows = ctx.sbuf(f"l0a_rows{sq}", [64, 512], BF16, sc_p4)
        selb = ctx.sbuf(f"l0a_selb{sq}", [64, 32 * 128], BF16, sc_p4)
        outb = ctx.sbuf(f"l0a_outb{sq}", [128, 8 * 512], BF16, sc_p4)
        s_oev = ctx.sem(scope=sc_seq)
        sc_nrm = contextlib.ExitStack()
        with nc.Block() as blk:
            pbc = [ctx.psum(f"l0a_pbc{sq}{j}", [128, 512], F32, sc_nrm) for j in range(2)]
            s_wl = [ctx.sem(scope=sc_p4) for _ in range(2)]
            s_wc = ctx.sem(scope=sc_p4)
            s_rc = ctx.sem(scope=sc_p4)
            s_bc = ctx.sem(scope=sc_p4)
            s_nm = ctx.sem(scope=sc_p4)

            s_rl = ctx.sem()

            @blk.sync
            def _(e):
                prime(e, s_wl, s_wc, s_rl)
                s_rl.inc(e.dma_start(rows[:, :], rs_h.ap()[sq]), 16)
                s_rl.wait(e, 16)
                s_rl.inc(e.dma_start(selb[:, :], ctx.cdram["seln"].ap()), 16)
                for ec in range(8):
                    if ec >= 2:
                        s_wc.wait(e, ec - 1)
                    s_wl[ec % 2].inc(e.dma_start(
                        wstg4[:, (ec % 2) * 1024:(ec % 2 + 1) * 1024],
                        w_in["wo"].ap()[ec * 128:(ec + 1) * 128, :]), 16)

            @blk.vector
            def _(e):
                prime(e, s_wl, s_wc, s_rc, s_bc, s_nm)
                for ec in range(8):
                    s_wl[ec % 2].wait(e, 16 * (ec // 2 + 1))
                    s_wc.inc(e.tensor_copy(wob[:, ec * 1024:(ec + 1) * 1024],
                                           wstg4[:, (ec % 2) * 1024:(ec % 2 + 1) * 1024]), 1)
                s_rl.wait(e, 16)
                s_rc.inc(e.reciprocal(recf[:, :], rows[:, :]), 1)
                s_rc.wait(e, 1)
                s_rc.inc(e.tensor_copy(recb[:, :], recf[:, :]), 1)
                for j in range(32):
                    p, tc = divmod(j, 4)
                    s_bc.wait(e, j + 1)
                    s_nm.inc(e.tensor_tensor(
                        avT[:, p * 2048 + tc * 512: p * 2048 + (tc + 1) * 512],
                        avT[:, p * 2048 + tc * 512: p * 2048 + (tc + 1) * 512],
                        pbc[j % 2][:, :], OP.mult), 1)

            @blk.tensor
            def _(e):
                prime(e, s_rc, s_bc, s_nm)
                s_rc.wait(e, 2)
                s_rl.wait(e, 32)
                for j in range(32):
                    if j >= 2:
                        s_nm.wait(e, j - 1)
                    s_bc.inc(e.matmul(pbc[j % 2][:, :],
                                      selb[:, j * 128:(j + 1) * 128],
                                      recb[:, :],
                                      start=True, stop=True), 1)
        sc_nrm.close()
        for tc in range(4):
            wsc2 = contextlib.ExitStack()
            with nc.Block() as blk:
                pw = [ctx.psum(f"l0a_pw{sq}{tc}{j}", [128, 512], F32, wsc2) for j in range(8)]
                s_wm = ctx.sem(scope=wsc2)

                @blk.tensor
                def _(e):
                    prime(e, s_wm)
                    for eci in range(8):
                        for hdc in range(8):
                            ins = e.matmul(
                                pw[eci][:, :],
                                wob[:, hdc * 1024 + eci * 128: hdc * 1024 + (eci + 1) * 128],
                                avT[:, hdc * 2048 + tc * 512: hdc * 2048 + (tc + 1) * 512],
                                start=(hdc == 0), stop=(hdc == 7))
                        s_wm.inc(ins, 1)

                @blk.scalar
                def _(e):
                    prime(e, s_oev, s_wm)
                    for eci in range(8):
                        s_wm.wait(e, eci + 1)
                        s_oev.inc(e.activation(outb[:, eci * 512:(eci + 1) * 512],
                                               pw[eci][:, :], AF.Copy), 1)
            wsc2.close()
            emit_out_ln_add(ctx, f"l0a_o{sq}_{tc}", outb, 512, sq * 2048 + tc * 512)
        sc_p4.close()
        sc_seq.close()


def emit_l0_ffn(ctx):
    nc = ctx.nc
    idn = ctx.b["idn"]
    rw_in = nc.dram_tensor("l0f_rw", [1024, 8], F32, kind="ExternalInput")
    wqkv_in = [nc.dram_tensor("l0f_mqkv", [1024, 1536], F32, kind="ExternalInput"),
               nc.dram_tensor("l0f_dqkv", [1024, 1536], F32, kind="ExternalInput")]
    wo_in = [nc.dram_tensor("l0f_mwo", [512, 1024], F32, kind="ExternalInput"),
             nc.dram_tensor("l0f_dwo", [512, 1024], F32, kind="ExternalInput")]
    qt_h = nc.dram_tensor("l0f_qt", [2, 2, 8, 64, 2048], BF16)
    kt_h = nc.dram_tensor("l0f_kt", [2, 2, 8, 64, 2048], BF16)
    va_h = nc.dram_tensor("l0f_va", [2, 2, 16, 128, 520], BF16)
    rs_h = nc.dram_tensor("l0f_rs", [2, 64, 512], BF16)
    gm_h = nc.dram_tensor("l0f_gm", [2, 2, 8, 2048], F32)

    for sq in range(2):
        t0 = sq * 16
        sc_seq = contextlib.ExitStack()
        avT = ctx.sbuf(f"l0f_avT{sq}", [128, 8 * 2048], BF16, sc_seq)
        cmt = ctx.sbuf(f"l0f_cmt{sq}", [128, 128], F32, sc_seq)

        sc_x = contextlib.ExitStack()
        sc_ln = contextlib.ExitStack()
        xlnT = ctx.sbuf(f"l0f_xlnT{sq}", [128, NCH * 2048], BF16, sc_x)
        with nc.Block() as blk:
            nev, s_ln = emit_ln_transpose(ctx, blk, f"l0f_ln1_{sq}",
                                          list(range(t0, t0 + 16)),
                                          xlnT, 2048, 0, scope=sc_ln, sem_scope=sc_x)
        sc_ln.close()

        sc_r = contextlib.ExitStack()
        with nc.Block() as blk:
            rsc = ctx.sbuf(f"l0f_rsc{sq}", [8, 2048], F32, sc_r)
            rwf = ctx.sbuf(f"l0f_rwf{sq}", [128, 64], F32, sc_r)
            rwb = ctx.sbuf(f"l0f_rwb{sq}", [128, 64], BF16, sc_r)
            bis = ctx.sbuf(f"l0f_bis{sq}", [8, 8], F32, sc_r)
            cmp = ctx.sbuf(f"l0f_cmp{sq}", [8, 2048], BF16, sc_r)
            sg = ctx.sbuf(f"l0f_sg{sq}", [8, 2048], F32, sc_r)
            msk = ctx.sbuf(f"l0f_msk{sq}", [8, 2048], F32, sc_r)
            gmb = ctx.sbuf(f"l0f_gmbf{sq}", [8, 2048], F32, sc_r)
            mskb = ctx.sbuf(f"l0f_mskb{sq}", [8, 2048], BF16, sc_r)
            prc = [ctx.psum(f"l0f_prc{sq}{j}", [8, 512], F32, sc_r) for j in range(2)]
            ptc = ctx.psum(f"l0f_ptc{sq}", [128, 128], BF16, sc_r)
            s_rl = ctx.sem()
            s_cst = ctx.sem(scope=sc_r)
            s_mm = ctx.sem(scope=sc_r)
            s_ev = ctx.sem(scope=sc_r)
            vf = ctx.sem(scope=sc_r)
            s_sg = ctx.sem(scope=sc_r)
            s_tm = ctx.sem(scope=sc_r)
            s_cme = ctx.sem(scope=sc_r)
            s_go = ctx.sem(scope=sc_r, kls="gdma")

            @blk.sync
            def _(e):
                prime(e, s_rl)
                for half in range(2):
                    s_rl.inc(e.dma_start(
                        rwf[:, half * 32:(half + 1) * 32],
                        bass.AP(rw_in, half * 4 * 128 * 8,
                                [[8, 128], [1024, 4], [1, 8]])), 16)

            @blk.vector
            def _(e):
                prime(e, vf, s_cst, s_mm, s_ev, s_sg, s_tm, s_go, s_cme)
                s_rl.wait(e, 32)
                s_cst.inc(e.tensor_copy(rwb[:, :], rwf[:, :]), 1)
                for tc in range(4):
                    s_mm.wait(e, tc + 1)
                    vf.inc(e.tensor_copy(rsc[:, tc * 512:(tc + 1) * 512],
                                         prc[tc % 2][:, :]), 1)
                v = 4
                vf.wait(e, v)
                vf.inc(e.memset(bis[:, 0:1], -16.0), 1); v += 1
                vf.wait(e, v)
                vf.inc(e.memset(bis[:, 1:2], 16.0), 1); v += 1
                for it in range(24):
                    vf.wait(e, v)
                    vf.inc(e.scalar_tensor_tensor(bis[:, 2:3], bis[:, 0:1], 1.0,
                                                  bis[:, 1:2], OP.bypass, OP.add), 1); v += 1
                    vf.wait(e, v)
                    vf.inc(e.tensor_scalar(bis[:, 2:3], bis[:, 2:3], 0.5, None,
                                           OP.mult), 1); v += 1
                    vf.wait(e, v)
                    vf.inc(e.tensor_scalar(cmp[:, :], rsc[:, :], bis[:, 2:3], None,
                                           OP.is_gt, OP.add, accum_out=bis[:, 3:4]), 1); v += 1
                    vf.wait(e, v)
                    vf.inc(e.tensor_scalar(bis[:, 4:5], bis[:, 3:4], 255.5, None,
                                           OP.is_gt), 1); v += 1
                    vf.wait(e, v)
                    vf.inc(e.tensor_sub(bis[:, 5:6], bis[:, 2:3], bis[:, 0:1]), 1); v += 1
                    vf.wait(e, v)
                    vf.inc(e.tensor_mul(bis[:, 6:7], bis[:, 5:6], bis[:, 4:5]), 1); v += 1
                    vf.wait(e, v)
                    vf.inc(e.tensor_add(bis[:, 0:1], bis[:, 0:1], bis[:, 6:7]), 1); v += 1
                    vf.wait(e, v)
                    vf.inc(e.tensor_sub(bis[:, 5:6], bis[:, 2:3], bis[:, 1:2]), 1); v += 1
                    vf.wait(e, v)
                    vf.inc(e.tensor_scalar(bis[:, 7:8], bis[:, 4:5], -1.0, 1.0,
                                           OP.mult, OP.add), 1); v += 1
                    vf.wait(e, v)
                    vf.inc(e.tensor_mul(bis[:, 6:7], bis[:, 5:6], bis[:, 7:8]), 1); v += 1
                    vf.wait(e, v)
                    vf.inc(e.tensor_add(bis[:, 1:2], bis[:, 1:2], bis[:, 6:7]), 1); v += 1
                vf.wait(e, v)
                vf.inc(e.tensor_scalar(msk[:, :], rsc[:, :], bis[:, 0:1], None,
                                       OP.is_gt), 1); v += 1
                vf.wait(e, v)
                vf.inc(e.tensor_copy(mskb[:, :], msk[:, :]), 1); v += 1
                s_sg.wait(e, 1)
                vf.wait(e, v)
                vf.inc(e.tensor_mul(gmb[:, :], sg[:, :], msk[:, :]), 1); v += 1
                s_tm.wait(e, 16)
                s_cme.inc(e.tensor_copy(cmt[:, :], ptc[:, :]), 1)

            @blk.scalar
            def _(e):
                prime(e, s_sg, vf)
                vf.wait(e, 4)
                s_sg.inc(e.activation(sg[:, :], rsc[:, :], AF.Sigmoid), 1)

            @blk.tensor
            def _(e):
                prime(e, s_mm, s_tm, s_cst, vf, s_ln, s_ev)
                s_ln.wait(e, nev)
                s_cst.wait(e, 1)
                for tc in range(4):
                    if tc >= 2:
                        vf.wait(e, tc - 1)
                    for ec in range(8):
                        ins = e.matmul(prc[tc % 2][:, :],
                                       rwb[:, ec * 8:(ec + 1) * 8],
                                       xlnT[:, ec * 2048 + tc * 512: ec * 2048 + (tc + 1) * 512],
                                       start=(ec == 0), stop=(ec == 7))
                    s_mm.inc(ins, 1)
                vf.wait(e, 6 + 24 * 11 + 2)
                for i in range(16):
                    ins = e.transpose(ptc[:, i * 8:(i + 1) * 8],
                                      mskb[0:8, i * 128:(i + 1) * 128], idn[0:8, 0:8])
                    s_tm.inc(ins, 1)

            @blk.gpsimd
            def _(e):
                prime(e, s_go, vf)
                vf.wait(e, 6 + 24 * 11 + 3)
                s_go.inc(e.dma_start(gm_h.ap()[sq, 0], gmb[:, :]), 16)
                s_go.wait(e, 16)
                s_go.inc(e.dma_start(gm_h.ap()[sq, 1], msk[:, :]), 16)
                s_go.wait(e, 32)
        sc_r.close()

        for br in range(2):
            fsc = contextlib.ExitStack()
            with nc.Block() as blk:
                wqb = ctx.sbuf(f"l0f_wqb{sq}{br}", [128, 8 * 1536], BF16, fsc)
                wstg = ctx.sbuf(f"l0f_wstg{sq}{br}", [128, 2 * 1536], F32, fsc)
                qkv = ctx.sbuf(f"l0f_qkv{sq}{br}", [128, 2 * 1536], BF16, fsc)
                rtmp = ctx.sbuf(f"l0f_rtmp{sq}{br}", [128, 4 * 128], BF16, fsc)
                cbuf = ctx.sbuf(f"l0f_cbuf{sq}{br}", [128, 32], F32, fsc)
                tstg = ctx.sbuf(f"l0f_tstg{sq}{br}", [128, 2 * 2 * 512], BF16, fsc)
                vstg = ctx.sbuf(f"l0f_vstg{sq}{br}", [128, 2 * 520], BF16, fsc)
                pj = [ctx.psum(f"l0f_pj{sq}{br}{j}", [128, 512], F32, fsc) for j in range(2)]
                ptr = [ctx.psum(f"l0f_ptr{sq}{br}{j}", [128, 512], BF16, fsc) for j in range(2)]
                s_wl = [ctx.sem(scope=fsc) for _ in range(2)]
                s_wc = ctx.sem(scope=fsc)
                s_pj = ctx.sem(scope=fsc)
                s_qe = ctx.sem(scope=fsc)
                s_rp = ctx.sem(scope=fsc)
                s_tr = ctx.sem(scope=fsc)
                s_te = ctx.sem(scope=fsc)
                s_va = ctx.sem(scope=fsc)
                s_od = [ctx.sem(scope=fsc, kls="gdma") for _ in range(2)]
                s_vo = [ctx.sem(scope=fsc, kls="gdma") for _ in range(2)]

                @blk.sync
                def _(e):
                    prime(e, s_wl, s_wc)
                    for ec in range(8):
                        if ec >= 2:
                            s_wc.wait(e, ec - 1)
                        s_wl[ec % 2].inc(e.dma_start(
                            wstg[:, (ec % 2) * 1536:(ec % 2 + 1) * 1536],
                            wqkv_in[br].ap()[ec * 128:(ec + 1) * 128, :]), 16)

                @blk.vector
                def _(e):
                    prime(e, s_wl, s_wc, s_pj, s_qe, s_rp, s_od, s_va, *s_vo)
                    for ec in range(8):
                        s_wl[ec % 2].wait(e, 16 * (ec // 2 + 1))
                        s_wc.inc(e.tensor_copy(wqb[:, ec * 1536:(ec + 1) * 1536],
                                               wstg[:, (ec % 2) * 1536:(ec % 2 + 1) * 1536]), 1)
                    for i in range(16):
                        sl = i % 2
                        base = sl * 1536
                        s_qe.wait(e, 3 * (i + 1))
                        s_rp.wait(e, 14 * i)
                        s_rp.inc(e.tensor_add(cbuf[:, 0:16],
                                              ctx.cb["cos_t"][:, i * 16:(i + 1) * 16],
                                              ctx.cb["cos_lo"][:, i * 16:(i + 1) * 16]), 1)
                        s_rp.wait(e, 14 * i + 1)
                        s_rp.inc(e.tensor_add(cbuf[:, 16:32],
                                              ctx.cb["sin_t"][:, i * 16:(i + 1) * 16],
                                              ctx.cb["sin_lo"][:, i * 16:(i + 1) * 16]), 1)
                        for qk in range(2):
                            off = base + qk * 64
                            cs = bass.AP(cbuf, 0, [[32, 128], [0, 8], [1, 16]])
                            sn = bass.AP(cbuf, 16, [[32, 128], [0, 8], [1, 16]])
                            x1 = bass.AP(qkv, off, [[3072, 128], [192, 8], [1, 16]])
                            x2 = bass.AP(qkv, off + 16, [[3072, 128], [192, 8], [1, 16]])
                            t1 = bass.AP(rtmp, 0, [[512, 128], [16, 8], [1, 16]])
                            t2 = bass.AP(rtmp, 128, [[512, 128], [16, 8], [1, 16]])
                            t3 = bass.AP(rtmp, 256, [[512, 128], [16, 8], [1, 16]])
                            t4 = bass.AP(rtmp, 384, [[512, 128], [16, 8], [1, 16]])
                            rb = 14 * i + 2 + 6 * qk
                            s_rp.wait(e, rb)
                            s_rp.inc(e.tensor_tensor(t1, x1, cs, OP.mult), 1)
                            s_rp.wait(e, rb + 1)
                            s_rp.inc(e.tensor_tensor(t2, x1, sn, OP.mult), 1)
                            s_rp.wait(e, rb + 2)
                            s_rp.inc(e.tensor_tensor(t3, x2, sn, OP.mult), 1)
                            s_rp.wait(e, rb + 3)
                            s_rp.inc(e.tensor_tensor(t4, x2, cs, OP.mult), 1)
                            s_rp.wait(e, rb + 4)
                            s_rp.inc(e.tensor_tensor(x1, t1, t3, OP.subtract), 1)
                            s_rp.wait(e, rb + 5)
                            s_rp.inc(e.tensor_tensor(x2, t2, t4, OP.add), 1)
                        if i >= 2:
                            s_vo[i % 2].wait(e, 16 * (i // 2))
                        for h in range(8):
                            vsl = bass.AP(vstg, (i % 2) * 520 + h * 65, [[1040, 128], [1, 64]])
                            vsrc = qkv[:, base + h * 192 + 128: base + h * 192 + 192]
                            if br == 0:
                                s_va.inc(e.tensor_scalar(vsl, vsrc,
                                                         cmt[:, i * 8 + h:i * 8 + h + 1],
                                                         None, OP.mult), 1)
                                s_va.inc(e.tensor_copy(
                                    bass.AP(vstg, (i % 2) * 520 + h * 65 + 64,
                                            [[1040, 128], [1, 1]]),
                                    cmt[:, i * 8 + h:i * 8 + h + 1]), 1)
                            else:
                                s_va.inc(e.tensor_copy(vsl, vsrc), 1)
                                s_va.inc(e.memset(
                                    bass.AP(vstg, (i % 2) * 520 + h * 65 + 64,
                                            [[1040, 128], [1, 1]]), 1.0), 1)

                @blk.tensor
                def _(e):
                    prime(e, s_pj, s_tr, s_wc, s_qe, s_rp, s_te, s_ln)
                    s_wc.wait(e, 8)
                    s_ln.wait(e, nev)

                    def transposes(i):
                        for qk in range(2):
                            kt = 2 * i + qk
                            buf = ptr[kt % 2]
                            if kt >= 2:
                                s_te.wait(e, kt - 1)
                            for hp in range(4):
                                for par in range(2):
                                    h = 2 * hp + par
                                    ins = e.transpose(
                                        buf[par * 64:par * 64 + 64, hp * 128:(hp + 1) * 128],
                                        qkv[:, (i % 2) * 1536 + h * 192 + qk * 64:
                                            (i % 2) * 1536 + h * 192 + qk * 64 + 64],
                                        idn[:, :])
                            s_tr.inc(ins, 1)

                    k = 0
                    for i in range(16):
                        for cc in range(3):
                            if k >= 2:
                                s_qe.wait(e, k - 1)
                            for ec in range(8):
                                ins = e.matmul(
                                    pj[k % 2][:, :],
                                    xlnT[:, ec * 2048 + i * 128: ec * 2048 + (i + 1) * 128],
                                    wqb[:, ec * 1536 + cc * 512: ec * 1536 + (cc + 1) * 512],
                                    start=(ec == 0), stop=(ec == 7))
                            s_pj.inc(ins, 1)
                            k += 1
                        if i >= 1:
                            s_rp.wait(e, 14 * i)
                            transposes(i - 1)
                    s_rp.wait(e, 14 * 16)
                    transposes(15)

                @blk.scalar
                def _(e):
                    prime(e, s_pj, s_qe, s_tr, s_te, s_od, s_rp, s_va)

                    def tstg_evac(kt):
                        s_tr.wait(e, kt + 1)
                        if kt >= 2:
                            s_od[kt % 2].wait(e, 16 * 4 * (kt // 2))
                        s_te.inc(e.activation(tstg[:, (kt % 2) * 512:(kt % 2 + 1) * 512],
                                              ptr[kt % 2][:, :], AF.Copy), 1)

                    for i in range(16):
                        for cc in range(3):
                            k = 3 * i + cc
                            s_pj.wait(e, k + 1)
                            if i >= 2 and cc == 0:
                                s_rp.wait(e, 14 * (i - 1))
                                s_va.wait(e, 16 * (i - 1))
                                s_tr.wait(e, 2 * (i - 1))
                            s_qe.inc(e.activation(
                                qkv[:, (i % 2) * 1536 + cc * 512:
                                    (i % 2) * 1536 + (cc + 1) * 512],
                                pj[k % 2][:, :], AF.Copy), 1)
                        if i >= 1:
                            tstg_evac(2 * (i - 1))
                            tstg_evac(2 * (i - 1) + 1)
                    tstg_evac(30)
                    tstg_evac(31)

                @blk.gpsimd
                def _(e):
                    prime(e, s_od, s_te, s_va, *s_vo)
                    for i in range(16):
                        for qk in range(2):
                            kt = 2 * i + qk
                            dst = qt_h if qk == 0 else kt_h
                            s_te.wait(e, kt + 1)
                            if kt >= 2:
                                s_od[kt % 2].wait(e, 16 * 4 * (kt // 2))
                            for hp in range(4):
                                s_od[kt % 2].inc(e.dma_start(
                                    bass.AP(dst, ((sq * 2 + br) * 8 + 2 * hp) * 64 * 2048 + i * 128,
                                            [[2048, 128], [1, 128]]),
                                    tstg[:, (kt % 2) * 512 + hp * 128:
                                         (kt % 2) * 512 + (hp + 1) * 128]), 16)
                        s_va.wait(e, 16 * (i + 1))
                        if i >= 2:
                            s_vo[i % 2].wait(e, 16 * (i // 2))
                        s_vo[i % 2].inc(e.dma_start(
                            va_h.ap()[sq, br, i],
                            vstg[:, (i % 2) * 520:(i % 2 + 1) * 520]), 16)
                    for sl in range(2):
                        s_od[sl].wait(e, 16 * 64)
                        s_vo[sl].wait(e, 16 * 8)
            fsc.close()

        sc_x.close()

        sc_a = contextlib.ExitStack()
        rstage = ctx.sbuf(f"l0f_rstage{sq}", [1, 2 * 512], BF16, sc_a)
        causb = ctx.sbuf(f"l0f_causb{sq}", [128, 4 * 512], BF16, sc_a)
        qs = ctx.sbuf(f"l0f_qs{sq}", [64, 2 * 2048], BF16, sc_a)
        ks = ctx.sbuf(f"l0f_ks{sq}", [64, 2 * 2048], BF16, sc_a)
        vab = ctx.sbuf(f"l0f_vab{sq}", [128, 2 * 1040], BF16, sc_a)
        ptb = ctx.sbuf(f"l0f_ptb{sq}", [128, 16 * 512], BF16, sc_a)
        with nc.Block() as blk:
            psc = [ctx.psum(f"l0f_psc{sq}{j}", [128, 1024], F32, sc_a) for j in range(2)]
            pav = ctx.psum(f"l0f_pav{sq}", [65, 512], F32, sc_a)
            s_hl = [ctx.sem(scope=sc_a) for _ in range(2)]
            s_sc = ctx.sem(scope=sc_a)
            s_ex = ctx.sem(scope=sc_a)
            s_mk = ctx.sem(scope=sc_a)
            s_av = ctx.sem(scope=sc_a)
            s_ae = ctx.sem(scope=sc_a)
            s_rst = ctx.sem(scope=sc_a)
            s_rod = [ctx.sem(scope=sc_a, kls="gdma") for _ in range(2)]

            def sc_count(hh):
                return 40 * hh

            s_cl = ctx.sem()

            @blk.sync
            def _(e):
                prime(e, s_hl, s_av, s_cl)
                for ci in range(4):
                    s_cl.inc(e.dma_start(causb[:, ci * 512:(ci + 1) * 512],
                                         ctx.cdram[f"caus{ci}"].ap()), 16)
                for hh in range(16):
                    br, h = divmod(hh, 8)
                    if hh >= 2:
                        s_av.wait(e, 4 * (hh - 1))
                    s_hl[hh % 2].inc(e.dma_start(
                        qs[:, (hh % 2) * 2048:(hh % 2 + 1) * 2048],
                        qt_h.ap()[sq, br, h]), 16)
                    s_hl[hh % 2].inc(e.dma_start(
                        ks[:, (hh % 2) * 2048:(hh % 2 + 1) * 2048],
                        kt_h.ap()[sq, br, h]), 16)
                    s_hl[hh % 2].inc(e.dma_start(
                        vab[:, (hh % 2) * 1040:(hh % 2 + 1) * 1040],
                        bass.AP(va_h, ((sq * 2 + br) * 16) * 128 * 520 + h * 65,
                                [[520, 128], [128 * 520, 16], [1, 65]])), 16)

            @blk.tensor
            def _(e):
                prime(e, s_sc, s_av, s_hl, s_ex, s_ae, s_mk)
                kk = 0
                for hh in range(16):
                    br, h = divmod(hh, 8)
                    s_hl[hh % 2].wait(e, 48 * (hh // 2 + 1))
                    for tc in range(4):
                        g = hh * 4 + tc
                        nts = 4 * tc + 4
                        for ng in range(nts // 2):
                            if kk >= 2:
                                s_ex.wait(e, kk - 1)
                            for j in range(2):
                                nt = 2 * ng + j
                                ins = e.matmul(
                                    psc[kk % 2][:, j * 512:(j + 1) * 512],
                                    ks[:, (hh % 2) * 2048 + nt * 128:
                                       (hh % 2) * 2048 + (nt + 1) * 128],
                                    qs[:, (hh % 2) * 2048 + tc * 512:
                                       (hh % 2) * 2048 + (tc + 1) * 512],
                                    start=True, stop=True)
                            s_sc.inc(ins, 1)
                            kk += 1
                        s_ae.wait(e, g)
                        kk0 = kk - nts // 2
                        for nt in range(nts):
                            if nt >= 4 * tc:
                                s_mk.wait(e, 4 * g + (nt - 4 * tc) + 1)
                            else:
                                s_ex.wait(e, kk0 + nt // 2 + 1)
                            ins = e.matmul(
                                pav[:, :],
                                vab[:, (hh % 2) * 1040 + nt * 65:
                                    (hh % 2) * 1040 + (nt + 1) * 65],
                                ptb[:, nt * 512:(nt + 1) * 512],
                                start=(nt == 0), stop=(nt == nts - 1))
                        s_av.inc(ins, 1)

            @blk.scalar
            def _(e):
                prime(e, s_ex, s_sc)
                kk = 0
                for hh in range(16):
                    for tc in range(4):
                        nts = 4 * tc + 4
                        for ng in range(nts // 2):
                            s_sc.wait(e, kk + 1)
                            s_ex.inc(e.activation(ptb[:, ng * 1024:(ng + 1) * 1024],
                                                  psc[kk % 2][:, :], AF.Exp,
                                                  scale=0.125), 1)
                            kk += 1

            @blk.vector
            def _(e):
                prime(e, s_av, s_ae, s_rst, s_rod, s_ex, s_mk)
                s_cl.wait(e, 64)
                kk = 0
                for hh in range(16):
                    for tc in range(4):
                        g = hh * 4 + tc
                        nts = 4 * tc + 4
                        kk += nts // 2
                        for dd in range(4):
                            nt = 4 * tc + dd
                            s_ex.wait(e, kk - nts // 2 + nt // 2 + 1)
                            s_mk.inc(e.tensor_tensor(
                                ptb[:, nt * 512:(nt + 1) * 512],
                                ptb[:, nt * 512:(nt + 1) * 512],
                                causb[:, dd * 512:(dd + 1) * 512], OP.mult), 1)
                        s_av.wait(e, g + 1)
                        if g >= 2:
                            s_rod[g % 2].wait(e, 16 * (g // 2))
                        s_rst.inc(e.tensor_copy(
                            rstage[0:1, (g % 2) * 512:(g % 2 + 1) * 512],
                            pav[64:65, :]), 1)
                        s_ae.inc(e.tensor_copy(
                            avT[(hh % 2) * 64:(hh % 2) * 64 + 64,
                                (hh // 2) * 2048 + tc * 512:(hh // 2) * 2048 + (tc + 1) * 512],
                            pav[0:64, :]), 1)

            @blk.gpsimd
            def _(e):
                prime(e, s_rod, s_rst)
                for g in range(64):
                    s_rst.wait(e, g + 1)
                    s_rod[g % 2].inc(e.dma_start(
                        rs_h.ap()[sq, g:g + 1, :],
                        rstage[0:1, (g % 2) * 512:(g % 2 + 1) * 512]), 16)
                for sl in range(2):
                    s_rod[sl].wait(e, 16 * 32)
        sc_a.close()

        sc_p4 = contextlib.ExitStack()
        wob = ctx.sbuf(f"l0f_wob{sq}", [128, 8 * 1024], BF16, sc_p4)
        wstg4 = ctx.sbuf(f"l0f_wstg4{sq}", [128, 2 * 1024], F32, sc_p4)
        rows = ctx.sbuf(f"l0f_rows{sq}", [64, 512], BF16, sc_p4)
        gm32 = ctx.sbuf(f"l0f_gm32{sq}", [32, 512], F32, sc_p4)
        mk32 = ctx.sbuf(f"l0f_mk32{sq}", [32, 512], F32, sc_p4)
        scl = ctx.sbuf(f"l0f_scl{sq}", [64, 512], F32, sc_p4)
        mrec = ctx.sbuf(f"l0f_mrec{sq}", [32, 512], F32, sc_p4)
        selb2 = ctx.sbuf(f"l0f_selb{sq}", [64, 32 * 128], BF16, sc_p4)
        sclb = ctx.sbuf(f"l0f_sclb{sq}", [64, 512], BF16, sc_p4)
        outb = ctx.sbuf(f"l0f_outb{sq}", [128, 8 * 512], BF16, sc_p4)
        s_oev = ctx.sem(scope=sc_seq)
        sc_nrm = contextlib.ExitStack()
        with nc.Block() as blk:
            pbc = [ctx.psum(f"l0f_pbc{sq}{j}", [128, 512], F32, sc_nrm) for j in range(2)]
            s_rl = ctx.sem()
            s_rc = ctx.sem(scope=sc_nrm)
            s_bc = ctx.sem(scope=sc_nrm)
            s_nm = ctx.sem(scope=sc_nrm)
            s_wl = [ctx.sem(scope=sc_p4) for _ in range(2)]
            s_wc = ctx.sem(scope=sc_p4)

            @blk.sync
            def _(e):
                prime(e, s_wl, s_wc, s_rl)
                s_rl.inc(e.dma_start(rows[:, :], rs_h.ap()[sq]), 16)
                s_rl.wait(e, 16)
                s_rl.inc(e.dma_start(gm32[:, :],
                                     bass.AP(gm_h, sq * 2 * 8 * 2048,
                                             [[2048, 8], [512, 4], [1, 512]])), 16)
                s_rl.wait(e, 32)
                s_rl.inc(e.dma_start(mk32[:, :],
                                     bass.AP(gm_h, sq * 2 * 8 * 2048 + 8 * 2048,
                                             [[2048, 8], [512, 4], [1, 512]])), 16)
                s_rl.wait(e, 48)
                s_rl.inc(e.dma_start(selb2[:, :], ctx.cdram["self"].ap()), 16)
                for br in range(2):
                    for ec4 in range(4):
                        ec = br * 4 + ec4
                        if ec >= 2:
                            s_wc.wait(e, ec - 1)
                        s_wl[ec % 2].inc(e.dma_start(
                            wstg4[:, (ec % 2) * 1024:(ec % 2 + 1) * 1024],
                            wo_in[br].ap()[ec4 * 128:(ec4 + 1) * 128, :]), 16)

            @blk.vector
            def _(e):
                prime(e, s_wl, s_wc, s_rc, s_bc, s_nm)
                for ec in range(8):
                    s_wl[ec % 2].wait(e, 16 * (ec // 2 + 1))
                    s_wc.inc(e.tensor_copy(wob[:, ec * 1024:(ec + 1) * 1024],
                                           wstg4[:, (ec % 2) * 1024:(ec % 2 + 1) * 1024]), 1)
                s_rl.wait(e, 48)
                v = 0
                s_rc.inc(e.tensor_scalar(scl[0:32, :], rows[0:32, :], 1.0, None,
                                         OP.add), 1); v += 1
                s_rc.wait(e, v)
                s_rc.inc(e.tensor_sub(scl[0:32, :], scl[0:32, :], mk32[:, :]), 1); v += 1
                s_rc.wait(e, v)
                s_rc.inc(e.reciprocal(mrec[:, :], scl[0:32, :]), 1); v += 1
                s_rc.wait(e, v)
                s_rc.inc(e.tensor_mul(scl[0:32, :], gm32[:, :], mrec[:, :]), 1); v += 1
                s_rc.wait(e, v)
                s_rc.inc(e.reciprocal(scl[32:64, :], rows[32:64, :]), 1); v += 1
                s_rc.wait(e, v)
                s_rc.inc(e.tensor_copy(sclb[:, :], scl[:, :]), 1); v += 1
                for j in range(32):
                    br, rem = divmod(j, 16)
                    p, tc = divmod(rem, 4)
                    s_bc.wait(e, j + 1)
                    s_nm.inc(e.tensor_tensor(
                        avT[:, (br * 4 + p) * 2048 + tc * 512:
                            (br * 4 + p) * 2048 + (tc + 1) * 512],
                        avT[:, (br * 4 + p) * 2048 + tc * 512:
                            (br * 4 + p) * 2048 + (tc + 1) * 512],
                        pbc[j % 2][:, :], OP.mult), 1)

            @blk.tensor
            def _(e):
                prime(e, s_rc, s_bc, s_nm)
                s_rc.wait(e, 6)
                s_rl.wait(e, 64)
                for j in range(32):
                    if j >= 2:
                        s_nm.wait(e, j - 1)
                    s_bc.inc(e.matmul(pbc[j % 2][:, :],
                                      selb2[:, j * 128:(j + 1) * 128],
                                      sclb[:, :], start=True, stop=True), 1)
        sc_nrm.close()
        for tc in range(4):
            wsc2 = contextlib.ExitStack()
            with nc.Block() as blk:
                pw = [ctx.psum(f"l0f_pw{sq}{tc}{j}", [128, 512], F32, wsc2) for j in range(8)]
                s_wm = ctx.sem(scope=wsc2)

                @blk.tensor
                def _(e):
                    prime(e, s_wm, s_wc)
                    s_wc.wait(e, 8)
                    for eci in range(8):
                        for hd8 in range(8):
                            br, hdc = divmod(hd8, 4)
                            ins = e.matmul(
                                pw[eci][:, :],
                                wob[:, (br * 4 + hdc) * 1024 + eci * 128:
                                    (br * 4 + hdc) * 1024 + (eci + 1) * 128],
                                avT[:, (br * 4 + hdc) * 2048 + tc * 512:
                                    (br * 4 + hdc) * 2048 + (tc + 1) * 512],
                                start=(hd8 == 0), stop=(hd8 == 7))
                        s_wm.inc(ins, 1)

                @blk.scalar
                def _(e):
                    prime(e, s_oev, s_wm)
                    for eci in range(8):
                        s_wm.wait(e, eci + 1)
                        s_oev.inc(e.activation(outb[:, eci * 512:(eci + 1) * 512],
                                               pw[eci][:, :], AF.Copy), 1)
            wsc2.close()
            emit_out_ln_add(ctx, f"l0f_o{sq}_{tc}", outb, 512, sq * 2048 + tc * 512)
        sc_p4.close()
        sc_seq.close()


def build_program(upto="all", taps=()):
    stack = contextlib.ExitStack()
    nc = bass.Bass("TRN2", target_bir_lowering=False)
    ctx = Ctx(nc, stack)
    ctx.tap_names = tuple(taps)

    x_in = nc.dram_tensor("x", [T_TOK, E], F32, kind="ExternalInput")
    y_out = nc.dram_tensor("y", [T_TOK, E], F32, kind="ExternalOutput")
    cdefs = host_constants()
    cdram = {}
    for k, v in cdefs.items():
        dt = BF16 if v.dtype == BF else F32
        cdram[k] = nc.dram_tensor(k, list(v.shape), dt, kind="ExternalInput")

    ctx.cdram = cdram
    for k, v in cdefs.items():
        if k in ("seln", "self", "caus0", "caus1", "caus2", "caus3"):
            continue
        dt = BF16 if v.dtype == BF else F32
        ctx.cb[k] = ctx.sbuf("c_" + k, list(v.shape), dt)
    ctx.b["idn"] = ctx.cb["idn"]
    xres = ctx.sbuf("xres", [128, N_TILE * 1024], BF16)
    ctx.sbuf("g_lnscr", [128, 2 * 1024], BF16)
    sc_ph0 = contextlib.ExitStack()
    stage = ctx.sbuf("stage", [128, 4 * 1024], F32, sc_ph0)

    s_xd = [ctx.sem(f"xd{j}") for j in range(4)]
    s_xc = ctx.sem("xc")
    s_cd = ctx.sem("cd")
    ncst = len(cdefs)
    with nc.Block() as blk:
        @blk.sync
        def _(e):
            for i in range(N_TILE):
                if i >= 4:
                    s_xc.wait(e, i - 3)
                s_xd[i % 4].inc(e.dma_start(
                    stage[:, (i % 4) * 1024:(i % 4 + 1) * 1024],
                    x_in.ap()[i * 128:(i + 1) * 128, :]), 16)

        @blk.gpsimd
        def _(e):
            for k in cdefs:
                if k in ("seln", "self", "caus0", "caus1", "caus2", "caus3"):
                    continue
                s_cd.inc(e.dma_start(ctx.cb[k][:, :], cdram[k].ap()), 16)
            s_cd.wait(e, 16 * (ncst - 6))

        @blk.vector
        def _(e):
            for i in range(N_TILE):
                s_xd[i % 4].wait(e, 16 * (i // 4 + 1))
                s_xc.inc(e.tensor_copy(
                    tile(xres, i),
                    stage[:, (i % 4) * 1024:(i % 4 + 1) * 1024]), 1)

    sc_ph0.close()
    if upto != "copy":
        order = ["l0a", "l0f", "l1a", "l1f"]
        emits = {"l0a": emit_l0_attn, "l0f": emit_l0_ffn,
                 "l1a": emit_l1_attn, "l1f": emit_l1_ffn}
        sel = order if upto == "all" else upto.split(",")
        for s in sel:
            emits[s](ctx)

    s_oc = ctx.sem("oc")
    s_od = [ctx.sem("od0"), ctx.sem("od1")]
    sc_out = contextlib.ExitStack()
    stage = ctx.sbuf("ostage", [128, 2 * 1024], F32, sc_out)
    with nc.Block() as blk:
        @blk.scalar
        def _(e):
            for i in range(N_TILE):
                if i >= 2:
                    s_od[i % 2].wait(e, 16 * (i // 2))
                s_oc.inc(e.activation(stage[:, (i % 2) * 1024:(i % 2 + 1) * 1024],
                                      tile(xres, i), AF.Copy), 1)

        @blk.sync
        def _(e):
            for i in range(N_TILE):
                s_oc.wait(e, i + 1)
                s_od[i % 2].inc(e.dma_start(
                    y_out.ap()[i * 128:(i + 1) * 128, :],
                    stage[:, (i % 2) * 1024:(i % 2 + 1) * 1024]), 16)
            s_od[0].wait(e, 16 * 16)
            s_od[1].wait(e, 16 * 16)
    sc_out.close()

    nc._ctx_keepalive = stack
    return nc, ctx


def make_in_maps(inputs, upto="all"):
    cdefs = host_constants()
    x = np.ascontiguousarray(np.asarray(inputs["L0_attn_x"], dtype=np.float32))
    sel = ["l0a", "l0f", "l1a", "l1f"] if upto == "all" else upto.split(",")
    shared = dict(cdefs)
    if "l0a" in sel:
        shared["l0a_wq"] = np.ascontiguousarray(np.asarray(inputs["L0_attn_Wq"], dtype=np.float32))
        shared["l0a_wk"] = np.ascontiguousarray(np.asarray(inputs["L0_attn_Wk"], dtype=np.float32))
        shared["l0a_wv"] = np.ascontiguousarray(np.asarray(inputs["L0_attn_Wv"], dtype=np.float32))
        shared["l0a_wo"] = np.ascontiguousarray(np.asarray(inputs["L0_attn_Wo"], dtype=np.float32))
    if "l0f" in sel:
        shared["l0f_rw"] = np.ascontiguousarray(np.asarray(inputs["L0_ffn_mosa_router_w"], dtype=np.float32))
        shared["l0f_mqkv"] = np.ascontiguousarray(np.asarray(inputs["L0_ffn_mosa_wqkv"], dtype=np.float32).reshape(1024, 1536))
        shared["l0f_dqkv"] = np.ascontiguousarray(np.asarray(inputs["L0_ffn_dense_wqkv"], dtype=np.float32).reshape(1024, 1536))
        shared["l0f_mwo"] = np.ascontiguousarray(np.asarray(inputs["L0_ffn_mosa_wo"], dtype=np.float32))
        shared["l0f_dwo"] = np.ascontiguousarray(np.asarray(inputs["L0_ffn_dense_wo"], dtype=np.float32))
    if "l1a" in sel:
        shared["l1a_k"] = np.ascontiguousarray(np.asarray(inputs["L1_attn_k"], dtype=np.float32))
        shared["l1a_v"] = np.ascontiguousarray(np.asarray(inputs["L1_attn_v"], dtype=np.float32))
    if "l1f" in sel:
        UTr, Vr = host_l1f_factors(inputs["L1_ffn_G1_R"], inputs["L1_ffn_G2_R"])
        UTi, Vi = host_l1f_factors(inputs["L1_ffn_G1_I"], inputs["L1_ffn_G2_I"])
        shared["l1f_ut"] = np.ascontiguousarray(np.stack([UTr, UTi]))
        shared["l1f_v"] = np.ascontiguousarray(np.stack([Vr, Vi]))
        shared["l1f_br"] = np.ascontiguousarray(np.asarray(inputs["L1_ffn_bias_real"], dtype=np.float32))
        shared["l1f_bi"] = np.ascontiguousarray(np.asarray(inputs["L1_ffn_bias_imag"], dtype=np.float32))
    in_maps = []
    for c in range(8):
        m = {"x": np.ascontiguousarray(x[2 * c:2 * c + 2].reshape(T_TOK, E))}
        m.update(shared)
        in_maps.append(m)
    return in_maps


def run(inputs, upto="all", taps=()):
    nc, ctx = build_program(upto=upto, taps=taps)
    in_maps = make_in_maps(inputs, upto=upto)
    res = run_bass_kernel_spmd(nc, in_maps, core_ids=list(range(8)))
    return res


def kernel(**inputs):
    res = run(inputs, upto="all")
    out = np.concatenate([r["y"].reshape(2, S_LEN, E) for r in res.results], axis=0)
    return out.astype(np.float32)


# version 10
# speedup vs baseline: 1.0043x; 1.0023x over previous
import contextlib
import math
import numpy as np
import ml_dtypes
import concourse.bass as bass
import concourse.mybir as mybir
from concourse.bass_utils import run_bass_kernel_spmd

F32 = mybir.dt.float32
BF16 = mybir.dt.bfloat16
AF = mybir.ActivationFunctionType
OP = mybir.AluOpType
AX = mybir.AxisListType

T_TOK = 4096
S_LEN = 2048
N_TILE = 32
S_TILE = 16
E = 1024
NCH = 8
BF = ml_dtypes.bfloat16


def host_constants():
    c = {}
    c["idn"] = np.eye(128).astype(BF)
    C64 = (np.eye(64, dtype=np.float64) - 1.0 / 64.0) / 8.0
    Cb = np.zeros((128, 128), dtype=np.float64)
    Cb[:64, :64] = C64
    Cb[64:, 64:] = C64
    c["cblk"] = Cb.astype(BF)
    ind = np.zeros((128, 2), dtype=np.float32)
    ind[:64, 0] = 1.0
    ind[64:, 1] = 1.0
    c["ind2"] = ind.astype(BF)
    c["ones_row"] = np.ones((1, 128), dtype=np.float32).astype(BF)
    c["ones_col"] = np.ones((128, 1), dtype=np.float32).astype(BF)
    seln = np.zeros((64, 32 * 128), dtype=np.float32)
    for p in range(8):
        for tc in range(4):
            j = p * 4 + tc
            seln[(2 * p) * 4 + tc, j * 128:j * 128 + 64] = 1.0
            seln[(2 * p + 1) * 4 + tc, j * 128 + 64:j * 128 + 128] = 1.0
    c["seln"] = seln.astype(BF)
    self_ = np.zeros((64, 32 * 128), dtype=np.float32)
    for br in range(2):
        for p in range(4):
            for tc in range(4):
                j = br * 16 + p * 4 + tc
                self_[br * 32 + (2 * p) * 4 + tc, j * 128:j * 128 + 64] = 1.0
                self_[br * 32 + (2 * p + 1) * 4 + tc, j * 128 + 64:j * 128 + 128] = 1.0
    c["self"] = self_.astype(BF)
    half = 16
    inv = 10000.0 ** (-np.arange(half, dtype=np.float64) / half)
    pos = np.arange(S_LEN, dtype=np.float64)
    ang = pos[:, None] * inv[None, :]
    def _tiled(a):
        return np.ascontiguousarray(
            a.astype(np.float32).reshape(16, 128, 16).transpose(1, 0, 2).reshape(128, 256))
    cosv, sinv = np.cos(ang), np.sin(ang)
    chi, shi = cosv.astype(np.float32).astype(np.float64), sinv.astype(np.float32).astype(np.float64)
    c["cos_t"] = _tiled(cosv)
    c["sin_t"] = _tiled(sinv)
    c["cos_lo"] = _tiled(cosv - chi)
    c["sin_lo"] = _tiled(sinv - shi)
    for k, off in enumerate((0, 128, 256, 384)):
        n = np.arange(128)[:, None]
        t = np.arange(512)[None, :]
        c[f"caus{k}"] = (t >= n + off).astype(np.float32).astype(BF)
    return c


def host_l1f_factors(G1, G2):
    G1 = np.asarray(G1, dtype=np.float32)
    G2 = np.asarray(G2, dtype=np.float32)
    UT = np.zeros((4, 512, 512), dtype=np.float32)
    V = np.zeros((4, 512, 512), dtype=np.float32)
    b_i, d_i, g_i, e_i = np.arange(32), np.arange(32), np.arange(16), np.arange(16)
    for c in range(4):
        A = G1[0, :, c, :]
        B = G2[c, :, 0, :]
        bb, dd, gg = np.meshgrid(b_i, d_i, g_i, indexing="ij")
        UT[c, bb * 16 + gg, dd * 16 + gg] = A[bb, dd]
        bb, gg, ee = np.meshgrid(b_i, g_i, e_i, indexing="ij")
        V[c, bb * 16 + gg, bb * 16 + ee] = B[ee, gg]
    return UT, V


class PSem:
    def __init__(self, handle):
        self.h = handle
        self.base = 0
        self.planned = 0

    def wait(self, e, v):
        e.wait_ge(self.h, self.base + v)

    def inc(self, ins, n=1):
        ins.then_inc(self.h, n)
        self.planned += n


class Ctx:
    def __init__(self, nc, stack):
        self.nc = nc
        self.stack = stack
        self._i = 0
        self.b = {}
        self.cb = {}
        self.tap_names = ()
        self._sem_pool = {}

    def sem(self, name=None, scope=None, kls="c"):
        pool = self._sem_pool.setdefault(kls, [])
        if scope is not None and pool:
            ps = pool.pop()
        else:
            self._i += 1
            ps = PSem(self.stack.enter_context(
                self.nc.semaphore(name or f"s{self._i}")))
        if scope is not None:
            scope.callback(self._ret_sem, ps, kls)
        return ps

    def _ret_sem(self, ps, kls="c"):
        ps.base += ps.planned
        ps.planned = 0
        self._sem_pool.setdefault(kls, []).append(ps)

    def sbuf(self, name, shape, dtype, scope=None):
        h = (scope or self.stack).enter_context(self.nc.sbuf_tensor(name, shape, dtype))
        self.b[name] = h
        return h

    def psum(self, name, shape, dtype, scope=None):
        h = (scope or self.stack).enter_context(self.nc.psum_tensor(name, shape, dtype))
        self.b[name] = h
        return h

    def tap(self, name, ap, rows, cols):
        if name not in self.tap_names:
            return
        nc = self.nc
        td = nc.dram_tensor("tap_" + name, [rows, cols], F32, kind="ExternalOutput")
        sc = contextlib.ExitStack()
        stg = self.sbuf("tapstg_" + name, [rows, cols], F32, sc)
        s = self.sem(scope=sc)
        with nc.Block() as blk:
            @blk.vector
            def _(e):
                prime(e, s)
                s.inc(e.tensor_copy(stg[:, :], ap), 1)

            @blk.sync
            def _(e):
                s.wait(e, 1)
                s.inc(e.dma_start(td.ap(), stg[:, :]), 16)
                s.wait(e, 17)
        sc.close()


def tile(x, i, w=1024):
    return x[:, i * w:(i + 1) * w]


def prime(e, *sems):
    for ps in sems:
        for x in (ps if isinstance(ps, (list, tuple)) else [ps]):
            x.wait(e, 0)


def emit_ln(ctx, blk, name, src_f, dst_f, n, src_gate=None, apply_gate=None,
            done=None, scope=None):
    st = ctx.sbuf(f"lnst_{name}", [128, 4 * n], F32, scope)
    scr = ctx.b["g_lnscr"]
    m2 = ctx.sbuf(f"lnm2_{name}", [128, n], F32, scope)
    C_SUM, C_SS, C_R, C_B = 0, n, 2 * n, 3 * n
    sf = ctx.sem(scope=scope)
    vf = ctx.sem(scope=scope)

    @blk.scalar
    def _(e):
        prime(e, sf, vf, *( [done] if done is not None else [] ))
        for j in range(n):
            if src_gate is not None:
                g = src_gate(j)
                if g is not None:
                    g[0].wait(e, g[1])
            if j >= 2:
                sf.wait(e, j - 1)
            sf.inc(e.activation(scr[:, (j % 2) * 1024:(j % 2 + 1) * 1024],
                                src_f(j), AF.Square,
                                accum_out=st[:, C_SS + j:C_SS + j + 1]), 1)
        sf.wait(e, n)
        vf.wait(e, n + 4)
        sf.inc(e.activation(st[:, C_R:C_R + n], st[:, C_SS:C_SS + n], AF.Sqrt), 1)
        vf.wait(e, n + 6)
        for j in range(n):
            if apply_gate is not None:
                g = apply_gate(j)
                if g is not None:
                    g[0].wait(e, g[1])
            ins = e.activation(dst_f(j), src_f(j), AF.Identity,
                               scale=m2[:, j:j + 1],
                               bias=st[:, C_B + j:C_B + j + 1])
            if done is not None:
                done.inc(ins, 1)

    @blk.vector
    def _(e):
        prime(e, vf, sf)
        for j in range(n):
            if src_gate is not None:
                g = src_gate(j)
                if g is not None:
                    g[0].wait(e, g[1])
            vf.inc(e.reduce_sum(st[:, C_SUM + j:C_SUM + j + 1], src_f(j),
                                axis=AX.X), 1)
        vf.wait(e, n)
        vf.inc(e.tensor_scalar(st[:, C_SUM:C_SUM + n], st[:, C_SUM:C_SUM + n],
                               1.0 / 1024.0, None, OP.mult), 1)
        sf.wait(e, n)
        vf.wait(e, n + 1)
        vf.inc(e.tensor_mul(m2[:, 0:n], st[:, C_SUM:C_SUM + n],
                            st[:, C_SUM:C_SUM + n]), 1)
        vf.wait(e, n + 2)
        vf.inc(e.scalar_tensor_tensor(st[:, C_R:C_R + n], st[:, C_SS:C_SS + n],
                                      1.0 / 1024.0, m2[:, 0:n],
                                      OP.mult, OP.subtract), 1)
        vf.wait(e, n + 3)
        vf.inc(e.tensor_scalar(st[:, C_SS:C_SS + n], st[:, C_R:C_R + n],
                               1e-5, None, OP.add), 1)
        sf.wait(e, n + 1)
        vf.wait(e, n + 4)
        vf.inc(e.reciprocal(m2[:, 0:n], st[:, C_R:C_R + n]), 1)
        vf.wait(e, n + 5)
        vf.inc(e.scalar_tensor_tensor(st[:, C_B:C_B + n], st[:, C_SUM:C_SUM + n],
                                      -1.0, m2[:, 0:n],
                                      OP.mult, OP.mult), 1)


def emit_ln_transpose(ctx, blk, name, tiles, dst, tcols, coloff,
                      src_gate=None, scope=None, sem_scope=None):
    n = len(tiles)
    nring = min(8, ((n + 3) // 4) * 4)
    ring = ctx.sbuf(f"ring_{name}", [128, nring * 1024], BF16, scope)
    pa = ctx.psum(f"pta_{name}", [128, 512], BF16, scope)
    pb = ctx.psum(f"ptb_{name}", [128, 512], BF16, scope)
    idn = ctx.b["idn"]
    s_app = ctx.sem(scope=scope)
    s_tr = ctx.sem(scope=scope)
    s_ev = ctx.sem(scope=sem_scope or scope)
    nsup = (n + 3) // 4

    def ring_gate(j):
        sup_prev = j // 4 - (nring // 4)
        if sup_prev < 0:
            return None
        return (s_ev, NCH * (sup_prev + 1))

    emit_ln(ctx, blk, name, lambda j: tile(ctx.b["xres"], tiles[j]),
            lambda j: tile(ring, j % nring), n, src_gate=src_gate,
            apply_gate=ring_gate, done=s_app, scope=scope)

    @blk.tensor
    def _(e):
        prime(e, s_app, s_tr, s_ev)
        k = 0
        for sup in range(nsup):
            j0 = sup * 4
            cnt = min(4, n - j0)
            s_app.wait(e, j0 + cnt)
            for c in range(NCH):
                buf = pa if (k % 2 == 0) else pb
                if k >= 2:
                    s_ev.wait(e, k - 1)
                for j in range(cnt):
                    ins = e.transpose(buf[:, j * 128:(j + 1) * 128],
                                      tile(ring, (j0 + j) % nring)[:, c * 128:(c + 1) * 128],
                                      idn[:, :])
                s_tr.inc(ins, 1)
                k += 1

    @blk.vector
    def _(e):
        prime(e, s_tr, s_ev)
        k = 0
        for sup in range(nsup):
            j0 = sup * 4
            cnt = min(4, n - j0)
            for c in range(NCH):
                buf = pa if (k % 2 == 0) else pb
                s_tr.wait(e, k + 1)
                s_ev.inc(e.tensor_copy(
                    dst[:, c * tcols + coloff + j0 * 128:
                        c * tcols + coloff + j0 * 128 + cnt * 128],
                    buf[:, 0:cnt * 128]), 1)
                k += 1
    return NCH * nsup, s_ev


def emit_out_ln_add(ctx, name, src, tcols, tok0, src_ready=None):
    nc = ctx.nc
    n = tcols // 128
    nring = min(8, n)
    sc = contextlib.ExitStack()
    with nc.Block() as blk:
        ring = ctx.sbuf(f"oring_{name}", [128, nring * 1024], BF16, sc)
        lnout = ctx.sbuf(f"olnout_{name}", [128, 2 * 1024], BF16, sc)
        pa = ctx.psum(f"opta_{name}", [128, 512], BF16, sc)
        pb = ctx.psum(f"optb_{name}", [128, 512], BF16, sc)
        idn = ctx.b["idn"]
        s_tr = ctx.sem(scope=sc)
        s_ev = ctx.sem(scope=sc)
        s_app = ctx.sem(scope=sc)
        s_add = ctx.sem(scope=sc)

        @blk.tensor
        def _(e):
            prime(e, s_tr, s_ev, s_app, s_add)
            if src_ready is not None:
                src_ready[0].wait(e, src_ready[1])
            k = 0
            for j in range(n):
                for cp in range(2):
                    buf = pa if (k % 2 == 0) else pb
                    if k >= 2:
                        s_ev.wait(e, k - 1)
                    if j >= nring and cp == 0:
                        s_app.wait(e, j - nring + 1)
                    for cc in range(4):
                        c = cp * 4 + cc
                        ins = e.transpose(
                            buf[:, cc * 128:(cc + 1) * 128],
                            src[:, c * tcols + j * 128: c * tcols + (j + 1) * 128],
                            idn[:, :])
                    s_tr.inc(ins, 1)
                    k += 1

        @blk.vector
        def _(e):
            prime(e, s_tr, s_ev, s_app, s_add)
            for k in range(2 * n):
                j, cp = divmod(k, 2)
                buf = pa if (k % 2 == 0) else pb
                s_tr.wait(e, k + 1)
                s_ev.inc(e.tensor_copy(tile(ring, j % nring)[:, cp * 512:(cp + 1) * 512],
                                       buf[:, 0:512]), 1)

        def src_gate(j):
            return (s_ev, 2 * (j + 1))

        def apply_gate(j):
            if j < 2:
                return None
            return (s_add, j - 1)

        emit_ln(ctx, blk, f"ln2_{name}", lambda j: tile(ring, j % nring),
                lambda j: tile(lnout, j % 2), n,
                src_gate=src_gate, apply_gate=apply_gate, done=s_app, scope=sc)

        @blk.vector
        def _(e):
            prime(e, s_add)
            for j in range(n):
                s_app.wait(e, j + 1)
                s_add.inc(e.tensor_add(tile(ctx.b["xres"], tok0 // 128 + j),
                                       tile(ctx.b["xres"], tok0 // 128 + j),
                                       tile(lnout, j % 2)), 1)
    ctx.tap(f"lnst_{name}", ctx.b[f"lnst_ln2_{name}"][:, :], 128, 4 * n)
    ctx.tap(f"ring_{name}", ctx.b[f"oring_{name}"][:, 0:1024], 128, 1024)
    sc.close()


def emit_l1_ffn(ctx):
    nc = ctx.nc
    ut_in = nc.dram_tensor("l1f_ut", [2, 4, 512, 512], F32, kind="ExternalInput")
    v_in = nc.dram_tensor("l1f_v", [2, 4, 512, 512], F32, kind="ExternalInput")
    br_in = nc.dram_tensor("l1f_br", [512], F32, kind="ExternalInput")
    bi_in = nc.dram_tensor("l1f_bi", [512], F32, kind="ExternalInput")

    sc_all = contextlib.ExitStack()
    sc_w = contextlib.ExitStack()
    wT = ctx.sbuf("l1f_wT", [128, 12 * 512], BF16, sc_all)
    bias = ctx.sbuf("l1f_bias", [128, 8], F32, sc_all)
    s_db = ctx.sem(scope=sc_all)

    with nc.Block() as blk:
        uv = ctx.sbuf("l1f_uv", [128, 4 * 512], F32, sc_w)
        uvb = ctx.sbuf("l1f_uvb", [128, 64 * 512], BF16, sc_w)
        pw = ctx.psum("l1f_pw", [128, 512], F32, sc_w)
        pw2 = ctx.psum("l1f_pw2", [128, 512], F32, sc_w)
        s_d = [ctx.sem(scope=sc_w) for _ in range(4)]
        s_c = ctx.sem(scope=sc_w)
        s_m = ctx.sem(scope=sc_w)
        s_w = ctx.sem(scope=sc_w)

        def ucol(uvsel, ri, c, kch):
            return (uvsel * 32 + ri * 16 + c * 4 + kch) * 512

        @blk.sync
        def _(e):
            prime(e, s_d, s_db, s_c)
            for uvsel in range(2):
                src = ut_in if uvsel == 0 else v_in
                for ri in range(2):
                    for c in range(4):
                        for kch in range(4):
                            k = uvsel * 32 + ri * 16 + c * 4 + kch
                            if k >= 4:
                                s_c.wait(e, k - 3)
                            s_d[k % 4].inc(e.dma_start(
                                uv[:, (k % 4) * 512:(k % 4 + 1) * 512],
                                src.ap()[ri, c, kch * 128:(kch + 1) * 128, :]), 16)
            s_db.inc(e.dma_start(bias[:, 0:4],
                                 bass.AP(br_in, 0, [[1, 128], [128, 4]]),
                                 allow_slow_non_contiguous=True), 16)
            s_db.inc(e.dma_start(bias[:, 4:8],
                                 bass.AP(bi_in, 0, [[1, 128], [128, 4]]),
                                 allow_slow_non_contiguous=True), 16)

        @blk.vector
        def _(e):
            prime(e, s_d, s_c)
            for k in range(64):
                s_d[k % 4].wait(e, 16 * (k // 4 + 1))
                s_c.inc(e.tensor_copy(uvb[:, k * 512:(k + 1) * 512],
                                      uv[:, (k % 4) * 512:(k % 4 + 1) * 512]), 1)

        @blk.tensor
        def _(e):
            prime(e, s_c, s_m, s_w)
            s_c.wait(e, 64)
            k = 0
            for ri in range(2):
                for mq in range(4):
                    buf = pw if (k % 2 == 0) else pw2
                    if k >= 2:
                        s_w.wait(e, k - 1)
                    for c in range(4):
                        for kch in range(4):
                            ins = e.matmul(
                                buf[:, :],
                                uvb[:, ucol(0, ri, c, kch) + mq * 128:
                                    ucol(0, ri, c, kch) + (mq + 1) * 128],
                                uvb[:, ucol(1, ri, c, kch):
                                    ucol(1, ri, c, kch) + 512],
                                start=(c == 0 and kch == 0),
                                stop=(c == 3 and kch == 3))
                    s_m.inc(ins, 1)
                    k += 1

        @blk.scalar
        def _(e):
            prime(e, s_m, s_w)
            for k in range(8):
                buf = pw if (k % 2 == 0) else pw2
                s_m.wait(e, k + 1)
                s_w.inc(e.activation(wT[:, k * 512:(k + 1) * 512],
                                     buf[:, :], AF.Copy), 1)
            s_w.wait(e, 8)
            for mq in range(4):
                e.activation(wT[:, (8 + mq) * 512:(9 + mq) * 512],
                             wT[:, (4 + mq) * 512:(5 + mq) * 512],
                             AF.Copy, scale=-1.0)
    ctx.tap("uvb_ut00", uvb[:, 0:512], 128, 512)
    ctx.tap("uvb_v00", uvb[:, 32 * 512:33 * 512], 128, 512)
    ctx.tap("wTr0", wT[:, 0:512], 128, 512)
    ctx.tap("wTi0", wT[:, 4 * 512:5 * 512], 128, 512)
    sc_w.close()

    sc_x = contextlib.ExitStack()
    sc_ln = contextlib.ExitStack()
    xlnT = ctx.sbuf("l1f_xlnT", [128, NCH * T_TOK], BF16, sc_x)
    with nc.Block() as blk:
        nev, s_ln = emit_ln_transpose(ctx, blk, "l1f_ln1", list(range(N_TILE)),
                                      xlnT, T_TOK, 0, scope=sc_ln, sem_scope=sc_x)
    sc_ln.close()
    ctx.tap("xlnT_c0", xlnT[:, 0:512], 128, 512)
    ctx.tap("xlnT_c4", xlnT[:, 4 * T_TOK:4 * T_TOK + 512], 128, 512)

    for g in range(T_TOK // 512):
        gsc = contextlib.ExitStack()
        psc = contextlib.ExitStack()
        outbuf = ctx.sbuf(f"l1f_out{g}", [128, NCH * 512], BF16, gsc)
        s_mm = ctx.sem(scope=gsc)
        s_oev = ctx.sem(scope=gsc)
        with nc.Block() as blk:
            pts = [ctx.psum(f"l1f_pg{g}_{o}", [128, 512], F32, psc) for o in range(8)]

            @blk.tensor
            def _(e):
                prime(e, s_mm, s_ln)
                s_ln.wait(e, nev)
                for q in range(8):
                    ri, be = divmod(q, 4)
                    for part in range(2):
                        if ri == 0:
                            wreg = 0 if part == 0 else 8
                        else:
                            wreg = 4 if part == 0 else 0
                        for cq in range(4):
                            ins = e.matmul(
                                pts[q][:, :],
                                wT[:, (wreg + cq) * 512 + be * 128:
                                    (wreg + cq) * 512 + (be + 1) * 128],
                                xlnT[:, (part * 4 + cq) * T_TOK + g * 512:
                                     (part * 4 + cq) * T_TOK + (g + 1) * 512],
                                start=(part == 0 and cq == 0),
                                stop=(part == 1 and cq == 3))
                    s_mm.inc(ins, 1)

            @blk.scalar
            def _(e):
                prime(e, s_oev, s_mm)
                s_db.wait(e, 32)
                for q in range(8):
                    ri, be = divmod(q, 4)
                    s_mm.wait(e, q + 1)
                    s_oev.inc(e.activation(outbuf[:, q * 512:(q + 1) * 512],
                                           pts[q][:, :], AF.Identity,
                                           bias=bias[:, ri * 4 + be:ri * 4 + be + 1]), 1)

        psc.close()
        if g == 0:
            ctx.tap("outbuf0", outbuf[:, 0:8 * 512], 128, 8 * 512)
        emit_out_ln_add(ctx, f"l1f_g{g}", outbuf, 512, g * 512,
                        src_ready=(s_oev, 8))
        gsc.close()
    sc_x.close()
    sc_all.close()


def emit_l1_attn(ctx):
    nc = ctx.nc
    k_in = nc.dram_tensor("l1a_k", [8192, 1024], F32, kind="ExternalInput")
    v_in = nc.dram_tensor("l1a_v", [8192, 1024], F32, kind="ExternalInput")
    kT_h = nc.dram_tensor("l1a_kT", [1024, 8192], BF16)
    kb_h = nc.dram_tensor("l1a_kb", [8192, 1024], BF16)
    vb_h = nc.dram_tensor("l1a_vb", [8192, 1024], BF16)
    idn = ctx.b["idn"]

    sc_all = contextlib.ExitStack()
    mb = ctx.sbuf("l1a_mb", [128, 8 * 1024], BF16, sc_all)

    sc_p = contextlib.ExitStack()
    with nc.Block() as blk:
        stg = ctx.sbuf("l1a_stg", [128, 2 * 1024], F32, sc_p)
        kbb = ctx.sbuf("l1a_kbb", [128, 2 * 1024], BF16, sc_p)
        ktst = ctx.sbuf("l1a_ktst", [128, 2 * 1024], BF16, sc_p)
        pt = ctx.psum("l1a_pt", [128, 1024], BF16, sc_p)
        pt2 = ctx.psum("l1a_pt2", [128, 1024], BF16, sc_p)
        s_l = [ctx.sem(scope=sc_p) for _ in range(2)]
        s_cst = ctx.sem(scope=sc_p)
        s_tr = ctx.sem(scope=sc_p)
        s_te = ctx.sem(scope=sc_p)
        s_out = [ctx.sem(scope=sc_p, kls="gdma") for _ in range(4)]

        def outs_done(e, nk):
            for sl in range(4):
                cnt = (nk + 3 - sl) // 4
                if cnt > 0:
                    s_out[sl].wait(e, 16 * cnt)

        @blk.sync
        def _(e):
            prime(e, s_l, s_cst)
            for i in range(128):
                if i >= 2:
                    s_cst.wait(e, i - 1)
                src = k_in if i % 2 == 0 else v_in
                ch = i // 2
                s_l[i % 2].inc(e.dma_start(
                    stg[:, (i % 2) * 1024:(i % 2 + 1) * 1024],
                    src.ap()[ch * 128:(ch + 1) * 128, :]), 16)

        def _T(j):
            if j < 0:
                return 0
            return 10 * (j // 2) + (9 if j % 2 == 0 else 10)

        @blk.vector
        def _(e):
            prime(e, s_l, s_cst, s_out, s_tr)
            for i in range(128):
                s_l[i % 2].wait(e, 16 * (i // 2 + 1))
                if i >= 2:
                    outs_done(e, _T(i - 2))
                    if i % 2 == 0:
                        s_tr.wait(e, i // 2)
                s_cst.inc(e.tensor_copy(
                    kbb[:, (i % 2) * 1024:(i % 2 + 1) * 1024],
                    stg[:, (i % 2) * 1024:(i % 2 + 1) * 1024]), 1)

        @blk.tensor
        def _(e):
            prime(e, s_cst, s_tr, s_te)
            for ch in range(64):
                buf = pt if ch % 2 == 0 else pt2
                s_cst.wait(e, 2 * ch + 1)
                if ch >= 2:
                    s_te.wait(e, ch - 1)
                for dc in range(8):
                    ins = e.transpose(buf[:, dc * 128:(dc + 1) * 128],
                                      kbb[:, dc * 128:(dc + 1) * 128],
                                      idn[:, :])
                s_tr.inc(ins, 1)

        @blk.scalar
        def _(e):
            prime(e, s_tr, s_te, s_out)
            for ch in range(64):
                buf = pt if ch % 2 == 0 else pt2
                s_tr.wait(e, ch + 1)
                if ch >= 2:
                    outs_done(e, _T(2 * ch - 4))
                s_te.inc(e.activation(ktst[:, (ch % 2) * 1024:(ch % 2 + 1) * 1024],
                                      buf[:, :], AF.Copy), 1)

        @blk.gpsimd
        def _(e):
            prime(e, s_out, s_cst, s_te)
            kdma = 0

            def issue(out_ap, in_ap):
                nonlocal kdma
                sl = kdma % 4
                if kdma >= 4:
                    s_out[sl].wait(e, 16 * (kdma // 4))
                s_out[sl].inc(e.dma_start(out_ap, in_ap), 16)
                kdma += 1

            for i in range(128):
                if i % 2 == 0:
                    ch = i // 2
                    s_cst.wait(e, i + 1)
                    issue(kb_h.ap()[ch * 128:(ch + 1) * 128, :], kbb[:, 0:1024])
                    s_te.wait(e, ch + 1)
                    for dc in range(8):
                        issue(kT_h.ap()[dc * 128:(dc + 1) * 128, ch * 128:(ch + 1) * 128],
                              ktst[:, (ch % 2) * 1024 + dc * 128:(ch % 2) * 1024 + (dc + 1) * 128])
                else:
                    ch = i // 2
                    s_cst.wait(e, i + 1)
                    issue(vb_h.ap()[ch * 128:(ch + 1) * 128, :], kbb[:, 1024:2048])
            for sl in range(4):
                s_out[sl].wait(e, 16 * 160)
    sc_p.close()

    sc_g = contextlib.ExitStack()
    with nc.Block() as blk:
        gstg = ctx.sbuf("l1a_gstg", [128, 4 * 1024], BF16, sc_g)
        s_l = [ctx.sem(scope=sc_g) for _ in range(4)]
        s_mm = ctx.sem(scope=sc_g)
        s_me = ctx.sem(scope=sc_g)
        pg = [ctx.psum(f"l1a_pg{o}", [128, 512], F32, sc_g) for o in range(8)]

        @blk.sync
        def _(e):
            prime(e, s_l, s_mm)
            for half in range(2):
                for ch in range(64):
                    i = half * 64 + ch
                    if i >= 4:
                        s_mm.wait(e, i - 3)
                    s_l[i % 4].inc(e.dma_start(
                        gstg[:, (i % 4) * 1024:(i % 4 + 1) * 1024],
                        kb_h.ap()[ch * 128:(ch + 1) * 128, :]), 16)

        @blk.tensor
        def _(e):
            prime(e, s_l, s_mm, s_me)
            for half in range(2):
                for ch in range(64):
                    i = half * 64 + ch
                    s_l[i % 4].wait(e, 16 * (i // 4 + 1))
                    if half == 1 and ch == 0:
                        s_me.wait(e, 8)
                    for dc in range(8):
                        ins = e.matmul(
                            pg[dc][:, :],
                            gstg[:, (i % 4) * 1024 + dc * 128:(i % 4) * 1024 + (dc + 1) * 128],
                            gstg[:, (i % 4) * 1024 + half * 512:(i % 4) * 1024 + half * 512 + 512],
                            start=(ch == 0), stop=(ch == 63))
                    s_mm.inc(ins, 1)

        @blk.vector
        def _(e):
            prime(e, s_mm, s_me)
            for half in range(2):
                s_mm.wait(e, 64 * (half + 1))
                for dc in range(8):
                    s_me.inc(e.tensor_copy(
                        mb[:, dc * 1024 + half * 512: dc * 1024 + half * 512 + 512],
                        pg[dc][:, :]), 1)

    sc_g.close()

    sc_b = contextlib.ExitStack()
    qT = ctx.sbuf("l1a_qT", [128, 8 * 512], BF16, sc_b)
    gT = ctx.sbuf("l1a_gT", [128, 64 * 512], BF16, sc_b)
    ktb = ctx.sbuf("l1a_ktb", [128, 2 * 8 * 512], BF16, sc_b)
    vbuf = ctx.sbuf("l1a_vbuf", [128, 4 * 1024], BF16, sc_b)
    outb = ctx.sbuf("l1a_outb", [128, 8 * 512], BF16, sc_b)

    for tb in range(8):
        bsc = contextlib.ExitStack()
        mqp = ctx.sbuf(f"l1a_mqp{tb}", [128, 2 * 512], BF16, bsc)
        sqb = ctx.sbuf(f"l1a_sqb{tb}", [128, 512], F32, bsc)
        cb = ctx.sbuf(f"l1a_cb{tb}", [128, 512], F32, bsc)
        ssb = sqb[0:1, 0:512]
        with nc.Block() as blk:
            pmq = ctx.psum(f"l1a_pmq_{tb}", [128, 512], F32, bsc)
            pmq2 = ctx.psum(f"l1a_pmq2_{tb}", [128, 512], F32, bsc)
            pss = ctx.psum(f"l1a_pss_{tb}", [1, 512], F32, bsc)
            pbc = ctx.psum(f"l1a_pbc_{tb}", [128, 512], F32, bsc)
            psc = [ctx.psum(f"l1a_psc{tb}_{o}", [128, 512], F32, bsc) for o in range(2)]
            s_kl = [ctx.sem(scope=bsc) for _ in range(2)]
            s_mq = ctx.sem(scope=bsc)
            s_mqv = ctx.sem(scope=bsc)
            s_ssm = ctx.sem(scope=bsc)
            s_cchain = ctx.sem(scope=bsc)
            s_qs = ctx.sem(scope=bsc)
            s_sc = ctx.sem(scope=bsc)
            s_ge = ctx.sem(scope=bsc)

            ssbb = mqp[0:1, 0:512]
            nev, s_qe = emit_ln_transpose(ctx, blk, f"l1a_q{tb}",
                                          list(range(4 * tb, 4 * tb + 4)),
                                          qT, 512, 0, scope=bsc)

            @blk.tensor
            def _(e):
                prime(e, s_mq, s_ssm, s_cchain, s_sc, s_kl, s_mqv, s_qs, s_ge)
                s_qe.wait(e, nev)
                for dc in range(8):
                    buf = pmq if dc % 2 == 0 else pmq2
                    if dc >= 2:
                        s_mqv.wait(e, dc - 1)
                    for dcp in range(8):
                        ins = e.matmul(
                            buf[:, :],
                            mb[:, dcp * 1024 + dc * 128: dcp * 1024 + (dc + 1) * 128],
                            qT[:, dcp * 512:(dcp + 1) * 512],
                            start=(dcp == 0), stop=(dcp == 7))
                    s_mq.inc(ins, 1)
                    if dc >= 1:
                        s_mqv.wait(e, dc)
                        s_ssm.inc(e.matmul(pss[:, :], ctx.cb["ones_col"][:, :],
                                           mqp[:, ((dc - 1) % 2) * 512:((dc - 1) % 2 + 1) * 512],
                                           start=(dc == 1), stop=False,
                                           skip_group_check=True), 1)
                s_mqv.wait(e, 8)
                s_ssm.inc(e.matmul(pss[:, :], ctx.cb["ones_col"][:, :],
                                   mqp[:, 1 * 512:2 * 512],
                                   start=False, stop=True,
                                   skip_group_check=True), 1)
                s_cchain.wait(e, 2)
                s_cchain.inc(e.matmul(pbc[:, :], ctx.cb["ones_row"][:, :],
                                      ssbb[:, :], start=True, stop=True), 1)
                s_qs.wait(e, 8)
                k = 0
                for sc in range(16):
                    s_kl[sc % 2].wait(e, 16 * 8 * (sc // 2 + 1))
                    for nt in range(4):
                        buf = psc[k % 2]
                        if k >= 2:
                            s_ge.wait(e, k - 1)
                        for dc in range(8):
                            ins = e.matmul(
                                buf[:, :],
                                ktb[:, (sc % 2) * 4096 + dc * 512 + nt * 128:
                                    (sc % 2) * 4096 + dc * 512 + (nt + 1) * 128],
                                qT[:, dc * 512:(dc + 1) * 512],
                                start=(dc == 0), stop=(dc == 7))
                        s_sc.inc(ins, 1)
                        k += 1

            @blk.vector
            def _(e):
                prime(e, s_mq, s_mqv, s_cchain, s_qs, s_ssm)
                for dc in range(8):
                    s_mq.wait(e, dc + 1)
                    if dc >= 2:
                        s_ssm.wait(e, dc - 1)
                    s_mqv.inc(e.tensor_mul(mqp[:, (dc % 2) * 512:(dc % 2 + 1) * 512],
                                           pmq[:, :] if dc % 2 == 0 else pmq2[:, :],
                                           qT[:, dc * 512:(dc + 1) * 512]), 1)
                s_ssm.wait(e, 8)
                s_cchain.inc(e.tensor_copy(ssb, pss[:, :]), 1)
                s_cchain.wait(e, 1)
                s_cchain.inc(e.tensor_copy(ssbb[:, :], ssb), 1)
                s_cchain.wait(e, 4)
                s_cchain.inc(e.reciprocal(cb[:, :], sqb[:, :]), 1)
                s_cchain.wait(e, 5)
                s_cchain.inc(e.tensor_scalar(cb[:, :], cb[:, :],
                                             90.50966799187809, None, OP.mult), 1)
                s_cchain.wait(e, 6)
                for dc in range(8):
                    s_qs.inc(e.tensor_tensor(qT[:, dc * 512:(dc + 1) * 512],
                                             qT[:, dc * 512:(dc + 1) * 512],
                                             cb[:, :], OP.mult), 1)

            @blk.scalar
            def _(e):
                prime(e, s_cchain, s_ge, s_sc)
                s_cchain.wait(e, 3)
                s_cchain.inc(e.activation(sqb[:, :], pbc[:, :], AF.Sqrt), 1)
                for k in range(64):
                    s_sc.wait(e, k + 1)
                    s_ge.inc(e.activation(gT[:, k * 512:(k + 1) * 512],
                                          psc[k % 2][:, :], AF.Gelu), 1)

            @blk.sync
            def _(e):
                prime(e, s_kl, s_sc)
                for sc in range(16):
                    if sc >= 2:
                        s_sc.wait(e, 4 * (sc - 1))
                    for dc in range(8):
                        s_kl[sc % 2].inc(e.dma_start(
                            ktb[:, (sc % 2) * 4096 + dc * 512:(sc % 2) * 4096 + (dc + 1) * 512],
                            kT_h.ap()[dc * 128:(dc + 1) * 128,
                                      sc * 512:(sc + 1) * 512]), 16)

        bsc.close()

        bsc2 = contextlib.ExitStack()
        with nc.Block() as blk:
            po = [ctx.psum(f"l1a_po{tb}_{o}", [128, 512], F32, bsc2) for o in range(8)]
            s_vl = [ctx.sem(scope=bsc2) for _ in range(2)]
            s_av = ctx.sem(scope=bsc2)
            s_oe = ctx.sem(scope=bsc2)

            @blk.sync
            def _(e):
                prime(e, s_vl, s_av)
                for grp in range(32):
                    if grp >= 2:
                        s_av.wait(e, 2 * (grp - 1))
                    for c4 in range(2):
                        ch = grp * 2 + c4
                        s_vl[grp % 2].inc(e.dma_start(
                            vbuf[:, ((grp % 2) * 2 + c4) * 1024:
                                 ((grp % 2) * 2 + c4 + 1) * 1024],
                            vb_h.ap()[ch * 128:(ch + 1) * 128, :]), 16)

            @blk.tensor
            def _(e):
                prime(e, s_vl, s_av)
                for grp in range(32):
                    s_vl[grp % 2].wait(e, 16 * 2 * (grp // 2 + 1))
                    for c4 in range(2):
                        ch = grp * 2 + c4
                        for oc in range(8):
                            ins = e.matmul(
                                po[oc][:, :],
                                vbuf[:, ((grp % 2) * 2 + c4) * 1024 + oc * 128:
                                     ((grp % 2) * 2 + c4) * 1024 + (oc + 1) * 128],
                                gT[:, ch * 512:(ch + 1) * 512],
                                start=(ch == 0), stop=(ch == 63))
                        s_av.inc(ins, 1)

            @blk.vector
            def _(e):
                prime(e, s_oe, s_av)
                s_av.wait(e, 64)
                for oc in range(8):
                    s_oe.inc(e.tensor_copy(outb[:, oc * 512:(oc + 1) * 512],
                                           po[oc][:, :]), 1)
        bsc2.close()

        emit_out_ln_add(ctx, f"l1a_o{tb}", outb, 512, tb * 512)
    sc_b.close()
    sc_all.close()


def emit_l0_attn(ctx):
    nc = ctx.nc
    idn = ctx.b["idn"]
    w_in = {w: nc.dram_tensor(f"l0a_{w}", [1024, 1024], F32, kind="ExternalInput")
            for w in ("wq", "wk", "wv", "wo")}
    qt_h = nc.dram_tensor("l0a_qt", [2, 16, 65, 2048], BF16)
    rs_h = nc.dram_tensor("l0a_rs", [2, 64, 512], BF16)
    kt_h = nc.dram_tensor("l0a_kt", [2, 16, 65, 2048], BF16)
    va_h = nc.dram_tensor("l0a_va", [2, 16, 128, 1040], BF16)
    VAR_F = 64.0 / 3969.0

    for sq in range(2):
        t0 = sq * 16
        sc_seq = contextlib.ExitStack()
        avT = ctx.sbuf(f"l0a_avT{sq}", [128, 8 * 2048], BF16, sc_seq)

        sc_x = contextlib.ExitStack()
        sc_ln = contextlib.ExitStack()
        xlnT = ctx.sbuf(f"l0a_xlnT{sq}", [128, NCH * 2048], BF16, sc_x)
        with nc.Block() as blk:
            nev, s_ln = emit_ln_transpose(ctx, blk, f"l0a_ln1_{sq}",
                                          list(range(t0, t0 + 16)),
                                          xlnT, 2048, 0, scope=sc_ln, sem_scope=sc_x)
        sc_ln.close()

        sc_w = contextlib.ExitStack()
        wb = ctx.sbuf(f"l0a_wb{sq}", [128, 8 * 1024], BF16, sc_w)
        wstg = ctx.sbuf(f"l0a_wstg{sq}", [128, 2 * 1024], F32, sc_w)
        ev1 = ctx.sbuf(f"l0a_ev1_{sq}", [128, 3 * 512], BF16, sc_w)
        ev2 = ctx.sbuf(f"l0a_ev2_{sq}", [128, 3 * 512], BF16, sc_w)
        sqv = ctx.sbuf(f"l0a_sqv{sq}", [128, 3 * 512], BF16, sc_w)
        vrow = ctx.sbuf(f"l0a_vrow{sq}", [2, 3 * 512], BF16, sc_w)

        for wi, wname in enumerate(("wq", "wk")):
            dst_h = qt_h if wname == "wq" else kt_h
            wsc = contextlib.ExitStack()
            with nc.Block() as blk:
                pj = [ctx.psum(f"l0a_pj{sq}{wi}{j}", [128, 512], F32, wsc) for j in range(3)]
                pc = [ctx.psum(f"l0a_pc{sq}{wi}{j}", [128, 512], F32, wsc) for j in range(3)]
                pv = ctx.psum(f"l0a_pv{sq}{wi}", [2, 512], F32, wsc)
                s_wl = [ctx.sem(scope=wsc) for _ in range(2)]
                s_wc = ctx.sem(scope=wsc)
                s_pj = ctx.sem(scope=wsc)
                s_e1 = ctx.sem(scope=wsc)
                s_cm = ctx.sem(scope=wsc)
                s_e2 = ctx.sem(scope=wsc)
                s_sq = ctx.sem(scope=wsc)
                s_iv = ctx.sem(scope=wsc)
                s_vr = ctx.sem(scope=wsc)
                s_od = [ctx.sem(scope=wsc, kls="gdma") for _ in range(2)]

                @blk.sync
                def _(e):
                    prime(e, s_wl, s_wc)
                    for ec in range(8):
                        if ec >= 2:
                            s_wc.wait(e, ec - 1)
                        s_wl[ec % 2].inc(e.dma_start(
                            wstg[:, (ec % 2) * 1024:(ec % 2 + 1) * 1024],
                            w_in[wname].ap()[ec * 128:(ec + 1) * 128, :]), 16)

                @blk.vector
                def _(e):
                    prime(e, s_wl, s_wc)
                    for ec in range(8):
                        s_wl[ec % 2].wait(e, 16 * (ec // 2 + 1))
                        s_wc.inc(e.tensor_copy(wb[:, ec * 1024:(ec + 1) * 1024],
                                               wstg[:, (ec % 2) * 1024:(ec % 2 + 1) * 1024]), 1)

                @blk.tensor
                def _(e):
                    prime(e, s_pj, s_cm, s_iv, s_wc, s_e1, s_e2, s_sq, s_vr, s_ln)
                    s_wc.wait(e, 8)
                    s_ln.wait(e, nev)
                    k = 0
                    for dc in range(8):
                        for tc in range(4):
                            if k >= 3:
                                s_e1.wait(e, k - 2)
                            for ec in range(8):
                                ins = e.matmul(
                                    pj[k % 3][:, :],
                                    wb[:, ec * 1024 + dc * 128: ec * 1024 + (dc + 1) * 128],
                                    xlnT[:, ec * 2048 + tc * 512: ec * 2048 + (tc + 1) * 512],
                                    start=(ec == 0), stop=(ec == 7))
                            s_pj.inc(ins, 1)
                            s_e1.wait(e, k + 1)
                            if k >= 3:
                                s_e2.wait(e, k - 2)
                            s_cm.inc(e.matmul(pc[k % 3][:, :], ctx.cb["cblk"][:, :],
                                              ev1[:, (k % 3) * 512:(k % 3 + 1) * 512],
                                              start=True, stop=True), 1)
                            s_sq.wait(e, k + 1)
                            if k >= 1:
                                s_vr.wait(e, k)
                            s_iv.inc(e.matmul(pv[:, :], ctx.cb["ind2"][:, :],
                                              sqv[:, (k % 3) * 512:(k % 3 + 1) * 512],
                                              start=True, stop=True), 1)
                            k += 1

                @blk.scalar
                def _(e):
                    prime(e, s_pj, s_e1, s_cm, s_e2, s_sq, s_iv, s_vr, *s_od)
                    for k in range(32):
                        s_pj.wait(e, k + 1)
                        if k >= 3:
                            s_cm.wait(e, k - 2)
                            s_sq.wait(e, k - 2)
                        if k >= 3:
                            s_od[(k - 3) % 2].wait(e, 16 * 4 * ((k - 3) // 2 + 1))
                        s_e1.inc(e.activation(ev1[:, (k % 3) * 512:(k % 3 + 1) * 512],
                                              pj[k % 3][:, :], AF.Copy), 1)
                        s_cm.wait(e, k + 1)
                        s_e2.inc(e.activation(ev2[:, (k % 3) * 512:(k % 3 + 1) * 512],
                                              pc[k % 3][:, :], AF.Copy), 1)
                        s_e2.wait(e, k + 1)
                        s_sq.inc(e.activation(sqv[:, (k % 3) * 512:(k % 3 + 1) * 512],
                                              ev2[:, (k % 3) * 512:(k % 3 + 1) * 512],
                                              AF.Square), 1)
                        s_iv.wait(e, k + 1)
                        s_vr.inc(e.activation(vrow[:, (k % 3) * 512:(k % 3 + 1) * 512],
                                              pv[:, :], AF.Identity,
                                              scale=(VAR_F if wname == "wq" else 1.0)), 1)

                @blk.gpsimd
                def _(e):
                    prime(e, s_od, s_e2, s_vr)
                    for k in range(32):
                        dc, tc = divmod(k, 4)
                        if k >= 2:
                            s_od[k % 2].wait(e, 16 * 4 * (k // 2))
                        s_e2.wait(e, k + 1)
                        for hh in range(2):
                            s_od[k % 2].inc(e.dma_start(
                                dst_h.ap()[sq, 2 * dc + hh, 0:64, tc * 512:(tc + 1) * 512],
                                ev2[(hh * 64):(hh * 64 + 64),
                                    (k % 3) * 512:(k % 3 + 1) * 512]), 16)
                        s_vr.wait(e, k + 1)
                        for hh in range(2):
                            s_od[k % 2].inc(e.dma_start(
                                dst_h.ap()[sq, 2 * dc + hh, 64:65, tc * 512:(tc + 1) * 512],
                                vrow[hh:hh + 1, (k % 3) * 512:(k % 3 + 1) * 512]), 16)
                    for sl in range(2):
                        s_od[sl].wait(e, 16 * 4 * 16)
            wsc.close()

        wsc = contextlib.ExitStack()
        with nc.Block() as blk:
            vstg = ctx.sbuf(f"l0a_vstg{sq}", [128, 1040], BF16, wsc)
            pj = [ctx.psum(f"l0a_pvj{sq}{j}", [128, 512], F32, wsc) for j in range(2)]
            s_wl = [ctx.sem(scope=wsc) for _ in range(2)]
            s_wc = ctx.sem(scope=wsc)
            s_pj = ctx.sem(scope=wsc)
            s_ev = ctx.sem(scope=wsc)
            s_od = ctx.sem(scope=wsc, kls="gdma")

            @blk.sync
            def _(e):
                prime(e, s_wl, s_wc)
                for ec in range(8):
                    if ec >= 2:
                        s_wc.wait(e, ec - 1)
                    s_wl[ec % 2].inc(e.dma_start(
                        wstg[:, (ec % 2) * 1024:(ec % 2 + 1) * 1024],
                        w_in["wv"].ap()[ec * 128:(ec + 1) * 128, :]), 16)

            @blk.vector
            def _(e):
                prime(e, s_wl, s_wc, s_ev, s_pj, s_od)
                for ec in range(8):
                    s_wl[ec % 2].wait(e, 16 * (ec // 2 + 1))
                    s_wc.inc(e.tensor_copy(wb[:, ec * 1024:(ec + 1) * 1024],
                                           wstg[:, (ec % 2) * 1024:(ec % 2 + 1) * 1024]), 1)
                s_ev.inc(e.memset(bass.AP(vstg, 64, [[1040, 128], [65, 16], [1, 1]]), 1.0), 1)
                for k in range(32):
                    nt, hf = divmod(k, 2)
                    s_pj.wait(e, k + 1)
                    if nt >= 1:
                        s_od.wait(e, 16 * nt)
                    s_ev.inc(e.tensor_copy(
                        bass.AP(vstg, hf * 8 * 65, [[1040, 128], [65, 8], [1, 64]]),
                        pj[k % 2][:, :]), 1)

            @blk.tensor
            def _(e):
                prime(e, s_pj, s_wc, s_ev, s_ln)
                s_wc.wait(e, 8)
                s_ln.wait(e, nev)
                k = 0
                for nt in range(16):
                    for hf in range(2):
                        if k >= 2:
                            s_ev.wait(e, k)
                        for ec in range(8):
                            ins = e.matmul(
                                pj[k % 2][:, :],
                                xlnT[:, ec * 2048 + nt * 128: ec * 2048 + (nt + 1) * 128],
                                wb[:, ec * 1024 + hf * 512: ec * 1024 + (hf + 1) * 512],
                                start=(ec == 0), stop=(ec == 7))
                        s_pj.inc(ins, 1)
                        k += 1

            @blk.gpsimd
            def _(e):
                prime(e, s_od, s_ev)
                for nt in range(16):
                    s_ev.wait(e, 2 * nt + 3)
                    s_od.inc(e.dma_start(va_h.ap()[sq, nt, :, :], vstg[:, :]), 16)
                s_od.wait(e, 16 * 16)
        wsc.close()
        sc_w.close()
        sc_x.close()

        sc_a = contextlib.ExitStack()
        rstage = ctx.sbuf(f"l0a_rstage{sq}", [1, 2 * 512], BF16, sc_a)
        qkt = ctx.sbuf(f"l0a_qkt{sq}", [65, 4 * 2048], BF16, sc_a)
        vab = ctx.sbuf(f"l0a_vab{sq}", [128, 2 * 1040], BF16, sc_a)
        ptb = ctx.sbuf(f"l0a_ptb{sq}", [128, 16 * 512], BF16, sc_a)
        with nc.Block() as blk:
            psc = [ctx.psum(f"l0a_psc{sq}{j}", [128, 1024], F32, sc_a) for j in range(2)]
            pav = ctx.psum(f"l0a_pav{sq}", [65, 512], F32, sc_a)
            s_hl = [ctx.sem(scope=sc_a) for _ in range(2)]
            s_sc = ctx.sem(scope=sc_a)
            s_ex = ctx.sem(scope=sc_a)
            s_av = ctx.sem(scope=sc_a)
            s_ae = ctx.sem(scope=sc_a)
            s_rst = ctx.sem(scope=sc_a)
            s_rod = [ctx.sem(scope=sc_a, kls="gdma") for _ in range(2)]

            @blk.sync
            def _(e):
                prime(e, s_hl, s_av)
                for h in range(16):
                    if h >= 2:
                        s_av.wait(e, 4 * (h - 1))
                    s_hl[h % 2].inc(e.dma_start(
                        qkt[:, (h % 2) * 2048:(h % 2 + 1) * 2048],
                        qt_h.ap()[sq, h, :, :]), 16)
                    s_hl[h % 2].inc(e.dma_start(
                        qkt[:, (2 + h % 2) * 2048:(3 + h % 2) * 2048],
                        kt_h.ap()[sq, h, :, :]), 16)
                    s_hl[h % 2].inc(e.dma_start(
                        vab[:, (h % 2) * 1040:(h % 2 + 1) * 1040],
                        bass.AP(va_h, (sq * 16 * 128 * 1040) + h * 65,
                                [[1040, 128], [128 * 1040, 16], [1, 65]])), 16)

            @blk.tensor
            def _(e):
                prime(e, s_sc, s_av, s_hl, s_ex, s_ae)
                for h in range(16):
                    s_hl[h % 2].wait(e, 48 * (h // 2 + 1))
                    for tc in range(4):
                        k = h * 4 + tc
                        for ng in range(8):
                            kk = k * 8 + ng
                            if kk >= 2:
                                s_ex.wait(e, kk - 1)
                            for j in range(2):
                                nt = 2 * ng + j
                                ins = e.matmul(
                                    psc[kk % 2][:, j * 512:(j + 1) * 512],
                                    qkt[:, (2 + h % 2) * 2048 + nt * 128:
                                        (2 + h % 2) * 2048 + (nt + 1) * 128],
                                    qkt[:, (h % 2) * 2048 + tc * 512:
                                        (h % 2) * 2048 + (tc + 1) * 512],
                                    start=True, stop=True)
                            s_sc.inc(ins, 1)
                        s_ae.wait(e, k)
                        for nt in range(16):
                            s_ex.wait(e, k * 8 + nt // 2 + 1)
                            ins = e.matmul(
                                pav[:, :],
                                vab[:, (h % 2) * 1040 + nt * 65:
                                    (h % 2) * 1040 + (nt + 1) * 65],
                                ptb[:, nt * 512:(nt + 1) * 512],
                                start=(nt == 0), stop=(nt == 15))
                        s_av.inc(ins, 1)

            @blk.scalar
            def _(e):
                prime(e, s_ex, s_sc)
                for kk in range(16 * 4 * 8):
                    s_sc.wait(e, kk + 1)
                    ng = kk % 8
                    s_ex.inc(e.activation(ptb[:, ng * 1024:(ng + 1) * 1024],
                                          psc[kk % 2][:, :], AF.Exp), 1)

            @blk.vector
            def _(e):
                prime(e, s_av, s_ae, s_rst, s_rod)
                for h in range(16):
                    for tc in range(4):
                        k = h * 4 + tc
                        s_av.wait(e, k + 1)
                        if k >= 2:
                            s_rod[k % 2].wait(e, 16 * (k // 2))
                        s_rst.inc(e.tensor_copy(
                            rstage[0:1, (k % 2) * 512:(k % 2 + 1) * 512],
                            pav[64:65, :]), 1)
                        s_ae.inc(e.tensor_copy(
                            avT[(h % 2) * 64:(h % 2) * 64 + 64,
                                (h // 2) * 2048 + tc * 512:(h // 2) * 2048 + (tc + 1) * 512],
                            pav[0:64, :]), 1)

            @blk.gpsimd
            def _(e):
                prime(e, s_rod, s_rst)
                for k in range(64):
                    s_rst.wait(e, k + 1)
                    s_rod[k % 2].inc(e.dma_start(
                        rs_h.ap()[sq, k:k + 1, :],
                        rstage[0:1, (k % 2) * 512:(k % 2 + 1) * 512]), 16)
                for sl in range(2):
                    s_rod[sl].wait(e, 16 * 32)
        sc_a.close()

        sc_p4 = contextlib.ExitStack()
        wob = ctx.sbuf(f"l0a_wob{sq}", [128, 8 * 1024], BF16, sc_p4)
        wstg4 = ctx.sbuf(f"l0a_wstg4{sq}", [128, 2 * 1024], F32, sc_p4)
        recb = ctx.sbuf(f"l0a_recb{sq}", [64, 512], BF16, sc_p4)
        recf = ctx.sbuf(f"l0a_recf{sq}", [64, 512], F32, sc_p4)
        rows = ctx.sbuf(f"l0a_rows{sq}", [64, 512], BF16, sc_p4)
        selb = ctx.sbuf(f"l0a_selb{sq}", [64, 32 * 128], BF16, sc_p4)
        outb = ctx.sbuf(f"l0a_outb{sq}", [128, 8 * 512], BF16, sc_p4)
        s_oev = ctx.sem(scope=sc_seq)
        sc_nrm = contextlib.ExitStack()
        with nc.Block() as blk:
            pbc = [ctx.psum(f"l0a_pbc{sq}{j}", [128, 512], F32, sc_nrm) for j in range(2)]
            s_wl = [ctx.sem(scope=sc_p4) for _ in range(2)]
            s_wc = ctx.sem(scope=sc_p4)
            s_rc = ctx.sem(scope=sc_p4)
            s_bc = ctx.sem(scope=sc_p4)
            s_nm = ctx.sem(scope=sc_p4)

            s_rl = ctx.sem()

            @blk.sync
            def _(e):
                prime(e, s_wl, s_wc, s_rl)
                s_rl.inc(e.dma_start(rows[:, :], rs_h.ap()[sq]), 16)
                s_rl.wait(e, 16)
                s_rl.inc(e.dma_start(selb[:, :], ctx.cdram["seln"].ap()), 16)
                for ec in range(8):
                    if ec >= 2:
                        s_wc.wait(e, ec - 1)
                    s_wl[ec % 2].inc(e.dma_start(
                        wstg4[:, (ec % 2) * 1024:(ec % 2 + 1) * 1024],
                        w_in["wo"].ap()[ec * 128:(ec + 1) * 128, :]), 16)

            @blk.vector
            def _(e):
                prime(e, s_wl, s_wc, s_rc, s_bc, s_nm)
                for ec in range(8):
                    s_wl[ec % 2].wait(e, 16 * (ec // 2 + 1))
                    s_wc.inc(e.tensor_copy(wob[:, ec * 1024:(ec + 1) * 1024],
                                           wstg4[:, (ec % 2) * 1024:(ec % 2 + 1) * 1024]), 1)
                s_rl.wait(e, 16)
                s_rc.inc(e.reciprocal(recf[:, :], rows[:, :]), 1)
                s_rc.wait(e, 1)
                s_rc.inc(e.tensor_copy(recb[:, :], recf[:, :]), 1)
                for j in range(32):
                    p, tc = divmod(j, 4)
                    s_bc.wait(e, j + 1)
                    s_nm.inc(e.tensor_tensor(
                        avT[:, p * 2048 + tc * 512: p * 2048 + (tc + 1) * 512],
                        avT[:, p * 2048 + tc * 512: p * 2048 + (tc + 1) * 512],
                        pbc[j % 2][:, :], OP.mult), 1)

            @blk.tensor
            def _(e):
                prime(e, s_rc, s_bc, s_nm)
                s_rc.wait(e, 2)
                s_rl.wait(e, 32)
                for j in range(32):
                    if j >= 2:
                        s_nm.wait(e, j - 1)
                    s_bc.inc(e.matmul(pbc[j % 2][:, :],
                                      selb[:, j * 128:(j + 1) * 128],
                                      recb[:, :],
                                      start=True, stop=True), 1)
        sc_nrm.close()
        for tc in range(4):
            wsc2 = contextlib.ExitStack()
            with nc.Block() as blk:
                pw = [ctx.psum(f"l0a_pw{sq}{tc}{j}", [128, 512], F32, wsc2) for j in range(8)]
                s_wm = ctx.sem(scope=wsc2)

                @blk.tensor
                def _(e):
                    prime(e, s_wm)
                    for eci in range(8):
                        for hdc in range(8):
                            ins = e.matmul(
                                pw[eci][:, :],
                                wob[:, hdc * 1024 + eci * 128: hdc * 1024 + (eci + 1) * 128],
                                avT[:, hdc * 2048 + tc * 512: hdc * 2048 + (tc + 1) * 512],
                                start=(hdc == 0), stop=(hdc == 7))
                        s_wm.inc(ins, 1)

                @blk.vector
                def _(e):
                    prime(e, s_oev, s_wm)
                    for eci in range(8):
                        s_wm.wait(e, eci + 1)
                        s_oev.inc(e.tensor_copy(outb[:, eci * 512:(eci + 1) * 512],
                                                pw[eci][:, :]), 1)
            wsc2.close()
            emit_out_ln_add(ctx, f"l0a_o{sq}_{tc}", outb, 512, sq * 2048 + tc * 512)
        sc_p4.close()
        sc_seq.close()


def emit_l0_ffn(ctx):
    nc = ctx.nc
    idn = ctx.b["idn"]
    rw_in = nc.dram_tensor("l0f_rw", [1024, 8], F32, kind="ExternalInput")
    wqkv_in = [nc.dram_tensor("l0f_mqkv", [1024, 1536], F32, kind="ExternalInput"),
               nc.dram_tensor("l0f_dqkv", [1024, 1536], F32, kind="ExternalInput")]
    wo_in = [nc.dram_tensor("l0f_mwo", [512, 1024], F32, kind="ExternalInput"),
             nc.dram_tensor("l0f_dwo", [512, 1024], F32, kind="ExternalInput")]
    qt_h = nc.dram_tensor("l0f_qt", [2, 2, 8, 64, 2048], BF16)
    kt_h = nc.dram_tensor("l0f_kt", [2, 2, 8, 64, 2048], BF16)
    va_h = nc.dram_tensor("l0f_va", [2, 2, 16, 128, 520], BF16)
    rs_h = nc.dram_tensor("l0f_rs", [2, 64, 512], BF16)
    gm_h = nc.dram_tensor("l0f_gm", [2, 2, 8, 2048], F32)

    for sq in range(2):
        t0 = sq * 16
        sc_seq = contextlib.ExitStack()
        avT = ctx.sbuf(f"l0f_avT{sq}", [128, 8 * 2048], BF16, sc_seq)
        cmt = ctx.sbuf(f"l0f_cmt{sq}", [128, 128], F32, sc_seq)

        sc_x = contextlib.ExitStack()
        sc_ln = contextlib.ExitStack()
        xlnT = ctx.sbuf(f"l0f_xlnT{sq}", [128, NCH * 2048], BF16, sc_x)
        with nc.Block() as blk:
            nev, s_ln = emit_ln_transpose(ctx, blk, f"l0f_ln1_{sq}",
                                          list(range(t0, t0 + 16)),
                                          xlnT, 2048, 0, scope=sc_ln, sem_scope=sc_x)
        sc_ln.close()

        sc_r = contextlib.ExitStack()
        with nc.Block() as blk:
            rsc = ctx.sbuf(f"l0f_rsc{sq}", [8, 2048], F32, sc_r)
            rwf = ctx.sbuf(f"l0f_rwf{sq}", [128, 64], F32, sc_r)
            rwb = ctx.sbuf(f"l0f_rwb{sq}", [128, 64], BF16, sc_r)
            bis = ctx.sbuf(f"l0f_bis{sq}", [8, 8], F32, sc_r)
            cmp = ctx.sbuf(f"l0f_cmp{sq}", [8, 2048], BF16, sc_r)
            sg = ctx.sbuf(f"l0f_sg{sq}", [8, 2048], F32, sc_r)
            msk = ctx.sbuf(f"l0f_msk{sq}", [8, 2048], F32, sc_r)
            gmb = ctx.sbuf(f"l0f_gmbf{sq}", [8, 2048], F32, sc_r)
            mskb = ctx.sbuf(f"l0f_mskb{sq}", [8, 2048], BF16, sc_r)
            prc = [ctx.psum(f"l0f_prc{sq}{j}", [8, 512], F32, sc_r) for j in range(2)]
            ptc = ctx.psum(f"l0f_ptc{sq}", [128, 128], BF16, sc_r)
            s_rl = ctx.sem()
            s_cst = ctx.sem(scope=sc_r)
            s_mm = ctx.sem(scope=sc_r)
            s_ev = ctx.sem(scope=sc_r)
            vf = ctx.sem(scope=sc_r)
            s_sg = ctx.sem(scope=sc_r)
            s_tm = ctx.sem(scope=sc_r)
            s_cme = ctx.sem(scope=sc_r)
            s_go = ctx.sem(scope=sc_r, kls="gdma")

            @blk.sync
            def _(e):
                prime(e, s_rl)
                for half in range(2):
                    s_rl.inc(e.dma_start(
                        rwf[:, half * 32:(half + 1) * 32],
                        bass.AP(rw_in, half * 4 * 128 * 8,
                                [[8, 128], [1024, 4], [1, 8]])), 16)

            @blk.vector
            def _(e):
                prime(e, vf, s_cst, s_mm, s_ev, s_sg, s_tm, s_go, s_cme)
                s_rl.wait(e, 32)
                s_cst.inc(e.tensor_copy(rwb[:, :], rwf[:, :]), 1)
                for tc in range(4):
                    s_mm.wait(e, tc + 1)
                    vf.inc(e.tensor_copy(rsc[:, tc * 512:(tc + 1) * 512],
                                         prc[tc % 2][:, :]), 1)
                v = 4
                vf.wait(e, v)
                vf.inc(e.memset(bis[:, 0:1], -16.0), 1); v += 1
                vf.wait(e, v)
                vf.inc(e.memset(bis[:, 1:2], 16.0), 1); v += 1
                for it in range(24):
                    vf.wait(e, v)
                    vf.inc(e.scalar_tensor_tensor(bis[:, 2:3], bis[:, 0:1], 1.0,
                                                  bis[:, 1:2], OP.bypass, OP.add), 1); v += 1
                    vf.wait(e, v)
                    vf.inc(e.tensor_scalar(bis[:, 2:3], bis[:, 2:3], 0.5, None,
                                           OP.mult), 1); v += 1
                    vf.wait(e, v)
                    vf.inc(e.tensor_scalar(cmp[:, :], rsc[:, :], bis[:, 2:3], None,
                                           OP.is_gt, OP.add, accum_out=bis[:, 3:4]), 1); v += 1
                    vf.wait(e, v)
                    vf.inc(e.tensor_scalar(bis[:, 4:5], bis[:, 3:4], 255.5, None,
                                           OP.is_gt), 1); v += 1
                    vf.wait(e, v)
                    vf.inc(e.tensor_sub(bis[:, 5:6], bis[:, 2:3], bis[:, 0:1]), 1); v += 1
                    vf.wait(e, v)
                    vf.inc(e.tensor_mul(bis[:, 6:7], bis[:, 5:6], bis[:, 4:5]), 1); v += 1
                    vf.wait(e, v)
                    vf.inc(e.tensor_add(bis[:, 0:1], bis[:, 0:1], bis[:, 6:7]), 1); v += 1
                    vf.wait(e, v)
                    vf.inc(e.tensor_sub(bis[:, 5:6], bis[:, 2:3], bis[:, 1:2]), 1); v += 1
                    vf.wait(e, v)
                    vf.inc(e.tensor_scalar(bis[:, 7:8], bis[:, 4:5], -1.0, 1.0,
                                           OP.mult, OP.add), 1); v += 1
                    vf.wait(e, v)
                    vf.inc(e.tensor_mul(bis[:, 6:7], bis[:, 5:6], bis[:, 7:8]), 1); v += 1
                    vf.wait(e, v)
                    vf.inc(e.tensor_add(bis[:, 1:2], bis[:, 1:2], bis[:, 6:7]), 1); v += 1
                vf.wait(e, v)
                vf.inc(e.tensor_scalar(msk[:, :], rsc[:, :], bis[:, 0:1], None,
                                       OP.is_gt), 1); v += 1
                vf.wait(e, v)
                vf.inc(e.tensor_copy(mskb[:, :], msk[:, :]), 1); v += 1
                s_sg.wait(e, 1)
                vf.wait(e, v)
                vf.inc(e.tensor_mul(gmb[:, :], sg[:, :], msk[:, :]), 1); v += 1
                s_tm.wait(e, 16)
                s_cme.inc(e.tensor_copy(cmt[:, :], ptc[:, :]), 1)

            @blk.scalar
            def _(e):
                prime(e, s_sg, vf)
                vf.wait(e, 4)
                s_sg.inc(e.activation(sg[:, :], rsc[:, :], AF.Sigmoid), 1)

            @blk.tensor
            def _(e):
                prime(e, s_mm, s_tm, s_cst, vf, s_ln, s_ev)
                s_ln.wait(e, nev)
                s_cst.wait(e, 1)
                for tc in range(4):
                    if tc >= 2:
                        vf.wait(e, tc - 1)
                    for ec in range(8):
                        ins = e.matmul(prc[tc % 2][:, :],
                                       rwb[:, ec * 8:(ec + 1) * 8],
                                       xlnT[:, ec * 2048 + tc * 512: ec * 2048 + (tc + 1) * 512],
                                       start=(ec == 0), stop=(ec == 7))
                    s_mm.inc(ins, 1)
                vf.wait(e, 6 + 24 * 11 + 2)
                for i in range(16):
                    ins = e.transpose(ptc[:, i * 8:(i + 1) * 8],
                                      mskb[0:8, i * 128:(i + 1) * 128], idn[0:8, 0:8])
                    s_tm.inc(ins, 1)

            @blk.gpsimd
            def _(e):
                prime(e, s_go, vf)
                vf.wait(e, 6 + 24 * 11 + 3)
                s_go.inc(e.dma_start(gm_h.ap()[sq, 0], gmb[:, :]), 16)
                s_go.wait(e, 16)
                s_go.inc(e.dma_start(gm_h.ap()[sq, 1], msk[:, :]), 16)
                s_go.wait(e, 32)
        sc_r.close()

        for br in range(2):
            fsc = contextlib.ExitStack()
            with nc.Block() as blk:
                wqb = ctx.sbuf(f"l0f_wqb{sq}{br}", [128, 8 * 1536], BF16, fsc)
                wstg = ctx.sbuf(f"l0f_wstg{sq}{br}", [128, 2 * 1536], F32, fsc)
                qkv = ctx.sbuf(f"l0f_qkv{sq}{br}", [128, 2 * 1536], BF16, fsc)
                rtmp = ctx.sbuf(f"l0f_rtmp{sq}{br}", [128, 4 * 128], BF16, fsc)
                cbuf = ctx.sbuf(f"l0f_cbuf{sq}{br}", [128, 32], F32, fsc)
                tstg = ctx.sbuf(f"l0f_tstg{sq}{br}", [128, 2 * 2 * 512], BF16, fsc)
                vstg = ctx.sbuf(f"l0f_vstg{sq}{br}", [128, 2 * 520], BF16, fsc)
                pj = [ctx.psum(f"l0f_pj{sq}{br}{j}", [128, 512], F32, fsc) for j in range(2)]
                ptr = [ctx.psum(f"l0f_ptr{sq}{br}{j}", [128, 512], BF16, fsc) for j in range(2)]
                s_wl = [ctx.sem(scope=fsc) for _ in range(2)]
                s_wc = ctx.sem(scope=fsc)
                s_pj = ctx.sem(scope=fsc)
                s_qe = ctx.sem(scope=fsc)
                s_rp = ctx.sem(scope=fsc)
                s_tr = ctx.sem(scope=fsc)
                s_te = ctx.sem(scope=fsc)
                s_va = ctx.sem(scope=fsc)
                s_od = [ctx.sem(scope=fsc, kls="gdma") for _ in range(2)]
                s_vo = [ctx.sem(scope=fsc, kls="gdma") for _ in range(2)]

                @blk.sync
                def _(e):
                    prime(e, s_wl, s_wc)
                    for ec in range(8):
                        if ec >= 2:
                            s_wc.wait(e, ec - 1)
                        s_wl[ec % 2].inc(e.dma_start(
                            wstg[:, (ec % 2) * 1536:(ec % 2 + 1) * 1536],
                            wqkv_in[br].ap()[ec * 128:(ec + 1) * 128, :]), 16)

                @blk.vector
                def _(e):
                    prime(e, s_wl, s_wc, s_pj, s_qe, s_rp, s_od, s_va, *s_vo)
                    for ec in range(8):
                        s_wl[ec % 2].wait(e, 16 * (ec // 2 + 1))
                        s_wc.inc(e.tensor_copy(wqb[:, ec * 1536:(ec + 1) * 1536],
                                               wstg[:, (ec % 2) * 1536:(ec % 2 + 1) * 1536]), 1)
                    for i in range(16):
                        sl = i % 2
                        base = sl * 1536
                        s_qe.wait(e, 3 * (i + 1))
                        s_rp.wait(e, 14 * i)
                        s_rp.inc(e.tensor_add(cbuf[:, 0:16],
                                              ctx.cb["cos_t"][:, i * 16:(i + 1) * 16],
                                              ctx.cb["cos_lo"][:, i * 16:(i + 1) * 16]), 1)
                        s_rp.wait(e, 14 * i + 1)
                        s_rp.inc(e.tensor_add(cbuf[:, 16:32],
                                              ctx.cb["sin_t"][:, i * 16:(i + 1) * 16],
                                              ctx.cb["sin_lo"][:, i * 16:(i + 1) * 16]), 1)
                        for qk in range(2):
                            off = base + qk * 64
                            cs = bass.AP(cbuf, 0, [[32, 128], [0, 8], [1, 16]])
                            sn = bass.AP(cbuf, 16, [[32, 128], [0, 8], [1, 16]])
                            x1 = bass.AP(qkv, off, [[3072, 128], [192, 8], [1, 16]])
                            x2 = bass.AP(qkv, off + 16, [[3072, 128], [192, 8], [1, 16]])
                            t1 = bass.AP(rtmp, 0, [[512, 128], [16, 8], [1, 16]])
                            t2 = bass.AP(rtmp, 128, [[512, 128], [16, 8], [1, 16]])
                            t3 = bass.AP(rtmp, 256, [[512, 128], [16, 8], [1, 16]])
                            t4 = bass.AP(rtmp, 384, [[512, 128], [16, 8], [1, 16]])
                            rb = 14 * i + 2 + 6 * qk
                            s_rp.wait(e, rb)
                            s_rp.inc(e.tensor_tensor(t1, x1, cs, OP.mult), 1)
                            s_rp.wait(e, rb + 1)
                            s_rp.inc(e.tensor_tensor(t2, x1, sn, OP.mult), 1)
                            s_rp.wait(e, rb + 2)
                            s_rp.inc(e.tensor_tensor(t3, x2, sn, OP.mult), 1)
                            s_rp.wait(e, rb + 3)
                            s_rp.inc(e.tensor_tensor(t4, x2, cs, OP.mult), 1)
                            s_rp.wait(e, rb + 4)
                            s_rp.inc(e.tensor_tensor(x1, t1, t3, OP.subtract), 1)
                            s_rp.wait(e, rb + 5)
                            s_rp.inc(e.tensor_tensor(x2, t2, t4, OP.add), 1)
                        if i >= 2:
                            s_vo[i % 2].wait(e, 16 * (i // 2))
                        for h in range(8):
                            vsl = bass.AP(vstg, (i % 2) * 520 + h * 65, [[1040, 128], [1, 64]])
                            vsrc = qkv[:, base + h * 192 + 128: base + h * 192 + 192]
                            if br == 0:
                                s_va.inc(e.tensor_scalar(vsl, vsrc,
                                                         cmt[:, i * 8 + h:i * 8 + h + 1],
                                                         None, OP.mult), 1)
                                s_va.inc(e.tensor_copy(
                                    bass.AP(vstg, (i % 2) * 520 + h * 65 + 64,
                                            [[1040, 128], [1, 1]]),
                                    cmt[:, i * 8 + h:i * 8 + h + 1]), 1)
                            else:
                                s_va.inc(e.tensor_copy(vsl, vsrc), 1)
                                s_va.inc(e.memset(
                                    bass.AP(vstg, (i % 2) * 520 + h * 65 + 64,
                                            [[1040, 128], [1, 1]]), 1.0), 1)

                @blk.tensor
                def _(e):
                    prime(e, s_pj, s_tr, s_wc, s_qe, s_rp, s_te, s_ln)
                    s_wc.wait(e, 8)
                    s_ln.wait(e, nev)

                    def transposes(i):
                        for qk in range(2):
                            kt = 2 * i + qk
                            buf = ptr[kt % 2]
                            if kt >= 2:
                                s_te.wait(e, kt - 1)
                            for hp in range(4):
                                for par in range(2):
                                    h = 2 * hp + par
                                    ins = e.transpose(
                                        buf[par * 64:par * 64 + 64, hp * 128:(hp + 1) * 128],
                                        qkv[:, (i % 2) * 1536 + h * 192 + qk * 64:
                                            (i % 2) * 1536 + h * 192 + qk * 64 + 64],
                                        idn[:, :])
                            s_tr.inc(ins, 1)

                    k = 0
                    for i in range(16):
                        for cc in range(3):
                            if k >= 2:
                                s_qe.wait(e, k - 1)
                            for ec in range(8):
                                ins = e.matmul(
                                    pj[k % 2][:, :],
                                    xlnT[:, ec * 2048 + i * 128: ec * 2048 + (i + 1) * 128],
                                    wqb[:, ec * 1536 + cc * 512: ec * 1536 + (cc + 1) * 512],
                                    start=(ec == 0), stop=(ec == 7))
                            s_pj.inc(ins, 1)
                            k += 1
                        if i >= 1:
                            s_rp.wait(e, 14 * i)
                            transposes(i - 1)
                    s_rp.wait(e, 14 * 16)
                    transposes(15)

                @blk.scalar
                def _(e):
                    prime(e, s_pj, s_qe, s_tr, s_te, s_od, s_rp, s_va)

                    def tstg_evac(kt):
                        s_tr.wait(e, kt + 1)
                        if kt >= 2:
                            s_od[kt % 2].wait(e, 16 * 4 * (kt // 2))
                        s_te.inc(e.activation(tstg[:, (kt % 2) * 512:(kt % 2 + 1) * 512],
                                              ptr[kt % 2][:, :], AF.Copy), 1)

                    for i in range(16):
                        for cc in range(3):
                            k = 3 * i + cc
                            s_pj.wait(e, k + 1)
                            if i >= 2 and cc == 0:
                                s_rp.wait(e, 14 * (i - 1))
                                s_va.wait(e, 16 * (i - 1))
                                s_tr.wait(e, 2 * (i - 1))
                            s_qe.inc(e.activation(
                                qkv[:, (i % 2) * 1536 + cc * 512:
                                    (i % 2) * 1536 + (cc + 1) * 512],
                                pj[k % 2][:, :], AF.Copy), 1)
                        if i >= 1:
                            tstg_evac(2 * (i - 1))
                            tstg_evac(2 * (i - 1) + 1)
                    tstg_evac(30)
                    tstg_evac(31)

                @blk.gpsimd
                def _(e):
                    prime(e, s_od, s_te, s_va, *s_vo)
                    for i in range(16):
                        for qk in range(2):
                            kt = 2 * i + qk
                            dst = qt_h if qk == 0 else kt_h
                            s_te.wait(e, kt + 1)
                            if kt >= 2:
                                s_od[kt % 2].wait(e, 16 * 4 * (kt // 2))
                            for hp in range(4):
                                s_od[kt % 2].inc(e.dma_start(
                                    bass.AP(dst, ((sq * 2 + br) * 8 + 2 * hp) * 64 * 2048 + i * 128,
                                            [[2048, 128], [1, 128]]),
                                    tstg[:, (kt % 2) * 512 + hp * 128:
                                         (kt % 2) * 512 + (hp + 1) * 128]), 16)
                        s_va.wait(e, 16 * (i + 1))
                        if i >= 2:
                            s_vo[i % 2].wait(e, 16 * (i // 2))
                        s_vo[i % 2].inc(e.dma_start(
                            va_h.ap()[sq, br, i],
                            vstg[:, (i % 2) * 520:(i % 2 + 1) * 520]), 16)
                    for sl in range(2):
                        s_od[sl].wait(e, 16 * 64)
                        s_vo[sl].wait(e, 16 * 8)
            fsc.close()

        sc_x.close()

        sc_a = contextlib.ExitStack()
        rstage = ctx.sbuf(f"l0f_rstage{sq}", [1, 2 * 512], BF16, sc_a)
        causb = ctx.sbuf(f"l0f_causb{sq}", [128, 4 * 512], BF16, sc_a)
        qs = ctx.sbuf(f"l0f_qs{sq}", [64, 2 * 2048], BF16, sc_a)
        ks = ctx.sbuf(f"l0f_ks{sq}", [64, 2 * 2048], BF16, sc_a)
        vab = ctx.sbuf(f"l0f_vab{sq}", [128, 2 * 1040], BF16, sc_a)
        ptb = ctx.sbuf(f"l0f_ptb{sq}", [128, 16 * 512], BF16, sc_a)
        with nc.Block() as blk:
            psc = [ctx.psum(f"l0f_psc{sq}{j}", [128, 1024], F32, sc_a) for j in range(2)]
            pav = ctx.psum(f"l0f_pav{sq}", [65, 512], F32, sc_a)
            s_hl = [ctx.sem(scope=sc_a) for _ in range(2)]
            s_sc = ctx.sem(scope=sc_a)
            s_ex = ctx.sem(scope=sc_a)
            s_mk = ctx.sem(scope=sc_a)
            s_av = ctx.sem(scope=sc_a)
            s_ae = ctx.sem(scope=sc_a)
            s_rst = ctx.sem(scope=sc_a)
            s_rod = [ctx.sem(scope=sc_a, kls="gdma") for _ in range(2)]

            def sc_count(hh):
                return 40 * hh

            s_cl = ctx.sem()

            @blk.sync
            def _(e):
                prime(e, s_hl, s_av, s_cl)
                for ci in range(4):
                    s_cl.inc(e.dma_start(causb[:, ci * 512:(ci + 1) * 512],
                                         ctx.cdram[f"caus{ci}"].ap()), 16)
                for hh in range(16):
                    br, h = divmod(hh, 8)
                    if hh >= 2:
                        s_av.wait(e, 4 * (hh - 1))
                    s_hl[hh % 2].inc(e.dma_start(
                        qs[:, (hh % 2) * 2048:(hh % 2 + 1) * 2048],
                        qt_h.ap()[sq, br, h]), 16)
                    s_hl[hh % 2].inc(e.dma_start(
                        ks[:, (hh % 2) * 2048:(hh % 2 + 1) * 2048],
                        kt_h.ap()[sq, br, h]), 16)
                    s_hl[hh % 2].inc(e.dma_start(
                        vab[:, (hh % 2) * 1040:(hh % 2 + 1) * 1040],
                        bass.AP(va_h, ((sq * 2 + br) * 16) * 128 * 520 + h * 65,
                                [[520, 128], [128 * 520, 16], [1, 65]])), 16)

            @blk.tensor
            def _(e):
                prime(e, s_sc, s_av, s_hl, s_ex, s_ae, s_mk)
                kk = 0
                for hh in range(16):
                    br, h = divmod(hh, 8)
                    s_hl[hh % 2].wait(e, 48 * (hh // 2 + 1))
                    for tc in range(4):
                        g = hh * 4 + tc
                        nts = 4 * tc + 4
                        for ng in range(nts // 2):
                            if kk >= 2:
                                s_ex.wait(e, kk - 1)
                            for j in range(2):
                                nt = 2 * ng + j
                                ins = e.matmul(
                                    psc[kk % 2][:, j * 512:(j + 1) * 512],
                                    ks[:, (hh % 2) * 2048 + nt * 128:
                                       (hh % 2) * 2048 + (nt + 1) * 128],
                                    qs[:, (hh % 2) * 2048 + tc * 512:
                                       (hh % 2) * 2048 + (tc + 1) * 512],
                                    start=True, stop=True)
                            s_sc.inc(ins, 1)
                            kk += 1
                        s_ae.wait(e, g)
                        kk0 = kk - nts // 2
                        for nt in range(nts):
                            if nt >= 4 * tc:
                                s_mk.wait(e, 4 * g + (nt - 4 * tc) + 1)
                            else:
                                s_ex.wait(e, kk0 + nt // 2 + 1)
                            ins = e.matmul(
                                pav[:, :],
                                vab[:, (hh % 2) * 1040 + nt * 65:
                                    (hh % 2) * 1040 + (nt + 1) * 65],
                                ptb[:, nt * 512:(nt + 1) * 512],
                                start=(nt == 0), stop=(nt == nts - 1))
                        s_av.inc(ins, 1)

            @blk.scalar
            def _(e):
                prime(e, s_ex, s_sc)
                kk = 0
                for hh in range(16):
                    for tc in range(4):
                        nts = 4 * tc + 4
                        for ng in range(nts // 2):
                            s_sc.wait(e, kk + 1)
                            s_ex.inc(e.activation(ptb[:, ng * 1024:(ng + 1) * 1024],
                                                  psc[kk % 2][:, :], AF.Exp,
                                                  scale=0.125), 1)
                            kk += 1

            @blk.vector
            def _(e):
                prime(e, s_av, s_ae, s_rst, s_rod, s_ex, s_mk)
                s_cl.wait(e, 64)
                kk = 0
                for hh in range(16):
                    for tc in range(4):
                        g = hh * 4 + tc
                        nts = 4 * tc + 4
                        kk += nts // 2
                        for dd in range(4):
                            nt = 4 * tc + dd
                            s_ex.wait(e, kk - nts // 2 + nt // 2 + 1)
                            s_mk.inc(e.tensor_tensor(
                                ptb[:, nt * 512:(nt + 1) * 512],
                                ptb[:, nt * 512:(nt + 1) * 512],
                                causb[:, dd * 512:(dd + 1) * 512], OP.mult), 1)
                        s_av.wait(e, g + 1)
                        if g >= 2:
                            s_rod[g % 2].wait(e, 16 * (g // 2))
                        s_rst.inc(e.tensor_copy(
                            rstage[0:1, (g % 2) * 512:(g % 2 + 1) * 512],
                            pav[64:65, :]), 1)
                        s_ae.inc(e.tensor_copy(
                            avT[(hh % 2) * 64:(hh % 2) * 64 + 64,
                                (hh // 2) * 2048 + tc * 512:(hh // 2) * 2048 + (tc + 1) * 512],
                            pav[0:64, :]), 1)

            @blk.gpsimd
            def _(e):
                prime(e, s_rod, s_rst)
                for g in range(64):
                    s_rst.wait(e, g + 1)
                    s_rod[g % 2].inc(e.dma_start(
                        rs_h.ap()[sq, g:g + 1, :],
                        rstage[0:1, (g % 2) * 512:(g % 2 + 1) * 512]), 16)
                for sl in range(2):
                    s_rod[sl].wait(e, 16 * 32)
        sc_a.close()

        sc_p4 = contextlib.ExitStack()
        wob = ctx.sbuf(f"l0f_wob{sq}", [128, 8 * 1024], BF16, sc_p4)
        wstg4 = ctx.sbuf(f"l0f_wstg4{sq}", [128, 2 * 1024], F32, sc_p4)
        rows = ctx.sbuf(f"l0f_rows{sq}", [64, 512], BF16, sc_p4)
        gm32 = ctx.sbuf(f"l0f_gm32{sq}", [32, 512], F32, sc_p4)
        mk32 = ctx.sbuf(f"l0f_mk32{sq}", [32, 512], F32, sc_p4)
        scl = ctx.sbuf(f"l0f_scl{sq}", [64, 512], F32, sc_p4)
        mrec = ctx.sbuf(f"l0f_mrec{sq}", [32, 512], F32, sc_p4)
        selb2 = ctx.sbuf(f"l0f_selb{sq}", [64, 32 * 128], BF16, sc_p4)
        sclb = ctx.sbuf(f"l0f_sclb{sq}", [64, 512], BF16, sc_p4)
        outb = ctx.sbuf(f"l0f_outb{sq}", [128, 8 * 512], BF16, sc_p4)
        s_oev = ctx.sem(scope=sc_seq)
        sc_nrm = contextlib.ExitStack()
        with nc.Block() as blk:
            pbc = [ctx.psum(f"l0f_pbc{sq}{j}", [128, 512], F32, sc_nrm) for j in range(2)]
            s_rl = ctx.sem()
            s_rc = ctx.sem(scope=sc_nrm)
            s_bc = ctx.sem(scope=sc_nrm)
            s_nm = ctx.sem(scope=sc_nrm)
            s_wl = [ctx.sem(scope=sc_p4) for _ in range(2)]
            s_wc = ctx.sem(scope=sc_p4)

            @blk.sync
            def _(e):
                prime(e, s_wl, s_wc, s_rl)
                s_rl.inc(e.dma_start(rows[:, :], rs_h.ap()[sq]), 16)
                s_rl.wait(e, 16)
                s_rl.inc(e.dma_start(gm32[:, :],
                                     bass.AP(gm_h, sq * 2 * 8 * 2048,
                                             [[2048, 8], [512, 4], [1, 512]])), 16)
                s_rl.wait(e, 32)
                s_rl.inc(e.dma_start(mk32[:, :],
                                     bass.AP(gm_h, sq * 2 * 8 * 2048 + 8 * 2048,
                                             [[2048, 8], [512, 4], [1, 512]])), 16)
                s_rl.wait(e, 48)
                s_rl.inc(e.dma_start(selb2[:, :], ctx.cdram["self"].ap()), 16)
                for br in range(2):
                    for ec4 in range(4):
                        ec = br * 4 + ec4
                        if ec >= 2:
                            s_wc.wait(e, ec - 1)
                        s_wl[ec % 2].inc(e.dma_start(
                            wstg4[:, (ec % 2) * 1024:(ec % 2 + 1) * 1024],
                            wo_in[br].ap()[ec4 * 128:(ec4 + 1) * 128, :]), 16)

            @blk.vector
            def _(e):
                prime(e, s_wl, s_wc, s_rc, s_bc, s_nm)
                for ec in range(8):
                    s_wl[ec % 2].wait(e, 16 * (ec // 2 + 1))
                    s_wc.inc(e.tensor_copy(wob[:, ec * 1024:(ec + 1) * 1024],
                                           wstg4[:, (ec % 2) * 1024:(ec % 2 + 1) * 1024]), 1)
                s_rl.wait(e, 48)
                v = 0
                s_rc.inc(e.tensor_scalar(scl[0:32, :], rows[0:32, :], 1.0, None,
                                         OP.add), 1); v += 1
                s_rc.wait(e, v)
                s_rc.inc(e.tensor_sub(scl[0:32, :], scl[0:32, :], mk32[:, :]), 1); v += 1
                s_rc.wait(e, v)
                s_rc.inc(e.reciprocal(mrec[:, :], scl[0:32, :]), 1); v += 1
                s_rc.wait(e, v)
                s_rc.inc(e.tensor_mul(scl[0:32, :], gm32[:, :], mrec[:, :]), 1); v += 1
                s_rc.wait(e, v)
                s_rc.inc(e.reciprocal(scl[32:64, :], rows[32:64, :]), 1); v += 1
                s_rc.wait(e, v)
                s_rc.inc(e.tensor_copy(sclb[:, :], scl[:, :]), 1); v += 1
                for j in range(32):
                    br, rem = divmod(j, 16)
                    p, tc = divmod(rem, 4)
                    s_bc.wait(e, j + 1)
                    s_nm.inc(e.tensor_tensor(
                        avT[:, (br * 4 + p) * 2048 + tc * 512:
                            (br * 4 + p) * 2048 + (tc + 1) * 512],
                        avT[:, (br * 4 + p) * 2048 + tc * 512:
                            (br * 4 + p) * 2048 + (tc + 1) * 512],
                        pbc[j % 2][:, :], OP.mult), 1)

            @blk.tensor
            def _(e):
                prime(e, s_rc, s_bc, s_nm)
                s_rc.wait(e, 6)
                s_rl.wait(e, 64)
                for j in range(32):
                    if j >= 2:
                        s_nm.wait(e, j - 1)
                    s_bc.inc(e.matmul(pbc[j % 2][:, :],
                                      selb2[:, j * 128:(j + 1) * 128],
                                      sclb[:, :], start=True, stop=True), 1)
        sc_nrm.close()
        for tc in range(4):
            wsc2 = contextlib.ExitStack()
            with nc.Block() as blk:
                pw = [ctx.psum(f"l0f_pw{sq}{tc}{j}", [128, 512], F32, wsc2) for j in range(8)]
                s_wm = ctx.sem(scope=wsc2)

                @blk.tensor
                def _(e):
                    prime(e, s_wm, s_wc)
                    s_wc.wait(e, 8)
                    for eci in range(8):
                        for hd8 in range(8):
                            br, hdc = divmod(hd8, 4)
                            ins = e.matmul(
                                pw[eci][:, :],
                                wob[:, (br * 4 + hdc) * 1024 + eci * 128:
                                    (br * 4 + hdc) * 1024 + (eci + 1) * 128],
                                avT[:, (br * 4 + hdc) * 2048 + tc * 512:
                                    (br * 4 + hdc) * 2048 + (tc + 1) * 512],
                                start=(hd8 == 0), stop=(hd8 == 7))
                        s_wm.inc(ins, 1)

                @blk.vector
                def _(e):
                    prime(e, s_oev, s_wm)
                    for eci in range(8):
                        s_wm.wait(e, eci + 1)
                        s_oev.inc(e.tensor_copy(outb[:, eci * 512:(eci + 1) * 512],
                                                pw[eci][:, :]), 1)
            wsc2.close()
            emit_out_ln_add(ctx, f"l0f_o{sq}_{tc}", outb, 512, sq * 2048 + tc * 512)
        sc_p4.close()
        sc_seq.close()


def build_program(upto="all", taps=()):
    stack = contextlib.ExitStack()
    nc = bass.Bass("TRN2", target_bir_lowering=False)
    ctx = Ctx(nc, stack)
    ctx.tap_names = tuple(taps)

    x_in = nc.dram_tensor("x", [T_TOK, E], F32, kind="ExternalInput")
    y_out = nc.dram_tensor("y", [T_TOK, E], F32, kind="ExternalOutput")
    cdefs = host_constants()
    cdram = {}
    for k, v in cdefs.items():
        dt = BF16 if v.dtype == BF else F32
        cdram[k] = nc.dram_tensor(k, list(v.shape), dt, kind="ExternalInput")

    ctx.cdram = cdram
    for k, v in cdefs.items():
        if k in ("seln", "self", "caus0", "caus1", "caus2", "caus3"):
            continue
        dt = BF16 if v.dtype == BF else F32
        ctx.cb[k] = ctx.sbuf("c_" + k, list(v.shape), dt)
    ctx.b["idn"] = ctx.cb["idn"]
    xres = ctx.sbuf("xres", [128, N_TILE * 1024], BF16)
    ctx.sbuf("g_lnscr", [128, 2 * 1024], BF16)
    sc_ph0 = contextlib.ExitStack()
    stage = ctx.sbuf("stage", [128, 4 * 1024], F32, sc_ph0)

    s_xd = [ctx.sem(f"xd{j}") for j in range(4)]
    s_xc = ctx.sem("xc")
    s_cd = ctx.sem("cd")
    ncst = len(cdefs)
    with nc.Block() as blk:
        @blk.sync
        def _(e):
            for i in range(N_TILE):
                if i >= 4:
                    s_xc.wait(e, i - 3)
                s_xd[i % 4].inc(e.dma_start(
                    stage[:, (i % 4) * 1024:(i % 4 + 1) * 1024],
                    x_in.ap()[i * 128:(i + 1) * 128, :]), 16)

        @blk.gpsimd
        def _(e):
            for k in cdefs:
                if k in ("seln", "self", "caus0", "caus1", "caus2", "caus3"):
                    continue
                s_cd.inc(e.dma_start(ctx.cb[k][:, :], cdram[k].ap()), 16)
            s_cd.wait(e, 16 * (ncst - 6))

        @blk.vector
        def _(e):
            for i in range(N_TILE):
                s_xd[i % 4].wait(e, 16 * (i // 4 + 1))
                s_xc.inc(e.tensor_copy(
                    tile(xres, i),
                    stage[:, (i % 4) * 1024:(i % 4 + 1) * 1024]), 1)

    sc_ph0.close()
    if upto != "copy":
        order = ["l0a", "l0f", "l1a", "l1f"]
        emits = {"l0a": emit_l0_attn, "l0f": emit_l0_ffn,
                 "l1a": emit_l1_attn, "l1f": emit_l1_ffn}
        sel = order if upto == "all" else upto.split(",")
        for s in sel:
            emits[s](ctx)

    s_oc = ctx.sem("oc")
    s_od = [ctx.sem("od0"), ctx.sem("od1")]
    sc_out = contextlib.ExitStack()
    stage = ctx.sbuf("ostage", [128, 2 * 1024], F32, sc_out)
    with nc.Block() as blk:
        @blk.scalar
        def _(e):
            for i in range(N_TILE):
                if i >= 2:
                    s_od[i % 2].wait(e, 16 * (i // 2))
                s_oc.inc(e.activation(stage[:, (i % 2) * 1024:(i % 2 + 1) * 1024],
                                      tile(xres, i), AF.Copy), 1)

        @blk.sync
        def _(e):
            for i in range(N_TILE):
                s_oc.wait(e, i + 1)
                s_od[i % 2].inc(e.dma_start(
                    y_out.ap()[i * 128:(i + 1) * 128, :],
                    stage[:, (i % 2) * 1024:(i % 2 + 1) * 1024]), 16)
            s_od[0].wait(e, 16 * 16)
            s_od[1].wait(e, 16 * 16)
    sc_out.close()

    nc._ctx_keepalive = stack
    return nc, ctx


def make_in_maps(inputs, upto="all"):
    cdefs = host_constants()
    x = np.ascontiguousarray(np.asarray(inputs["L0_attn_x"], dtype=np.float32))
    sel = ["l0a", "l0f", "l1a", "l1f"] if upto == "all" else upto.split(",")
    shared = dict(cdefs)
    if "l0a" in sel:
        shared["l0a_wq"] = np.ascontiguousarray(np.asarray(inputs["L0_attn_Wq"], dtype=np.float32))
        shared["l0a_wk"] = np.ascontiguousarray(np.asarray(inputs["L0_attn_Wk"], dtype=np.float32))
        shared["l0a_wv"] = np.ascontiguousarray(np.asarray(inputs["L0_attn_Wv"], dtype=np.float32))
        shared["l0a_wo"] = np.ascontiguousarray(np.asarray(inputs["L0_attn_Wo"], dtype=np.float32))
    if "l0f" in sel:
        shared["l0f_rw"] = np.ascontiguousarray(np.asarray(inputs["L0_ffn_mosa_router_w"], dtype=np.float32))
        shared["l0f_mqkv"] = np.ascontiguousarray(np.asarray(inputs["L0_ffn_mosa_wqkv"], dtype=np.float32).reshape(1024, 1536))
        shared["l0f_dqkv"] = np.ascontiguousarray(np.asarray(inputs["L0_ffn_dense_wqkv"], dtype=np.float32).reshape(1024, 1536))
        shared["l0f_mwo"] = np.ascontiguousarray(np.asarray(inputs["L0_ffn_mosa_wo"], dtype=np.float32))
        shared["l0f_dwo"] = np.ascontiguousarray(np.asarray(inputs["L0_ffn_dense_wo"], dtype=np.float32))
    if "l1a" in sel:
        shared["l1a_k"] = np.ascontiguousarray(np.asarray(inputs["L1_attn_k"], dtype=np.float32))
        shared["l1a_v"] = np.ascontiguousarray(np.asarray(inputs["L1_attn_v"], dtype=np.float32))
    if "l1f" in sel:
        UTr, Vr = host_l1f_factors(inputs["L1_ffn_G1_R"], inputs["L1_ffn_G2_R"])
        UTi, Vi = host_l1f_factors(inputs["L1_ffn_G1_I"], inputs["L1_ffn_G2_I"])
        shared["l1f_ut"] = np.ascontiguousarray(np.stack([UTr, UTi]))
        shared["l1f_v"] = np.ascontiguousarray(np.stack([Vr, Vi]))
        shared["l1f_br"] = np.ascontiguousarray(np.asarray(inputs["L1_ffn_bias_real"], dtype=np.float32))
        shared["l1f_bi"] = np.ascontiguousarray(np.asarray(inputs["L1_ffn_bias_imag"], dtype=np.float32))
    in_maps = []
    for c in range(8):
        m = {"x": np.ascontiguousarray(x[2 * c:2 * c + 2].reshape(T_TOK, E))}
        m.update(shared)
        in_maps.append(m)
    return in_maps


def run(inputs, upto="all", taps=()):
    nc, ctx = build_program(upto=upto, taps=taps)
    in_maps = make_in_maps(inputs, upto=upto)
    res = run_bass_kernel_spmd(nc, in_maps, core_ids=list(range(8)))
    return res


def kernel(**inputs):
    res = run(inputs, upto="all")
    out = np.concatenate([r["y"].reshape(2, S_LEN, E) for r in res.results], axis=0)
    return out.astype(np.float32)


# version 12
# speedup vs baseline: 1.0058x; 1.0015x over previous
import contextlib
import math
import numpy as np
import ml_dtypes
import concourse.bass as bass
import concourse.mybir as mybir
from concourse.bass_utils import run_bass_kernel_spmd

F32 = mybir.dt.float32
BF16 = mybir.dt.bfloat16
AF = mybir.ActivationFunctionType
OP = mybir.AluOpType
AX = mybir.AxisListType

T_TOK = 4096
S_LEN = 2048
N_TILE = 32
S_TILE = 16
E = 1024
NCH = 8
BF = ml_dtypes.bfloat16


def host_constants():
    c = {}
    c["idn"] = np.eye(128).astype(BF)
    C64 = (np.eye(64, dtype=np.float64) - 1.0 / 64.0) / 8.0
    Cb = np.zeros((128, 128), dtype=np.float64)
    Cb[:64, :64] = C64
    Cb[64:, 64:] = C64
    c["cblk"] = Cb.astype(BF)
    ind = np.zeros((128, 2), dtype=np.float32)
    ind[:64, 0] = 1.0
    ind[64:, 1] = 1.0
    c["ind2"] = ind.astype(BF)
    c["ones_row"] = np.ones((1, 128), dtype=np.float32).astype(BF)
    c["ones_col"] = np.ones((128, 1), dtype=np.float32).astype(BF)
    seln = np.zeros((64, 32 * 128), dtype=np.float32)
    for p in range(8):
        for tc in range(4):
            j = p * 4 + tc
            seln[(2 * p) * 4 + tc, j * 128:j * 128 + 64] = 1.0
            seln[(2 * p + 1) * 4 + tc, j * 128 + 64:j * 128 + 128] = 1.0
    c["seln"] = seln.astype(BF)
    self_ = np.zeros((64, 32 * 128), dtype=np.float32)
    for br in range(2):
        for p in range(4):
            for tc in range(4):
                j = br * 16 + p * 4 + tc
                self_[br * 32 + (2 * p) * 4 + tc, j * 128:j * 128 + 64] = 1.0
                self_[br * 32 + (2 * p + 1) * 4 + tc, j * 128 + 64:j * 128 + 128] = 1.0
    c["self"] = self_.astype(BF)
    half = 16
    inv = 10000.0 ** (-np.arange(half, dtype=np.float64) / half)
    pos = np.arange(S_LEN, dtype=np.float64)
    ang = pos[:, None] * inv[None, :]
    def _tiled(a):
        return np.ascontiguousarray(
            a.astype(np.float32).reshape(16, 128, 16).transpose(1, 0, 2).reshape(128, 256))
    cosv, sinv = np.cos(ang), np.sin(ang)
    chi, shi = cosv.astype(np.float32).astype(np.float64), sinv.astype(np.float32).astype(np.float64)
    c["cos_t"] = _tiled(cosv)
    c["sin_t"] = _tiled(sinv)
    c["cos_lo"] = _tiled(cosv - chi)
    c["sin_lo"] = _tiled(sinv - shi)
    for k, off in enumerate((0, 128, 256, 384)):
        n = np.arange(128)[:, None]
        t = np.arange(512)[None, :]
        c[f"caus{k}"] = (t >= n + off).astype(np.float32).astype(BF)
    return c


def host_l1f_factors(G1, G2):
    G1 = np.asarray(G1, dtype=np.float32)
    G2 = np.asarray(G2, dtype=np.float32)
    UT = np.zeros((4, 512, 512), dtype=np.float32)
    V = np.zeros((4, 512, 512), dtype=np.float32)
    b_i, d_i, g_i, e_i = np.arange(32), np.arange(32), np.arange(16), np.arange(16)
    for c in range(4):
        A = G1[0, :, c, :]
        B = G2[c, :, 0, :]
        bb, dd, gg = np.meshgrid(b_i, d_i, g_i, indexing="ij")
        UT[c, bb * 16 + gg, dd * 16 + gg] = A[bb, dd]
        bb, gg, ee = np.meshgrid(b_i, g_i, e_i, indexing="ij")
        V[c, bb * 16 + gg, bb * 16 + ee] = B[ee, gg]
    return UT, V


class PSem:
    def __init__(self, handle):
        self.h = handle
        self.base = 0
        self.planned = 0

    def wait(self, e, v):
        e.wait_ge(self.h, self.base + v)

    def inc(self, ins, n=1):
        ins.then_inc(self.h, n)
        self.planned += n


class Ctx:
    def __init__(self, nc, stack):
        self.nc = nc
        self.stack = stack
        self._i = 0
        self.b = {}
        self.cb = {}
        self.tap_names = ()
        self._sem_pool = {}

    def sem(self, name=None, scope=None, kls="c"):
        pool = self._sem_pool.setdefault(kls, [])
        if scope is not None and pool:
            ps = pool.pop()
        else:
            self._i += 1
            ps = PSem(self.stack.enter_context(
                self.nc.semaphore(name or f"s{self._i}")))
        if scope is not None:
            scope.callback(self._ret_sem, ps, kls)
        return ps

    def _ret_sem(self, ps, kls="c"):
        ps.base += ps.planned
        ps.planned = 0
        self._sem_pool.setdefault(kls, []).append(ps)

    def sbuf(self, name, shape, dtype, scope=None):
        h = (scope or self.stack).enter_context(self.nc.sbuf_tensor(name, shape, dtype))
        self.b[name] = h
        return h

    def psum(self, name, shape, dtype, scope=None):
        h = (scope or self.stack).enter_context(self.nc.psum_tensor(name, shape, dtype))
        self.b[name] = h
        return h

    def tap(self, name, ap, rows, cols):
        if name not in self.tap_names:
            return
        nc = self.nc
        td = nc.dram_tensor("tap_" + name, [rows, cols], F32, kind="ExternalOutput")
        sc = contextlib.ExitStack()
        stg = self.sbuf("tapstg_" + name, [rows, cols], F32, sc)
        s = self.sem(scope=sc)
        with nc.Block() as blk:
            @blk.vector
            def _(e):
                prime(e, s)
                s.inc(e.tensor_copy(stg[:, :], ap), 1)

            @blk.sync
            def _(e):
                s.wait(e, 1)
                s.inc(e.dma_start(td.ap(), stg[:, :]), 16)
                s.wait(e, 17)
        sc.close()


def tile(x, i, w=1024):
    return x[:, i * w:(i + 1) * w]


def prime(e, *sems):
    for ps in sems:
        for x in (ps if isinstance(ps, (list, tuple)) else [ps]):
            x.wait(e, 0)


def emit_ln(ctx, blk, name, src_f, dst_f, n, src_gate=None, apply_gate=None,
            done=None, scope=None):
    st = ctx.sbuf(f"lnst_{name}", [128, 4 * n], F32, scope)
    scr = ctx.b["g_lnscr"]
    m2 = ctx.sbuf(f"lnm2_{name}", [128, n], F32, scope)
    C_SUM, C_SS, C_R, C_B = 0, n, 2 * n, 3 * n
    sf = ctx.sem(scope=scope)
    vf = ctx.sem(scope=scope)

    @blk.scalar
    def _(e):
        prime(e, sf, vf, *( [done] if done is not None else [] ))
        for j in range(n):
            if src_gate is not None:
                g = src_gate(j)
                if g is not None:
                    g[0].wait(e, g[1])
            if j >= 2:
                sf.wait(e, j - 1)
            sf.inc(e.activation(scr[:, (j % 2) * 1024:(j % 2 + 1) * 1024],
                                src_f(j), AF.Square,
                                accum_out=st[:, C_SS + j:C_SS + j + 1]), 1)
        sf.wait(e, n)
        vf.wait(e, n + 4)
        sf.inc(e.activation(st[:, C_R:C_R + n], st[:, C_SS:C_SS + n], AF.Sqrt), 1)
        vf.wait(e, n + 6)
        for j in range(n):
            if apply_gate is not None:
                g = apply_gate(j)
                if g is not None:
                    g[0].wait(e, g[1])
            ins = e.activation(dst_f(j), src_f(j), AF.Identity,
                               scale=m2[:, j:j + 1],
                               bias=st[:, C_B + j:C_B + j + 1])
            if done is not None:
                done.inc(ins, 1)

    @blk.vector
    def _(e):
        prime(e, vf, sf)
        for j in range(n):
            if src_gate is not None:
                g = src_gate(j)
                if g is not None:
                    g[0].wait(e, g[1])
            vf.inc(e.reduce_sum(st[:, C_SUM + j:C_SUM + j + 1], src_f(j),
                                axis=AX.X), 1)
        vf.wait(e, n)
        vf.inc(e.tensor_scalar(st[:, C_SUM:C_SUM + n], st[:, C_SUM:C_SUM + n],
                               1.0 / 1024.0, None, OP.mult), 1)
        sf.wait(e, n)
        vf.wait(e, n + 1)
        vf.inc(e.tensor_mul(m2[:, 0:n], st[:, C_SUM:C_SUM + n],
                            st[:, C_SUM:C_SUM + n]), 1)
        vf.wait(e, n + 2)
        vf.inc(e.scalar_tensor_tensor(st[:, C_R:C_R + n], st[:, C_SS:C_SS + n],
                                      1.0 / 1024.0, m2[:, 0:n],
                                      OP.mult, OP.subtract), 1)
        vf.wait(e, n + 3)
        vf.inc(e.tensor_scalar(st[:, C_SS:C_SS + n], st[:, C_R:C_R + n],
                               1e-5, None, OP.add), 1)
        sf.wait(e, n + 1)
        vf.wait(e, n + 4)
        vf.inc(e.reciprocal(m2[:, 0:n], st[:, C_R:C_R + n]), 1)
        vf.wait(e, n + 5)
        vf.inc(e.scalar_tensor_tensor(st[:, C_B:C_B + n], st[:, C_SUM:C_SUM + n],
                                      -1.0, m2[:, 0:n],
                                      OP.mult, OP.mult), 1)


def emit_ln_transpose(ctx, blk, name, tiles, dst, tcols, coloff,
                      src_gate=None, scope=None, sem_scope=None):
    n = len(tiles)
    nring = min(8, ((n + 3) // 4) * 4)
    ring = ctx.sbuf(f"ring_{name}", [128, nring * 1024], BF16, scope)
    pa = ctx.psum(f"pta_{name}", [128, 512], BF16, scope)
    pb = ctx.psum(f"ptb_{name}", [128, 512], BF16, scope)
    idn = ctx.b["idn"]
    s_app = ctx.sem(scope=scope)
    s_tr = ctx.sem(scope=scope)
    s_ev = ctx.sem(scope=sem_scope or scope)
    nsup = (n + 3) // 4

    def ring_gate(j):
        sup_prev = j // 4 - (nring // 4)
        if sup_prev < 0:
            return None
        return (s_ev, NCH * (sup_prev + 1))

    emit_ln(ctx, blk, name, lambda j: tile(ctx.b["xres"], tiles[j]),
            lambda j: tile(ring, j % nring), n, src_gate=src_gate,
            apply_gate=ring_gate, done=s_app, scope=scope)

    @blk.tensor
    def _(e):
        prime(e, s_app, s_tr, s_ev)
        k = 0
        for sup in range(nsup):
            j0 = sup * 4
            cnt = min(4, n - j0)
            s_app.wait(e, j0 + cnt)
            for c in range(NCH):
                buf = pa if (k % 2 == 0) else pb
                if k >= 2:
                    s_ev.wait(e, k - 1)
                for j in range(cnt):
                    ins = e.transpose(buf[:, j * 128:(j + 1) * 128],
                                      tile(ring, (j0 + j) % nring)[:, c * 128:(c + 1) * 128],
                                      idn[:, :])
                s_tr.inc(ins, 1)
                k += 1

    @blk.vector
    def _(e):
        prime(e, s_tr, s_ev)
        k = 0
        for sup in range(nsup):
            j0 = sup * 4
            cnt = min(4, n - j0)
            for c in range(NCH):
                buf = pa if (k % 2 == 0) else pb
                s_tr.wait(e, k + 1)
                s_ev.inc(e.tensor_copy(
                    dst[:, c * tcols + coloff + j0 * 128:
                        c * tcols + coloff + j0 * 128 + cnt * 128],
                    buf[:, 0:cnt * 128]), 1)
                k += 1
    return NCH * nsup, s_ev


def emit_out_ln_add(ctx, name, src, tcols, tok0, src_ready=None):
    nc = ctx.nc
    n = tcols // 128
    nring = min(8, n)
    sc = contextlib.ExitStack()
    with nc.Block() as blk:
        ring = ctx.sbuf(f"oring_{name}", [128, nring * 1024], BF16, sc)
        lnout = ctx.sbuf(f"olnout_{name}", [128, 2 * 1024], BF16, sc)
        pa = ctx.psum(f"opta_{name}", [128, 512], BF16, sc)
        pb = ctx.psum(f"optb_{name}", [128, 512], BF16, sc)
        idn = ctx.b["idn"]
        s_tr = ctx.sem(scope=sc)
        s_ev = ctx.sem(scope=sc)
        s_app = ctx.sem(scope=sc)
        s_add = ctx.sem(scope=sc)

        @blk.tensor
        def _(e):
            prime(e, s_tr, s_ev, s_app, s_add)
            if src_ready is not None:
                src_ready[0].wait(e, src_ready[1])
            k = 0
            for j in range(n):
                for cp in range(2):
                    buf = pa if (k % 2 == 0) else pb
                    if k >= 2:
                        s_ev.wait(e, k - 1)
                    if j >= nring and cp == 0:
                        s_app.wait(e, j - nring + 1)
                    for cc in range(4):
                        c = cp * 4 + cc
                        ins = e.transpose(
                            buf[:, cc * 128:(cc + 1) * 128],
                            src[:, c * tcols + j * 128: c * tcols + (j + 1) * 128],
                            idn[:, :])
                    s_tr.inc(ins, 1)
                    k += 1

        @blk.vector
        def _(e):
            prime(e, s_tr, s_ev, s_app, s_add)
            for k in range(2 * n):
                j, cp = divmod(k, 2)
                buf = pa if (k % 2 == 0) else pb
                s_tr.wait(e, k + 1)
                s_ev.inc(e.tensor_copy(tile(ring, j % nring)[:, cp * 512:(cp + 1) * 512],
                                       buf[:, 0:512]), 1)

        def src_gate(j):
            return (s_ev, 2 * (j + 1))

        def apply_gate(j):
            if j < 2:
                return None
            return (s_add, j - 1)

        emit_ln(ctx, blk, f"ln2_{name}", lambda j: tile(ring, j % nring),
                lambda j: tile(lnout, j % 2), n,
                src_gate=src_gate, apply_gate=apply_gate, done=s_app, scope=sc)

        @blk.vector
        def _(e):
            prime(e, s_add)
            for j in range(n):
                s_app.wait(e, j + 1)
                s_add.inc(e.tensor_add(tile(ctx.b["xres"], tok0 // 128 + j),
                                       tile(ctx.b["xres"], tok0 // 128 + j),
                                       tile(lnout, j % 2)), 1)
    ctx.tap(f"lnst_{name}", ctx.b[f"lnst_ln2_{name}"][:, :], 128, 4 * n)
    ctx.tap(f"ring_{name}", ctx.b[f"oring_{name}"][:, 0:1024], 128, 1024)
    sc.close()


def emit_l1_ffn(ctx):
    nc = ctx.nc
    ut_in = nc.dram_tensor("l1f_ut", [2, 4, 512, 512], F32, kind="ExternalInput")
    v_in = nc.dram_tensor("l1f_v", [2, 4, 512, 512], F32, kind="ExternalInput")
    br_in = nc.dram_tensor("l1f_br", [512], F32, kind="ExternalInput")
    bi_in = nc.dram_tensor("l1f_bi", [512], F32, kind="ExternalInput")

    sc_all = contextlib.ExitStack()
    sc_w = contextlib.ExitStack()
    wT = ctx.sbuf("l1f_wT", [128, 12 * 512], BF16, sc_all)
    bias = ctx.sbuf("l1f_bias", [128, 8], F32, sc_all)
    s_db = ctx.sem(scope=sc_all)

    with nc.Block() as blk:
        uv = ctx.sbuf("l1f_uv", [128, 4 * 512], F32, sc_w)
        uvb = ctx.sbuf("l1f_uvb", [128, 64 * 512], BF16, sc_w)
        pw = ctx.psum("l1f_pw", [128, 512], F32, sc_w)
        pw2 = ctx.psum("l1f_pw2", [128, 512], F32, sc_w)
        s_d = [ctx.sem(scope=sc_w) for _ in range(4)]
        s_c = ctx.sem(scope=sc_w)
        s_m = ctx.sem(scope=sc_w)
        s_w = ctx.sem(scope=sc_w)

        def ucol(uvsel, ri, c, kch):
            return (uvsel * 32 + ri * 16 + c * 4 + kch) * 512

        @blk.sync
        def _(e):
            prime(e, s_d, s_db, s_c)
            for uvsel in range(2):
                src = ut_in if uvsel == 0 else v_in
                for ri in range(2):
                    for c in range(4):
                        for kch in range(4):
                            k = uvsel * 32 + ri * 16 + c * 4 + kch
                            if k >= 4:
                                s_c.wait(e, k - 3)
                            s_d[k % 4].inc(e.dma_start(
                                uv[:, (k % 4) * 512:(k % 4 + 1) * 512],
                                src.ap()[ri, c, kch * 128:(kch + 1) * 128, :]), 16)
            s_db.inc(e.dma_start(bias[:, 0:4],
                                 bass.AP(br_in, 0, [[1, 128], [128, 4]]),
                                 allow_slow_non_contiguous=True), 16)
            s_db.inc(e.dma_start(bias[:, 4:8],
                                 bass.AP(bi_in, 0, [[1, 128], [128, 4]]),
                                 allow_slow_non_contiguous=True), 16)

        @blk.vector
        def _(e):
            prime(e, s_d, s_c)
            for k in range(64):
                s_d[k % 4].wait(e, 16 * (k // 4 + 1))
                s_c.inc(e.tensor_copy(uvb[:, k * 512:(k + 1) * 512],
                                      uv[:, (k % 4) * 512:(k % 4 + 1) * 512]), 1)

        @blk.tensor
        def _(e):
            prime(e, s_c, s_m, s_w)
            s_c.wait(e, 64)
            k = 0
            for ri in range(2):
                for mq in range(4):
                    buf = pw if (k % 2 == 0) else pw2
                    if k >= 2:
                        s_w.wait(e, k - 1)
                    for c in range(4):
                        for kch in range(4):
                            ins = e.matmul(
                                buf[:, :],
                                uvb[:, ucol(0, ri, c, kch) + mq * 128:
                                    ucol(0, ri, c, kch) + (mq + 1) * 128],
                                uvb[:, ucol(1, ri, c, kch):
                                    ucol(1, ri, c, kch) + 512],
                                start=(c == 0 and kch == 0),
                                stop=(c == 3 and kch == 3))
                    s_m.inc(ins, 1)
                    k += 1

        @blk.scalar
        def _(e):
            prime(e, s_m, s_w)
            for k in range(8):
                buf = pw if (k % 2 == 0) else pw2
                s_m.wait(e, k + 1)
                s_w.inc(e.activation(wT[:, k * 512:(k + 1) * 512],
                                     buf[:, :], AF.Copy), 1)
            s_w.wait(e, 8)
            for mq in range(4):
                e.activation(wT[:, (8 + mq) * 512:(9 + mq) * 512],
                             wT[:, (4 + mq) * 512:(5 + mq) * 512],
                             AF.Copy, scale=-1.0)
    ctx.tap("uvb_ut00", uvb[:, 0:512], 128, 512)
    ctx.tap("uvb_v00", uvb[:, 32 * 512:33 * 512], 128, 512)
    ctx.tap("wTr0", wT[:, 0:512], 128, 512)
    ctx.tap("wTi0", wT[:, 4 * 512:5 * 512], 128, 512)
    sc_w.close()

    sc_x = contextlib.ExitStack()
    sc_ln = contextlib.ExitStack()
    xlnT = ctx.sbuf("l1f_xlnT", [128, NCH * T_TOK], BF16, sc_x)
    with nc.Block() as blk:
        nev, s_ln = emit_ln_transpose(ctx, blk, "l1f_ln1", list(range(N_TILE)),
                                      xlnT, T_TOK, 0, scope=sc_ln, sem_scope=sc_x)
    sc_ln.close()
    ctx.tap("xlnT_c0", xlnT[:, 0:512], 128, 512)
    ctx.tap("xlnT_c4", xlnT[:, 4 * T_TOK:4 * T_TOK + 512], 128, 512)

    for g in range(T_TOK // 512):
        gsc = contextlib.ExitStack()
        psc = contextlib.ExitStack()
        outbuf = ctx.sbuf(f"l1f_out{g}", [128, NCH * 512], BF16, gsc)
        s_mm = ctx.sem(scope=gsc)
        s_oev = ctx.sem(scope=gsc)
        with nc.Block() as blk:
            pts = [ctx.psum(f"l1f_pg{g}_{o}", [128, 512], F32, psc) for o in range(8)]

            @blk.tensor
            def _(e):
                prime(e, s_mm, s_ln)
                s_ln.wait(e, nev)
                for q in range(8):
                    ri, be = divmod(q, 4)
                    for part in range(2):
                        if ri == 0:
                            wreg = 0 if part == 0 else 8
                        else:
                            wreg = 4 if part == 0 else 0
                        for cq in range(4):
                            ins = e.matmul(
                                pts[q][:, :],
                                wT[:, (wreg + cq) * 512 + be * 128:
                                    (wreg + cq) * 512 + (be + 1) * 128],
                                xlnT[:, (part * 4 + cq) * T_TOK + g * 512:
                                     (part * 4 + cq) * T_TOK + (g + 1) * 512],
                                start=(part == 0 and cq == 0),
                                stop=(part == 1 and cq == 3))
                    s_mm.inc(ins, 1)

            @blk.scalar
            def _(e):
                prime(e, s_oev, s_mm)
                s_db.wait(e, 32)
                for q in range(8):
                    ri, be = divmod(q, 4)
                    s_mm.wait(e, q + 1)
                    s_oev.inc(e.activation(outbuf[:, q * 512:(q + 1) * 512],
                                           pts[q][:, :], AF.Identity,
                                           bias=bias[:, ri * 4 + be:ri * 4 + be + 1]), 1)

        psc.close()
        if g == 0:
            ctx.tap("outbuf0", outbuf[:, 0:8 * 512], 128, 8 * 512)
        emit_out_ln_add(ctx, f"l1f_g{g}", outbuf, 512, g * 512,
                        src_ready=(s_oev, 8))
        gsc.close()
    sc_x.close()
    sc_all.close()


def emit_l1_attn(ctx):
    nc = ctx.nc
    k_in = nc.dram_tensor("l1a_k", [8192, 1024], F32, kind="ExternalInput")
    v_in = nc.dram_tensor("l1a_v", [8192, 1024], F32, kind="ExternalInput")
    kT_h = nc.dram_tensor("l1a_kT", [1024, 8192], BF16)
    kb_h = nc.dram_tensor("l1a_kb", [8192, 1024], BF16)
    vb_h = nc.dram_tensor("l1a_vb", [8192, 1024], BF16)
    idn = ctx.b["idn"]

    sc_all = contextlib.ExitStack()
    mb = ctx.sbuf("l1a_mb", [128, 8 * 1024], BF16, sc_all)

    sc_p = contextlib.ExitStack()
    with nc.Block() as blk:
        stg = ctx.sbuf("l1a_stg", [128, 2 * 1024], F32, sc_p)
        kbb = ctx.sbuf("l1a_kbb", [128, 2 * 1024], BF16, sc_p)
        ktst = ctx.sbuf("l1a_ktst", [128, 2 * 1024], BF16, sc_p)
        pt = ctx.psum("l1a_pt", [128, 1024], BF16, sc_p)
        pt2 = ctx.psum("l1a_pt2", [128, 1024], BF16, sc_p)
        s_l = [ctx.sem(scope=sc_p) for _ in range(2)]
        s_cst = ctx.sem(scope=sc_p)
        s_tr = ctx.sem(scope=sc_p)
        s_te = ctx.sem(scope=sc_p)
        s_out = [ctx.sem(scope=sc_p, kls="gdma") for _ in range(4)]

        def outs_done(e, nk):
            for sl in range(4):
                cnt = (nk + 3 - sl) // 4
                if cnt > 0:
                    s_out[sl].wait(e, 16 * cnt)

        @blk.sync
        def _(e):
            prime(e, s_l, s_cst)
            for i in range(128):
                if i >= 2:
                    s_cst.wait(e, i - 1)
                src = k_in if i % 2 == 0 else v_in
                ch = i // 2
                s_l[i % 2].inc(e.dma_start(
                    stg[:, (i % 2) * 1024:(i % 2 + 1) * 1024],
                    src.ap()[ch * 128:(ch + 1) * 128, :]), 16)

        def _T(j):
            if j < 0:
                return 0
            return 10 * (j // 2) + (9 if j % 2 == 0 else 10)

        @blk.vector
        def _(e):
            prime(e, s_l, s_cst, s_out, s_tr)
            for i in range(128):
                s_l[i % 2].wait(e, 16 * (i // 2 + 1))
                if i >= 2:
                    outs_done(e, _T(i - 2))
                    if i % 2 == 0:
                        s_tr.wait(e, i // 2)
                s_cst.inc(e.tensor_copy(
                    kbb[:, (i % 2) * 1024:(i % 2 + 1) * 1024],
                    stg[:, (i % 2) * 1024:(i % 2 + 1) * 1024]), 1)

        @blk.tensor
        def _(e):
            prime(e, s_cst, s_tr, s_te)
            for ch in range(64):
                buf = pt if ch % 2 == 0 else pt2
                s_cst.wait(e, 2 * ch + 1)
                if ch >= 2:
                    s_te.wait(e, ch - 1)
                for dc in range(8):
                    ins = e.transpose(buf[:, dc * 128:(dc + 1) * 128],
                                      kbb[:, dc * 128:(dc + 1) * 128],
                                      idn[:, :])
                s_tr.inc(ins, 1)

        @blk.scalar
        def _(e):
            prime(e, s_tr, s_te, s_out)
            for ch in range(64):
                buf = pt if ch % 2 == 0 else pt2
                s_tr.wait(e, ch + 1)
                if ch >= 2:
                    outs_done(e, _T(2 * ch - 4))
                s_te.inc(e.activation(ktst[:, (ch % 2) * 1024:(ch % 2 + 1) * 1024],
                                      buf[:, :], AF.Copy), 1)

        @blk.gpsimd
        def _(e):
            prime(e, s_out, s_cst, s_te)
            kdma = 0

            def issue(out_ap, in_ap):
                nonlocal kdma
                sl = kdma % 4
                if kdma >= 4:
                    s_out[sl].wait(e, 16 * (kdma // 4))
                s_out[sl].inc(e.dma_start(out_ap, in_ap), 16)
                kdma += 1

            for i in range(128):
                if i % 2 == 0:
                    ch = i // 2
                    s_cst.wait(e, i + 1)
                    issue(kb_h.ap()[ch * 128:(ch + 1) * 128, :], kbb[:, 0:1024])
                    s_te.wait(e, ch + 1)
                    for dc in range(8):
                        issue(kT_h.ap()[dc * 128:(dc + 1) * 128, ch * 128:(ch + 1) * 128],
                              ktst[:, (ch % 2) * 1024 + dc * 128:(ch % 2) * 1024 + (dc + 1) * 128])
                else:
                    ch = i // 2
                    s_cst.wait(e, i + 1)
                    issue(vb_h.ap()[ch * 128:(ch + 1) * 128, :], kbb[:, 1024:2048])
            for sl in range(4):
                s_out[sl].wait(e, 16 * 160)
    sc_p.close()

    sc_g = contextlib.ExitStack()
    with nc.Block() as blk:
        gstg = ctx.sbuf("l1a_gstg", [128, 4 * 1024], BF16, sc_g)
        s_l = [ctx.sem(scope=sc_g) for _ in range(4)]
        s_mm = ctx.sem(scope=sc_g)
        s_me = ctx.sem(scope=sc_g)
        pg = [ctx.psum(f"l1a_pg{o}", [128, 512], F32, sc_g) for o in range(8)]

        @blk.sync
        def _(e):
            prime(e, s_l, s_mm)
            for half in range(2):
                for ch in range(64):
                    i = half * 64 + ch
                    if i >= 4:
                        s_mm.wait(e, i - 3)
                    s_l[i % 4].inc(e.dma_start(
                        gstg[:, (i % 4) * 1024:(i % 4 + 1) * 1024],
                        kb_h.ap()[ch * 128:(ch + 1) * 128, :]), 16)

        @blk.tensor
        def _(e):
            prime(e, s_l, s_mm, s_me)
            for half in range(2):
                for ch in range(64):
                    i = half * 64 + ch
                    s_l[i % 4].wait(e, 16 * (i // 4 + 1))
                    if half == 1 and ch == 0:
                        s_me.wait(e, 8)
                    for dc in range(8):
                        ins = e.matmul(
                            pg[dc][:, :],
                            gstg[:, (i % 4) * 1024 + dc * 128:(i % 4) * 1024 + (dc + 1) * 128],
                            gstg[:, (i % 4) * 1024 + half * 512:(i % 4) * 1024 + half * 512 + 512],
                            start=(ch == 0), stop=(ch == 63))
                    s_mm.inc(ins, 1)

        @blk.scalar
        def _(e):
            prime(e, s_mm, s_me)
            for half in range(2):
                s_mm.wait(e, 64 * (half + 1))
                for dc in range(8):
                    s_me.inc(e.activation(
                        mb[:, dc * 1024 + half * 512: dc * 1024 + half * 512 + 512],
                        pg[dc][:, :], AF.Copy), 1)

    sc_g.close()

    sc_b = contextlib.ExitStack()
    qT = ctx.sbuf("l1a_qT", [128, 8 * 512], BF16, sc_b)
    gT = ctx.sbuf("l1a_gT", [128, 64 * 512], BF16, sc_b)
    ktb = ctx.sbuf("l1a_ktb", [128, 2 * 8 * 512], BF16, sc_b)
    vbuf = ctx.sbuf("l1a_vbuf", [128, 4 * 1024], BF16, sc_b)
    outb = ctx.sbuf("l1a_outb", [128, 8 * 512], BF16, sc_b)

    for tb in range(8):
        bsc = contextlib.ExitStack()
        mqp = ctx.sbuf(f"l1a_mqp{tb}", [128, 2 * 512], BF16, bsc)
        sqb = ctx.sbuf(f"l1a_sqb{tb}", [128, 512], F32, bsc)
        cb = ctx.sbuf(f"l1a_cb{tb}", [128, 512], F32, bsc)
        ssb = sqb[0:1, 0:512]
        with nc.Block() as blk:
            pmq = ctx.psum(f"l1a_pmq_{tb}", [128, 512], F32, bsc)
            pmq2 = ctx.psum(f"l1a_pmq2_{tb}", [128, 512], F32, bsc)
            pss = ctx.psum(f"l1a_pss_{tb}", [1, 512], F32, bsc)
            pbc = ctx.psum(f"l1a_pbc_{tb}", [128, 512], F32, bsc)
            psc = [ctx.psum(f"l1a_psc{tb}_{o}", [128, 512], F32, bsc) for o in range(2)]
            s_kl = [ctx.sem(scope=bsc) for _ in range(2)]
            s_mq = ctx.sem(scope=bsc)
            s_mqv = ctx.sem(scope=bsc)
            s_ssm = ctx.sem(scope=bsc)
            s_cchain = ctx.sem(scope=bsc)
            s_qs = ctx.sem(scope=bsc)
            s_sc = ctx.sem(scope=bsc)
            s_ge = ctx.sem(scope=bsc)

            ssbb = mqp[0:1, 0:512]
            nev, s_qe = emit_ln_transpose(ctx, blk, f"l1a_q{tb}",
                                          list(range(4 * tb, 4 * tb + 4)),
                                          qT, 512, 0, scope=bsc)

            @blk.tensor
            def _(e):
                prime(e, s_mq, s_ssm, s_cchain, s_sc, s_kl, s_mqv, s_qs, s_ge)
                s_qe.wait(e, nev)
                for dc in range(8):
                    buf = pmq if dc % 2 == 0 else pmq2
                    if dc >= 2:
                        s_mqv.wait(e, dc - 1)
                    for dcp in range(8):
                        ins = e.matmul(
                            buf[:, :],
                            mb[:, dcp * 1024 + dc * 128: dcp * 1024 + (dc + 1) * 128],
                            qT[:, dcp * 512:(dcp + 1) * 512],
                            start=(dcp == 0), stop=(dcp == 7))
                    s_mq.inc(ins, 1)
                    if dc >= 1:
                        s_mqv.wait(e, dc)
                        s_ssm.inc(e.matmul(pss[:, :], ctx.cb["ones_col"][:, :],
                                           mqp[:, ((dc - 1) % 2) * 512:((dc - 1) % 2 + 1) * 512],
                                           start=(dc == 1), stop=False,
                                           skip_group_check=True), 1)
                s_mqv.wait(e, 8)
                s_ssm.inc(e.matmul(pss[:, :], ctx.cb["ones_col"][:, :],
                                   mqp[:, 1 * 512:2 * 512],
                                   start=False, stop=True,
                                   skip_group_check=True), 1)
                s_cchain.wait(e, 2)
                s_cchain.inc(e.matmul(pbc[:, :], ctx.cb["ones_row"][:, :],
                                      ssbb[:, :], start=True, stop=True), 1)
                s_qs.wait(e, 8)
                k = 0
                for sc in range(16):
                    s_kl[sc % 2].wait(e, 16 * 8 * (sc // 2 + 1))
                    for nt in range(4):
                        buf = psc[k % 2]
                        if k >= 2:
                            s_ge.wait(e, k - 1)
                        for dc in range(8):
                            ins = e.matmul(
                                buf[:, :],
                                ktb[:, (sc % 2) * 4096 + dc * 512 + nt * 128:
                                    (sc % 2) * 4096 + dc * 512 + (nt + 1) * 128],
                                qT[:, dc * 512:(dc + 1) * 512],
                                start=(dc == 0), stop=(dc == 7))
                        s_sc.inc(ins, 1)
                        k += 1

            @blk.vector
            def _(e):
                prime(e, s_mq, s_mqv, s_cchain, s_qs, s_ssm)
                for dc in range(8):
                    s_mq.wait(e, dc + 1)
                    if dc >= 2:
                        s_ssm.wait(e, dc - 1)
                    s_mqv.inc(e.tensor_mul(mqp[:, (dc % 2) * 512:(dc % 2 + 1) * 512],
                                           pmq[:, :] if dc % 2 == 0 else pmq2[:, :],
                                           qT[:, dc * 512:(dc + 1) * 512]), 1)
                s_ssm.wait(e, 8)
                s_cchain.inc(e.tensor_copy(ssb, pss[:, :]), 1)
                s_cchain.wait(e, 1)
                s_cchain.inc(e.tensor_copy(ssbb[:, :], ssb), 1)
                s_cchain.wait(e, 4)
                s_cchain.inc(e.reciprocal(cb[:, :], sqb[:, :]), 1)
                s_cchain.wait(e, 5)
                s_cchain.inc(e.tensor_scalar(cb[:, :], cb[:, :],
                                             90.50966799187809, None, OP.mult), 1)
                s_cchain.wait(e, 6)
                for dc in range(8):
                    s_qs.inc(e.tensor_tensor(qT[:, dc * 512:(dc + 1) * 512],
                                             qT[:, dc * 512:(dc + 1) * 512],
                                             cb[:, :], OP.mult), 1)

            @blk.scalar
            def _(e):
                prime(e, s_cchain, s_ge, s_sc)
                s_cchain.wait(e, 3)
                s_cchain.inc(e.activation(sqb[:, :], pbc[:, :], AF.Sqrt), 1)
                for k in range(64):
                    s_sc.wait(e, k + 1)
                    s_ge.inc(e.activation(gT[:, k * 512:(k + 1) * 512],
                                          psc[k % 2][:, :], AF.Gelu), 1)

            @blk.sync
            def _(e):
                prime(e, s_kl, s_sc)
                for sc in range(16):
                    if sc >= 2:
                        s_sc.wait(e, 4 * (sc - 1))
                    for dc in range(8):
                        s_kl[sc % 2].inc(e.dma_start(
                            ktb[:, (sc % 2) * 4096 + dc * 512:(sc % 2) * 4096 + (dc + 1) * 512],
                            kT_h.ap()[dc * 128:(dc + 1) * 128,
                                      sc * 512:(sc + 1) * 512]), 16)

        bsc.close()

        bsc2 = contextlib.ExitStack()
        with nc.Block() as blk:
            po = [ctx.psum(f"l1a_po{tb}_{o}", [128, 512], F32, bsc2) for o in range(8)]
            s_vl = [ctx.sem(scope=bsc2) for _ in range(2)]
            s_av = ctx.sem(scope=bsc2)
            s_oe = ctx.sem(scope=bsc2)

            @blk.sync
            def _(e):
                prime(e, s_vl, s_av)
                for grp in range(32):
                    if grp >= 2:
                        s_av.wait(e, 2 * (grp - 1))
                    for c4 in range(2):
                        ch = grp * 2 + c4
                        s_vl[grp % 2].inc(e.dma_start(
                            vbuf[:, ((grp % 2) * 2 + c4) * 1024:
                                 ((grp % 2) * 2 + c4 + 1) * 1024],
                            vb_h.ap()[ch * 128:(ch + 1) * 128, :]), 16)

            @blk.tensor
            def _(e):
                prime(e, s_vl, s_av)
                for grp in range(32):
                    s_vl[grp % 2].wait(e, 16 * 2 * (grp // 2 + 1))
                    for c4 in range(2):
                        ch = grp * 2 + c4
                        for oc in range(8):
                            ins = e.matmul(
                                po[oc][:, :],
                                vbuf[:, ((grp % 2) * 2 + c4) * 1024 + oc * 128:
                                     ((grp % 2) * 2 + c4) * 1024 + (oc + 1) * 128],
                                gT[:, ch * 512:(ch + 1) * 512],
                                start=(ch == 0), stop=(ch == 63))
                        s_av.inc(ins, 1)

            @blk.scalar
            def _(e):
                prime(e, s_oe, s_av)
                s_av.wait(e, 64)
                for oc in range(8):
                    s_oe.inc(e.activation(outb[:, oc * 512:(oc + 1) * 512],
                                          po[oc][:, :], AF.Copy), 1)
        bsc2.close()

        emit_out_ln_add(ctx, f"l1a_o{tb}", outb, 512, tb * 512)
    sc_b.close()
    sc_all.close()


def emit_l0_attn(ctx):
    nc = ctx.nc
    idn = ctx.b["idn"]
    w_in = {w: nc.dram_tensor(f"l0a_{w}", [1024, 1024], F32, kind="ExternalInput")
            for w in ("wq", "wk", "wv", "wo")}
    qt_h = nc.dram_tensor("l0a_qt", [2, 16, 65, 2048], BF16)
    rs_h = nc.dram_tensor("l0a_rs", [2, 64, 512], BF16)
    kt_h = nc.dram_tensor("l0a_kt", [2, 16, 65, 2048], BF16)
    va_h = nc.dram_tensor("l0a_va", [2, 16, 128, 1040], BF16)
    VAR_F = 64.0 / 3969.0

    for sq in range(2):
        t0 = sq * 16
        sc_seq = contextlib.ExitStack()
        avT = ctx.sbuf(f"l0a_avT{sq}", [128, 8 * 2048], BF16, sc_seq)

        sc_x = contextlib.ExitStack()
        sc_ln = contextlib.ExitStack()
        xlnT = ctx.sbuf(f"l0a_xlnT{sq}", [128, NCH * 2048], BF16, sc_x)
        with nc.Block() as blk:
            nev, s_ln = emit_ln_transpose(ctx, blk, f"l0a_ln1_{sq}",
                                          list(range(t0, t0 + 16)),
                                          xlnT, 2048, 0, scope=sc_ln, sem_scope=sc_x)
        sc_ln.close()

        sc_w = contextlib.ExitStack()
        wb = ctx.sbuf(f"l0a_wb{sq}", [128, 8 * 1024], BF16, sc_w)
        wstg = ctx.sbuf(f"l0a_wstg{sq}", [128, 2 * 1024], F32, sc_w)
        ev1 = ctx.sbuf(f"l0a_ev1_{sq}", [128, 3 * 512], BF16, sc_w)
        ev2 = ctx.sbuf(f"l0a_ev2_{sq}", [128, 3 * 512], BF16, sc_w)
        sqv = ctx.sbuf(f"l0a_sqv{sq}", [128, 3 * 512], BF16, sc_w)
        vrow = ctx.sbuf(f"l0a_vrow{sq}", [2, 3 * 512], BF16, sc_w)

        for wi, wname in enumerate(("wq", "wk")):
            dst_h = qt_h if wname == "wq" else kt_h
            wsc = contextlib.ExitStack()
            with nc.Block() as blk:
                pj = [ctx.psum(f"l0a_pj{sq}{wi}{j}", [128, 512], F32, wsc) for j in range(3)]
                pc = [ctx.psum(f"l0a_pc{sq}{wi}{j}", [128, 512], F32, wsc) for j in range(3)]
                pv = ctx.psum(f"l0a_pv{sq}{wi}", [2, 512], F32, wsc)
                s_wl = [ctx.sem(scope=wsc) for _ in range(2)]
                s_wc = ctx.sem(scope=wsc)
                s_pj = ctx.sem(scope=wsc)
                s_e1 = ctx.sem(scope=wsc)
                s_cm = ctx.sem(scope=wsc)
                s_e2 = ctx.sem(scope=wsc)
                s_sq = ctx.sem(scope=wsc)
                s_iv = ctx.sem(scope=wsc)
                s_vr = ctx.sem(scope=wsc)
                s_od = [ctx.sem(scope=wsc, kls="gdma") for _ in range(2)]

                @blk.sync
                def _(e):
                    prime(e, s_wl, s_wc)
                    for ec in range(8):
                        if ec >= 2:
                            s_wc.wait(e, ec - 1)
                        s_wl[ec % 2].inc(e.dma_start(
                            wstg[:, (ec % 2) * 1024:(ec % 2 + 1) * 1024],
                            w_in[wname].ap()[ec * 128:(ec + 1) * 128, :]), 16)

                @blk.vector
                def _(e):
                    prime(e, s_wl, s_wc)
                    for ec in range(8):
                        s_wl[ec % 2].wait(e, 16 * (ec // 2 + 1))
                        s_wc.inc(e.tensor_copy(wb[:, ec * 1024:(ec + 1) * 1024],
                                               wstg[:, (ec % 2) * 1024:(ec % 2 + 1) * 1024]), 1)

                @blk.tensor
                def _(e):
                    prime(e, s_pj, s_cm, s_iv, s_wc, s_e1, s_e2, s_sq, s_vr, s_ln)
                    s_wc.wait(e, 8)
                    s_ln.wait(e, nev)
                    k = 0
                    for dc in range(8):
                        for tc in range(4):
                            if k >= 3:
                                s_e1.wait(e, k - 2)
                            for ec in range(8):
                                ins = e.matmul(
                                    pj[k % 3][:, :],
                                    wb[:, ec * 1024 + dc * 128: ec * 1024 + (dc + 1) * 128],
                                    xlnT[:, ec * 2048 + tc * 512: ec * 2048 + (tc + 1) * 512],
                                    start=(ec == 0), stop=(ec == 7))
                            s_pj.inc(ins, 1)
                            s_e1.wait(e, k + 1)
                            if k >= 3:
                                s_e2.wait(e, k - 2)
                            s_cm.inc(e.matmul(pc[k % 3][:, :], ctx.cb["cblk"][:, :],
                                              ev1[:, (k % 3) * 512:(k % 3 + 1) * 512],
                                              start=True, stop=True), 1)
                            s_sq.wait(e, k + 1)
                            if k >= 1:
                                s_vr.wait(e, k)
                            s_iv.inc(e.matmul(pv[:, :], ctx.cb["ind2"][:, :],
                                              sqv[:, (k % 3) * 512:(k % 3 + 1) * 512],
                                              start=True, stop=True), 1)
                            k += 1

                @blk.scalar
                def _(e):
                    prime(e, s_pj, s_e1, s_cm, s_e2, s_sq, s_iv, s_vr, *s_od)
                    for k in range(32):
                        s_pj.wait(e, k + 1)
                        if k >= 3:
                            s_cm.wait(e, k - 2)
                            s_sq.wait(e, k - 2)
                        if k >= 3:
                            s_od[(k - 3) % 2].wait(e, 16 * 4 * ((k - 3) // 2 + 1))
                        s_e1.inc(e.activation(ev1[:, (k % 3) * 512:(k % 3 + 1) * 512],
                                              pj[k % 3][:, :], AF.Copy), 1)
                        s_cm.wait(e, k + 1)
                        s_e2.inc(e.activation(ev2[:, (k % 3) * 512:(k % 3 + 1) * 512],
                                              pc[k % 3][:, :], AF.Copy), 1)
                        s_e2.wait(e, k + 1)
                        s_sq.inc(e.activation(sqv[:, (k % 3) * 512:(k % 3 + 1) * 512],
                                              ev2[:, (k % 3) * 512:(k % 3 + 1) * 512],
                                              AF.Square), 1)
                        s_iv.wait(e, k + 1)
                        s_vr.inc(e.activation(vrow[:, (k % 3) * 512:(k % 3 + 1) * 512],
                                              pv[:, :], AF.Identity,
                                              scale=(VAR_F if wname == "wq" else 1.0)), 1)

                @blk.gpsimd
                def _(e):
                    prime(e, s_od, s_e2, s_vr)
                    for k in range(32):
                        dc, tc = divmod(k, 4)
                        if k >= 2:
                            s_od[k % 2].wait(e, 16 * 4 * (k // 2))
                        s_e2.wait(e, k + 1)
                        for hh in range(2):
                            s_od[k % 2].inc(e.dma_start(
                                dst_h.ap()[sq, 2 * dc + hh, 0:64, tc * 512:(tc + 1) * 512],
                                ev2[(hh * 64):(hh * 64 + 64),
                                    (k % 3) * 512:(k % 3 + 1) * 512]), 16)
                        s_vr.wait(e, k + 1)
                        for hh in range(2):
                            s_od[k % 2].inc(e.dma_start(
                                dst_h.ap()[sq, 2 * dc + hh, 64:65, tc * 512:(tc + 1) * 512],
                                vrow[hh:hh + 1, (k % 3) * 512:(k % 3 + 1) * 512]), 16)
                    for sl in range(2):
                        s_od[sl].wait(e, 16 * 4 * 16)
            wsc.close()

        wsc = contextlib.ExitStack()
        with nc.Block() as blk:
            vstg = ctx.sbuf(f"l0a_vstg{sq}", [128, 1040], BF16, wsc)
            pj = [ctx.psum(f"l0a_pvj{sq}{j}", [128, 512], F32, wsc) for j in range(2)]
            s_wl = [ctx.sem(scope=wsc) for _ in range(2)]
            s_wc = ctx.sem(scope=wsc)
            s_pj = ctx.sem(scope=wsc)
            s_ev = ctx.sem(scope=wsc)
            s_od = ctx.sem(scope=wsc, kls="gdma")

            @blk.sync
            def _(e):
                prime(e, s_wl, s_wc)
                for ec in range(8):
                    if ec >= 2:
                        s_wc.wait(e, ec - 1)
                    s_wl[ec % 2].inc(e.dma_start(
                        wstg[:, (ec % 2) * 1024:(ec % 2 + 1) * 1024],
                        w_in["wv"].ap()[ec * 128:(ec + 1) * 128, :]), 16)

            @blk.vector
            def _(e):
                prime(e, s_wl, s_wc, s_ev, s_pj, s_od)
                for ec in range(8):
                    s_wl[ec % 2].wait(e, 16 * (ec // 2 + 1))
                    s_wc.inc(e.tensor_copy(wb[:, ec * 1024:(ec + 1) * 1024],
                                           wstg[:, (ec % 2) * 1024:(ec % 2 + 1) * 1024]), 1)
                s_ev.inc(e.memset(bass.AP(vstg, 64, [[1040, 128], [65, 16], [1, 1]]), 1.0), 1)
                for k in range(32):
                    nt, hf = divmod(k, 2)
                    s_pj.wait(e, k + 1)
                    if nt >= 1:
                        s_od.wait(e, 16 * nt)
                    s_ev.inc(e.tensor_copy(
                        bass.AP(vstg, hf * 8 * 65, [[1040, 128], [65, 8], [1, 64]]),
                        pj[k % 2][:, :]), 1)

            @blk.tensor
            def _(e):
                prime(e, s_pj, s_wc, s_ev, s_ln)
                s_wc.wait(e, 8)
                s_ln.wait(e, nev)
                k = 0
                for nt in range(16):
                    for hf in range(2):
                        if k >= 2:
                            s_ev.wait(e, k)
                        for ec in range(8):
                            ins = e.matmul(
                                pj[k % 2][:, :],
                                xlnT[:, ec * 2048 + nt * 128: ec * 2048 + (nt + 1) * 128],
                                wb[:, ec * 1024 + hf * 512: ec * 1024 + (hf + 1) * 512],
                                start=(ec == 0), stop=(ec == 7))
                        s_pj.inc(ins, 1)
                        k += 1

            @blk.gpsimd
            def _(e):
                prime(e, s_od, s_ev)
                for nt in range(16):
                    s_ev.wait(e, 2 * nt + 3)
                    s_od.inc(e.dma_start(va_h.ap()[sq, nt, :, :], vstg[:, :]), 16)
                s_od.wait(e, 16 * 16)
        wsc.close()
        sc_w.close()
        sc_x.close()

        sc_a = contextlib.ExitStack()
        rstage = ctx.sbuf(f"l0a_rstage{sq}", [1, 2 * 512], BF16, sc_a)
        qkt = ctx.sbuf(f"l0a_qkt{sq}", [65, 6 * 2048], BF16, sc_a)
        vab = ctx.sbuf(f"l0a_vab{sq}", [128, 3 * 1040], BF16, sc_a)
        ptb = ctx.sbuf(f"l0a_ptb{sq}", [128, 16 * 512], BF16, sc_a)
        with nc.Block() as blk:
            psc = [ctx.psum(f"l0a_psc{sq}{j}", [128, 1024], F32, sc_a) for j in range(2)]
            pav = ctx.psum(f"l0a_pav{sq}", [65, 512], F32, sc_a)
            s_hl = [ctx.sem(scope=sc_a) for _ in range(3)]
            s_sc = ctx.sem(scope=sc_a)
            s_ex = ctx.sem(scope=sc_a)
            s_av = ctx.sem(scope=sc_a)
            s_ae = ctx.sem(scope=sc_a)
            s_rst = ctx.sem(scope=sc_a)
            s_rod = [ctx.sem(scope=sc_a, kls="gdma") for _ in range(2)]

            @blk.sync
            def _(e):
                prime(e, s_hl, s_av)
                for h in range(16):
                    if h >= 3:
                        s_av.wait(e, 4 * (h - 2))
                    s_hl[h % 3].inc(e.dma_start(
                        qkt[:, (h % 3) * 2048:(h % 3 + 1) * 2048],
                        qt_h.ap()[sq, h, :, :]), 16)
                    s_hl[h % 3].inc(e.dma_start(
                        qkt[:, (3 + h % 3) * 2048:(4 + h % 3) * 2048],
                        kt_h.ap()[sq, h, :, :]), 16)
                    s_hl[h % 3].inc(e.dma_start(
                        vab[:, (h % 3) * 1040:(h % 3 + 1) * 1040],
                        bass.AP(va_h, (sq * 16 * 128 * 1040) + h * 65,
                                [[1040, 128], [128 * 1040, 16], [1, 65]])), 16)

            @blk.tensor
            def _(e):
                prime(e, s_sc, s_av, s_hl, s_ex, s_ae)
                for h in range(16):
                    s_hl[h % 3].wait(e, 48 * (h // 3 + 1))
                    for tc in range(4):
                        k = h * 4 + tc
                        for ng in range(8):
                            kk = k * 8 + ng
                            if kk >= 2:
                                s_ex.wait(e, kk - 1)
                            for j in range(2):
                                nt = 2 * ng + j
                                ins = e.matmul(
                                    psc[kk % 2][:, j * 512:(j + 1) * 512],
                                    qkt[:, (3 + h % 3) * 2048 + nt * 128:
                                        (3 + h % 3) * 2048 + (nt + 1) * 128],
                                    qkt[:, (h % 3) * 2048 + tc * 512:
                                        (h % 3) * 2048 + (tc + 1) * 512],
                                    start=True, stop=True)
                            s_sc.inc(ins, 1)
                        s_ae.wait(e, k)
                        for nt in range(16):
                            s_ex.wait(e, k * 8 + nt // 2 + 1)
                            ins = e.matmul(
                                pav[:, :],
                                vab[:, (h % 3) * 1040 + nt * 65:
                                    (h % 3) * 1040 + (nt + 1) * 65],
                                ptb[:, nt * 512:(nt + 1) * 512],
                                start=(nt == 0), stop=(nt == 15))
                        s_av.inc(ins, 1)

            @blk.scalar
            def _(e):
                prime(e, s_ex, s_sc)
                for kk in range(16 * 4 * 8):
                    s_sc.wait(e, kk + 1)
                    ng = kk % 8
                    s_ex.inc(e.activation(ptb[:, ng * 1024:(ng + 1) * 1024],
                                          psc[kk % 2][:, :], AF.Exp), 1)

            @blk.vector
            def _(e):
                prime(e, s_av, s_ae, s_rst, s_rod)
                for h in range(16):
                    for tc in range(4):
                        k = h * 4 + tc
                        s_av.wait(e, k + 1)
                        if k >= 2:
                            s_rod[k % 2].wait(e, 16 * (k // 2))
                        s_rst.inc(e.tensor_copy(
                            rstage[0:1, (k % 2) * 512:(k % 2 + 1) * 512],
                            pav[64:65, :]), 1)
                        s_ae.inc(e.tensor_copy(
                            avT[(h % 2) * 64:(h % 2) * 64 + 64,
                                (h // 2) * 2048 + tc * 512:(h // 2) * 2048 + (tc + 1) * 512],
                            pav[0:64, :]), 1)

            @blk.gpsimd
            def _(e):
                prime(e, s_rod, s_rst)
                for k in range(64):
                    s_rst.wait(e, k + 1)
                    s_rod[k % 2].inc(e.dma_start(
                        rs_h.ap()[sq, k:k + 1, :],
                        rstage[0:1, (k % 2) * 512:(k % 2 + 1) * 512]), 16)
                for sl in range(2):
                    s_rod[sl].wait(e, 16 * 32)
        sc_a.close()

        sc_p4 = contextlib.ExitStack()
        wob = ctx.sbuf(f"l0a_wob{sq}", [128, 8 * 1024], BF16, sc_p4)
        wstg4 = ctx.sbuf(f"l0a_wstg4{sq}", [128, 2 * 1024], F32, sc_p4)
        recb = ctx.sbuf(f"l0a_recb{sq}", [64, 512], BF16, sc_p4)
        recf = ctx.sbuf(f"l0a_recf{sq}", [64, 512], F32, sc_p4)
        rows = ctx.sbuf(f"l0a_rows{sq}", [64, 512], BF16, sc_p4)
        selb = ctx.sbuf(f"l0a_selb{sq}", [64, 32 * 128], BF16, sc_p4)
        outb = ctx.sbuf(f"l0a_outb{sq}", [128, 8 * 512], BF16, sc_p4)
        s_oev = ctx.sem(scope=sc_seq)
        sc_nrm = contextlib.ExitStack()
        with nc.Block() as blk:
            pbc = [ctx.psum(f"l0a_pbc{sq}{j}", [128, 512], F32, sc_nrm) for j in range(2)]
            s_wl = [ctx.sem(scope=sc_p4) for _ in range(2)]
            s_wc = ctx.sem(scope=sc_p4)
            s_rc = ctx.sem(scope=sc_p4)
            s_bc = ctx.sem(scope=sc_p4)
            s_nm = ctx.sem(scope=sc_p4)

            s_rl = ctx.sem()

            @blk.sync
            def _(e):
                prime(e, s_wl, s_wc, s_rl)
                s_rl.inc(e.dma_start(rows[:, :], rs_h.ap()[sq]), 16)
                s_rl.wait(e, 16)
                s_rl.inc(e.dma_start(selb[:, :], ctx.cdram["seln"].ap()), 16)
                for ec in range(8):
                    if ec >= 2:
                        s_wc.wait(e, ec - 1)
                    s_wl[ec % 2].inc(e.dma_start(
                        wstg4[:, (ec % 2) * 1024:(ec % 2 + 1) * 1024],
                        w_in["wo"].ap()[ec * 128:(ec + 1) * 128, :]), 16)

            @blk.vector
            def _(e):
                prime(e, s_wl, s_wc, s_rc, s_bc, s_nm)
                for ec in range(8):
                    s_wl[ec % 2].wait(e, 16 * (ec // 2 + 1))
                    s_wc.inc(e.tensor_copy(wob[:, ec * 1024:(ec + 1) * 1024],
                                           wstg4[:, (ec % 2) * 1024:(ec % 2 + 1) * 1024]), 1)
                s_rl.wait(e, 16)
                s_rc.inc(e.reciprocal(recf[:, :], rows[:, :]), 1)
                s_rc.wait(e, 1)
                s_rc.inc(e.tensor_copy(recb[:, :], recf[:, :]), 1)
                for j in range(32):
                    p, tc = divmod(j, 4)
                    s_bc.wait(e, j + 1)
                    s_nm.inc(e.tensor_tensor(
                        avT[:, p * 2048 + tc * 512: p * 2048 + (tc + 1) * 512],
                        avT[:, p * 2048 + tc * 512: p * 2048 + (tc + 1) * 512],
                        pbc[j % 2][:, :], OP.mult), 1)

            @blk.tensor
            def _(e):
                prime(e, s_rc, s_bc, s_nm)
                s_rc.wait(e, 2)
                s_rl.wait(e, 32)
                for j in range(32):
                    if j >= 2:
                        s_nm.wait(e, j - 1)
                    s_bc.inc(e.matmul(pbc[j % 2][:, :],
                                      selb[:, j * 128:(j + 1) * 128],
                                      recb[:, :],
                                      start=True, stop=True), 1)
        sc_nrm.close()
        for tc in range(4):
            wsc2 = contextlib.ExitStack()
            with nc.Block() as blk:
                pw = [ctx.psum(f"l0a_pw{sq}{tc}{j}", [128, 512], F32, wsc2) for j in range(8)]
                s_wm = ctx.sem(scope=wsc2)

                @blk.tensor
                def _(e):
                    prime(e, s_wm)
                    for eci in range(8):
                        for hdc in range(8):
                            ins = e.matmul(
                                pw[eci][:, :],
                                wob[:, hdc * 1024 + eci * 128: hdc * 1024 + (eci + 1) * 128],
                                avT[:, hdc * 2048 + tc * 512: hdc * 2048 + (tc + 1) * 512],
                                start=(hdc == 0), stop=(hdc == 7))
                        s_wm.inc(ins, 1)

                @blk.scalar
                def _(e):
                    prime(e, s_oev, s_wm)
                    for eci in range(8):
                        s_wm.wait(e, eci + 1)
                        s_oev.inc(e.activation(outb[:, eci * 512:(eci + 1) * 512],
                                               pw[eci][:, :], AF.Copy), 1)
            wsc2.close()
            emit_out_ln_add(ctx, f"l0a_o{sq}_{tc}", outb, 512, sq * 2048 + tc * 512)
        sc_p4.close()
        sc_seq.close()


def emit_l0_ffn(ctx):
    nc = ctx.nc
    idn = ctx.b["idn"]
    rw_in = nc.dram_tensor("l0f_rw", [1024, 8], F32, kind="ExternalInput")
    wqkv_in = [nc.dram_tensor("l0f_mqkv", [1024, 1536], F32, kind="ExternalInput"),
               nc.dram_tensor("l0f_dqkv", [1024, 1536], F32, kind="ExternalInput")]
    wo_in = [nc.dram_tensor("l0f_mwo", [512, 1024], F32, kind="ExternalInput"),
             nc.dram_tensor("l0f_dwo", [512, 1024], F32, kind="ExternalInput")]
    qt_h = nc.dram_tensor("l0f_qt", [2, 2, 8, 64, 2048], BF16)
    kt_h = nc.dram_tensor("l0f_kt", [2, 2, 8, 64, 2048], BF16)
    va_h = nc.dram_tensor("l0f_va", [2, 2, 16, 128, 520], BF16)
    rs_h = nc.dram_tensor("l0f_rs", [2, 64, 512], BF16)
    gm_h = nc.dram_tensor("l0f_gm", [2, 2, 8, 2048], F32)

    for sq in range(2):
        t0 = sq * 16
        sc_seq = contextlib.ExitStack()
        avT = ctx.sbuf(f"l0f_avT{sq}", [128, 8 * 2048], BF16, sc_seq)
        cmt = ctx.sbuf(f"l0f_cmt{sq}", [128, 128], F32, sc_seq)

        sc_x = contextlib.ExitStack()
        sc_ln = contextlib.ExitStack()
        xlnT = ctx.sbuf(f"l0f_xlnT{sq}", [128, NCH * 2048], BF16, sc_x)
        with nc.Block() as blk:
            nev, s_ln = emit_ln_transpose(ctx, blk, f"l0f_ln1_{sq}",
                                          list(range(t0, t0 + 16)),
                                          xlnT, 2048, 0, scope=sc_ln, sem_scope=sc_x)
        sc_ln.close()

        sc_r = contextlib.ExitStack()
        with nc.Block() as blk:
            rsc = ctx.sbuf(f"l0f_rsc{sq}", [8, 2048], F32, sc_r)
            rwf = ctx.sbuf(f"l0f_rwf{sq}", [128, 64], F32, sc_r)
            rwb = ctx.sbuf(f"l0f_rwb{sq}", [128, 64], BF16, sc_r)
            bis = ctx.sbuf(f"l0f_bis{sq}", [8, 8], F32, sc_r)
            cmp = ctx.sbuf(f"l0f_cmp{sq}", [8, 2048], BF16, sc_r)
            sg = ctx.sbuf(f"l0f_sg{sq}", [8, 2048], F32, sc_r)
            msk = ctx.sbuf(f"l0f_msk{sq}", [8, 2048], F32, sc_r)
            gmb = ctx.sbuf(f"l0f_gmbf{sq}", [8, 2048], F32, sc_r)
            mskb = ctx.sbuf(f"l0f_mskb{sq}", [8, 2048], BF16, sc_r)
            prc = [ctx.psum(f"l0f_prc{sq}{j}", [8, 512], F32, sc_r) for j in range(2)]
            ptc = ctx.psum(f"l0f_ptc{sq}", [128, 128], BF16, sc_r)
            s_rl = ctx.sem()
            s_cst = ctx.sem(scope=sc_r)
            s_mm = ctx.sem(scope=sc_r)
            s_ev = ctx.sem(scope=sc_r)
            vf = ctx.sem(scope=sc_r)
            s_sg = ctx.sem(scope=sc_r)
            s_tm = ctx.sem(scope=sc_r)
            s_cme = ctx.sem(scope=sc_r)
            s_go = ctx.sem(scope=sc_r, kls="gdma")

            @blk.sync
            def _(e):
                prime(e, s_rl)
                for half in range(2):
                    s_rl.inc(e.dma_start(
                        rwf[:, half * 32:(half + 1) * 32],
                        bass.AP(rw_in, half * 4 * 128 * 8,
                                [[8, 128], [1024, 4], [1, 8]])), 16)

            @blk.vector
            def _(e):
                prime(e, vf, s_cst, s_mm, s_ev, s_sg, s_tm, s_go, s_cme)
                s_rl.wait(e, 32)
                s_cst.inc(e.tensor_copy(rwb[:, :], rwf[:, :]), 1)
                for tc in range(4):
                    s_mm.wait(e, tc + 1)
                    vf.inc(e.tensor_copy(rsc[:, tc * 512:(tc + 1) * 512],
                                         prc[tc % 2][:, :]), 1)
                v = 4
                vf.wait(e, v)
                vf.inc(e.memset(bis[:, 0:1], -16.0), 1); v += 1
                vf.wait(e, v)
                vf.inc(e.memset(bis[:, 1:2], 16.0), 1); v += 1
                for it in range(24):
                    vf.wait(e, v)
                    vf.inc(e.scalar_tensor_tensor(bis[:, 2:3], bis[:, 0:1], 1.0,
                                                  bis[:, 1:2], OP.bypass, OP.add), 1); v += 1
                    vf.wait(e, v)
                    vf.inc(e.tensor_scalar(bis[:, 2:3], bis[:, 2:3], 0.5, None,
                                           OP.mult), 1); v += 1
                    vf.wait(e, v)
                    vf.inc(e.tensor_scalar(cmp[:, :], rsc[:, :], bis[:, 2:3], None,
                                           OP.is_gt, OP.add, accum_out=bis[:, 3:4]), 1); v += 1
                    vf.wait(e, v)
                    vf.inc(e.tensor_scalar(bis[:, 4:5], bis[:, 3:4], 255.5, None,
                                           OP.is_gt), 1); v += 1
                    vf.wait(e, v)
                    vf.inc(e.tensor_sub(bis[:, 5:6], bis[:, 2:3], bis[:, 0:1]), 1); v += 1
                    vf.wait(e, v)
                    vf.inc(e.tensor_mul(bis[:, 6:7], bis[:, 5:6], bis[:, 4:5]), 1); v += 1
                    vf.wait(e, v)
                    vf.inc(e.tensor_add(bis[:, 0:1], bis[:, 0:1], bis[:, 6:7]), 1); v += 1
                    vf.wait(e, v)
                    vf.inc(e.tensor_sub(bis[:, 5:6], bis[:, 2:3], bis[:, 1:2]), 1); v += 1
                    vf.wait(e, v)
                    vf.inc(e.tensor_scalar(bis[:, 7:8], bis[:, 4:5], -1.0, 1.0,
                                           OP.mult, OP.add), 1); v += 1
                    vf.wait(e, v)
                    vf.inc(e.tensor_mul(bis[:, 6:7], bis[:, 5:6], bis[:, 7:8]), 1); v += 1
                    vf.wait(e, v)
                    vf.inc(e.tensor_add(bis[:, 1:2], bis[:, 1:2], bis[:, 6:7]), 1); v += 1
                vf.wait(e, v)
                vf.inc(e.tensor_scalar(msk[:, :], rsc[:, :], bis[:, 0:1], None,
                                       OP.is_gt), 1); v += 1
                vf.wait(e, v)
                vf.inc(e.tensor_copy(mskb[:, :], msk[:, :]), 1); v += 1
                s_sg.wait(e, 1)
                vf.wait(e, v)
                vf.inc(e.tensor_mul(gmb[:, :], sg[:, :], msk[:, :]), 1); v += 1
                s_tm.wait(e, 16)
                s_cme.inc(e.tensor_copy(cmt[:, :], ptc[:, :]), 1)

            @blk.scalar
            def _(e):
                prime(e, s_sg, vf)
                vf.wait(e, 4)
                s_sg.inc(e.activation(sg[:, :], rsc[:, :], AF.Sigmoid), 1)

            @blk.tensor
            def _(e):
                prime(e, s_mm, s_tm, s_cst, vf, s_ln, s_ev)
                s_ln.wait(e, nev)
                s_cst.wait(e, 1)
                for tc in range(4):
                    if tc >= 2:
                        vf.wait(e, tc - 1)
                    for ec in range(8):
                        ins = e.matmul(prc[tc % 2][:, :],
                                       rwb[:, ec * 8:(ec + 1) * 8],
                                       xlnT[:, ec * 2048 + tc * 512: ec * 2048 + (tc + 1) * 512],
                                       start=(ec == 0), stop=(ec == 7))
                    s_mm.inc(ins, 1)
                vf.wait(e, 6 + 24 * 11 + 2)
                for i in range(16):
                    ins = e.transpose(ptc[:, i * 8:(i + 1) * 8],
                                      mskb[0:8, i * 128:(i + 1) * 128], idn[0:8, 0:8])
                    s_tm.inc(ins, 1)

            @blk.gpsimd
            def _(e):
                prime(e, s_go, vf)
                vf.wait(e, 6 + 24 * 11 + 3)
                s_go.inc(e.dma_start(gm_h.ap()[sq, 0], gmb[:, :]), 16)
                s_go.wait(e, 16)
                s_go.inc(e.dma_start(gm_h.ap()[sq, 1], msk[:, :]), 16)
                s_go.wait(e, 32)
        sc_r.close()

        for br in range(2):
            fsc = contextlib.ExitStack()
            with nc.Block() as blk:
                wqb = ctx.sbuf(f"l0f_wqb{sq}{br}", [128, 8 * 1536], BF16, fsc)
                wstg = ctx.sbuf(f"l0f_wstg{sq}{br}", [128, 2 * 1536], F32, fsc)
                qkv = ctx.sbuf(f"l0f_qkv{sq}{br}", [128, 2 * 1536], BF16, fsc)
                rtmp = ctx.sbuf(f"l0f_rtmp{sq}{br}", [128, 4 * 128], BF16, fsc)
                cbuf = ctx.sbuf(f"l0f_cbuf{sq}{br}", [128, 32], F32, fsc)
                tstg = ctx.sbuf(f"l0f_tstg{sq}{br}", [128, 2 * 2 * 512], BF16, fsc)
                vstg = ctx.sbuf(f"l0f_vstg{sq}{br}", [128, 2 * 520], BF16, fsc)
                pj = [ctx.psum(f"l0f_pj{sq}{br}{j}", [128, 512], F32, fsc) for j in range(2)]
                ptr = [ctx.psum(f"l0f_ptr{sq}{br}{j}", [128, 512], BF16, fsc) for j in range(2)]
                s_wl = [ctx.sem(scope=fsc) for _ in range(2)]
                s_wc = ctx.sem(scope=fsc)
                s_pj = ctx.sem(scope=fsc)
                s_qe = ctx.sem(scope=fsc)
                s_rp = ctx.sem(scope=fsc)
                s_tr = ctx.sem(scope=fsc)
                s_te = ctx.sem(scope=fsc)
                s_va = ctx.sem(scope=fsc)
                s_od = [ctx.sem(scope=fsc, kls="gdma") for _ in range(2)]
                s_vo = [ctx.sem(scope=fsc, kls="gdma") for _ in range(2)]

                @blk.sync
                def _(e):
                    prime(e, s_wl, s_wc)
                    for ec in range(8):
                        if ec >= 2:
                            s_wc.wait(e, ec - 1)
                        s_wl[ec % 2].inc(e.dma_start(
                            wstg[:, (ec % 2) * 1536:(ec % 2 + 1) * 1536],
                            wqkv_in[br].ap()[ec * 128:(ec + 1) * 128, :]), 16)

                @blk.vector
                def _(e):
                    prime(e, s_wl, s_wc, s_pj, s_qe, s_rp, s_od, s_va, *s_vo)
                    for ec in range(8):
                        s_wl[ec % 2].wait(e, 16 * (ec // 2 + 1))
                        s_wc.inc(e.tensor_copy(wqb[:, ec * 1536:(ec + 1) * 1536],
                                               wstg[:, (ec % 2) * 1536:(ec % 2 + 1) * 1536]), 1)
                    for i in range(16):
                        sl = i % 2
                        base = sl * 1536
                        s_qe.wait(e, 3 * (i + 1))
                        s_rp.wait(e, 14 * i)
                        s_rp.inc(e.tensor_add(cbuf[:, 0:16],
                                              ctx.cb["cos_t"][:, i * 16:(i + 1) * 16],
                                              ctx.cb["cos_lo"][:, i * 16:(i + 1) * 16]), 1)
                        s_rp.wait(e, 14 * i + 1)
                        s_rp.inc(e.tensor_add(cbuf[:, 16:32],
                                              ctx.cb["sin_t"][:, i * 16:(i + 1) * 16],
                                              ctx.cb["sin_lo"][:, i * 16:(i + 1) * 16]), 1)
                        for qk in range(2):
                            off = base + qk * 64
                            cs = bass.AP(cbuf, 0, [[32, 128], [0, 8], [1, 16]])
                            sn = bass.AP(cbuf, 16, [[32, 128], [0, 8], [1, 16]])
                            x1 = bass.AP(qkv, off, [[3072, 128], [192, 8], [1, 16]])
                            x2 = bass.AP(qkv, off + 16, [[3072, 128], [192, 8], [1, 16]])
                            t1 = bass.AP(rtmp, 0, [[512, 128], [16, 8], [1, 16]])
                            t2 = bass.AP(rtmp, 128, [[512, 128], [16, 8], [1, 16]])
                            t3 = bass.AP(rtmp, 256, [[512, 128], [16, 8], [1, 16]])
                            t4 = bass.AP(rtmp, 384, [[512, 128], [16, 8], [1, 16]])
                            rb = 14 * i + 2 + 6 * qk
                            s_rp.wait(e, rb)
                            s_rp.inc(e.tensor_tensor(t1, x1, cs, OP.mult), 1)
                            s_rp.wait(e, rb + 1)
                            s_rp.inc(e.tensor_tensor(t2, x1, sn, OP.mult), 1)
                            s_rp.wait(e, rb + 2)
                            s_rp.inc(e.tensor_tensor(t3, x2, sn, OP.mult), 1)
                            s_rp.wait(e, rb + 3)
                            s_rp.inc(e.tensor_tensor(t4, x2, cs, OP.mult), 1)
                            s_rp.wait(e, rb + 4)
                            s_rp.inc(e.tensor_tensor(x1, t1, t3, OP.subtract), 1)
                            s_rp.wait(e, rb + 5)
                            s_rp.inc(e.tensor_tensor(x2, t2, t4, OP.add), 1)
                        if i >= 2:
                            s_vo[i % 2].wait(e, 16 * (i // 2))
                        for h in range(8):
                            vsl = bass.AP(vstg, (i % 2) * 520 + h * 65, [[1040, 128], [1, 64]])
                            vsrc = qkv[:, base + h * 192 + 128: base + h * 192 + 192]
                            if br == 0:
                                s_va.inc(e.tensor_scalar(vsl, vsrc,
                                                         cmt[:, i * 8 + h:i * 8 + h + 1],
                                                         None, OP.mult), 1)
                                s_va.inc(e.tensor_copy(
                                    bass.AP(vstg, (i % 2) * 520 + h * 65 + 64,
                                            [[1040, 128], [1, 1]]),
                                    cmt[:, i * 8 + h:i * 8 + h + 1]), 1)
                            else:
                                s_va.inc(e.tensor_copy(vsl, vsrc), 1)
                                s_va.inc(e.memset(
                                    bass.AP(vstg, (i % 2) * 520 + h * 65 + 64,
                                            [[1040, 128], [1, 1]]), 1.0), 1)

                @blk.tensor
                def _(e):
                    prime(e, s_pj, s_tr, s_wc, s_qe, s_rp, s_te, s_ln)
                    s_wc.wait(e, 8)
                    s_ln.wait(e, nev)

                    def transposes(i):
                        for qk in range(2):
                            kt = 2 * i + qk
                            buf = ptr[kt % 2]
                            if kt >= 2:
                                s_te.wait(e, kt - 1)
                            for hp in range(4):
                                for par in range(2):
                                    h = 2 * hp + par
                                    ins = e.transpose(
                                        buf[par * 64:par * 64 + 64, hp * 128:(hp + 1) * 128],
                                        qkv[:, (i % 2) * 1536 + h * 192 + qk * 64:
                                            (i % 2) * 1536 + h * 192 + qk * 64 + 64],
                                        idn[:, :])
                            s_tr.inc(ins, 1)

                    k = 0
                    for i in range(16):
                        for cc in range(3):
                            if k >= 2:
                                s_qe.wait(e, k - 1)
                            for ec in range(8):
                                ins = e.matmul(
                                    pj[k % 2][:, :],
                                    xlnT[:, ec * 2048 + i * 128: ec * 2048 + (i + 1) * 128],
                                    wqb[:, ec * 1536 + cc * 512: ec * 1536 + (cc + 1) * 512],
                                    start=(ec == 0), stop=(ec == 7))
                            s_pj.inc(ins, 1)
                            k += 1
                        if i >= 1:
                            s_rp.wait(e, 14 * i)
                            transposes(i - 1)
                    s_rp.wait(e, 14 * 16)
                    transposes(15)

                @blk.scalar
                def _(e):
                    prime(e, s_pj, s_qe, s_tr, s_te, s_od, s_rp, s_va)

                    def tstg_evac(kt):
                        s_tr.wait(e, kt + 1)
                        if kt >= 2:
                            s_od[kt % 2].wait(e, 16 * 4 * (kt // 2))
                        s_te.inc(e.activation(tstg[:, (kt % 2) * 512:(kt % 2 + 1) * 512],
                                              ptr[kt % 2][:, :], AF.Copy), 1)

                    for i in range(16):
                        for cc in range(3):
                            k = 3 * i + cc
                            s_pj.wait(e, k + 1)
                            if i >= 2 and cc == 0:
                                s_rp.wait(e, 14 * (i - 1))
                                s_va.wait(e, 16 * (i - 1))
                                s_tr.wait(e, 2 * (i - 1))
                            s_qe.inc(e.activation(
                                qkv[:, (i % 2) * 1536 + cc * 512:
                                    (i % 2) * 1536 + (cc + 1) * 512],
                                pj[k % 2][:, :], AF.Copy), 1)
                        if i >= 1:
                            tstg_evac(2 * (i - 1))
                            tstg_evac(2 * (i - 1) + 1)
                    tstg_evac(30)
                    tstg_evac(31)

                @blk.gpsimd
                def _(e):
                    prime(e, s_od, s_te, s_va, *s_vo)
                    for i in range(16):
                        for qk in range(2):
                            kt = 2 * i + qk
                            dst = qt_h if qk == 0 else kt_h
                            s_te.wait(e, kt + 1)
                            if kt >= 2:
                                s_od[kt % 2].wait(e, 16 * 4 * (kt // 2))
                            for hp in range(4):
                                s_od[kt % 2].inc(e.dma_start(
                                    bass.AP(dst, ((sq * 2 + br) * 8 + 2 * hp) * 64 * 2048 + i * 128,
                                            [[2048, 128], [1, 128]]),
                                    tstg[:, (kt % 2) * 512 + hp * 128:
                                         (kt % 2) * 512 + (hp + 1) * 128]), 16)
                        s_va.wait(e, 16 * (i + 1))
                        if i >= 2:
                            s_vo[i % 2].wait(e, 16 * (i // 2))
                        s_vo[i % 2].inc(e.dma_start(
                            va_h.ap()[sq, br, i],
                            vstg[:, (i % 2) * 520:(i % 2 + 1) * 520]), 16)
                    for sl in range(2):
                        s_od[sl].wait(e, 16 * 64)
                        s_vo[sl].wait(e, 16 * 8)
            fsc.close()

        sc_x.close()

        sc_a = contextlib.ExitStack()
        rstage = ctx.sbuf(f"l0f_rstage{sq}", [1, 2 * 512], BF16, sc_a)
        causb = ctx.sbuf(f"l0f_causb{sq}", [128, 4 * 512], BF16, sc_a)
        qs = ctx.sbuf(f"l0f_qs{sq}", [64, 3 * 2048], BF16, sc_a)
        ks = ctx.sbuf(f"l0f_ks{sq}", [64, 3 * 2048], BF16, sc_a)
        vab = ctx.sbuf(f"l0f_vab{sq}", [128, 3 * 1040], BF16, sc_a)
        ptb = ctx.sbuf(f"l0f_ptb{sq}", [128, 16 * 512], BF16, sc_a)
        with nc.Block() as blk:
            psc = [ctx.psum(f"l0f_psc{sq}{j}", [128, 1024], F32, sc_a) for j in range(2)]
            pav = ctx.psum(f"l0f_pav{sq}", [65, 512], F32, sc_a)
            s_hl = [ctx.sem(scope=sc_a) for _ in range(3)]
            s_sc = ctx.sem(scope=sc_a)
            s_ex = ctx.sem(scope=sc_a)
            s_mk = ctx.sem(scope=sc_a)
            s_av = ctx.sem(scope=sc_a)
            s_ae = ctx.sem(scope=sc_a)
            s_rst = ctx.sem(scope=sc_a)
            s_rod = [ctx.sem(scope=sc_a, kls="gdma") for _ in range(2)]

            def sc_count(hh):
                return 40 * hh

            s_cl = ctx.sem()

            @blk.sync
            def _(e):
                prime(e, s_hl, s_av, s_cl)
                for ci in range(4):
                    s_cl.inc(e.dma_start(causb[:, ci * 512:(ci + 1) * 512],
                                         ctx.cdram[f"caus{ci}"].ap()), 16)
                for hh in range(16):
                    br, h = divmod(hh, 8)
                    if hh >= 3:
                        s_av.wait(e, 4 * (hh - 2))
                    s_hl[hh % 3].inc(e.dma_start(
                        qs[:, (hh % 3) * 2048:(hh % 3 + 1) * 2048],
                        qt_h.ap()[sq, br, h]), 16)
                    s_hl[hh % 3].inc(e.dma_start(
                        ks[:, (hh % 3) * 2048:(hh % 3 + 1) * 2048],
                        kt_h.ap()[sq, br, h]), 16)
                    s_hl[hh % 3].inc(e.dma_start(
                        vab[:, (hh % 3) * 1040:(hh % 3 + 1) * 1040],
                        bass.AP(va_h, ((sq * 2 + br) * 16) * 128 * 520 + h * 65,
                                [[520, 128], [128 * 520, 16], [1, 65]])), 16)

            @blk.tensor
            def _(e):
                prime(e, s_sc, s_av, s_hl, s_ex, s_ae, s_mk)
                kk = 0
                for hh in range(16):
                    br, h = divmod(hh, 8)
                    s_hl[hh % 3].wait(e, 48 * (hh // 3 + 1))
                    for tc in range(4):
                        g = hh * 4 + tc
                        nts = 4 * tc + 4
                        for ng in range(nts // 2):
                            if kk >= 2:
                                s_ex.wait(e, kk - 1)
                            for j in range(2):
                                nt = 2 * ng + j
                                ins = e.matmul(
                                    psc[kk % 2][:, j * 512:(j + 1) * 512],
                                    ks[:, (hh % 3) * 2048 + nt * 128:
                                       (hh % 3) * 2048 + (nt + 1) * 128],
                                    qs[:, (hh % 3) * 2048 + tc * 512:
                                       (hh % 3) * 2048 + (tc + 1) * 512],
                                    start=True, stop=True)
                            s_sc.inc(ins, 1)
                            kk += 1
                        s_ae.wait(e, g)
                        kk0 = kk - nts // 2
                        for nt in range(nts):
                            if nt >= 4 * tc:
                                s_mk.wait(e, 4 * g + (nt - 4 * tc) + 1)
                            else:
                                s_ex.wait(e, kk0 + nt // 2 + 1)
                            ins = e.matmul(
                                pav[:, :],
                                vab[:, (hh % 3) * 1040 + nt * 65:
                                    (hh % 3) * 1040 + (nt + 1) * 65],
                                ptb[:, nt * 512:(nt + 1) * 512],
                                start=(nt == 0), stop=(nt == nts - 1))
                        s_av.inc(ins, 1)

            @blk.scalar
            def _(e):
                prime(e, s_ex, s_sc)
                kk = 0
                for hh in range(16):
                    for tc in range(4):
                        nts = 4 * tc + 4
                        for ng in range(nts // 2):
                            s_sc.wait(e, kk + 1)
                            s_ex.inc(e.activation(ptb[:, ng * 1024:(ng + 1) * 1024],
                                                  psc[kk % 2][:, :], AF.Exp,
                                                  scale=0.125), 1)
                            kk += 1

            @blk.vector
            def _(e):
                prime(e, s_av, s_ae, s_rst, s_rod, s_ex, s_mk)
                s_cl.wait(e, 64)
                kk = 0
                for hh in range(16):
                    for tc in range(4):
                        g = hh * 4 + tc
                        nts = 4 * tc + 4
                        kk += nts // 2
                        for dd in range(4):
                            nt = 4 * tc + dd
                            s_ex.wait(e, kk - nts // 2 + nt // 2 + 1)
                            s_mk.inc(e.tensor_tensor(
                                ptb[:, nt * 512:(nt + 1) * 512],
                                ptb[:, nt * 512:(nt + 1) * 512],
                                causb[:, dd * 512:(dd + 1) * 512], OP.mult), 1)
                        s_av.wait(e, g + 1)
                        if g >= 2:
                            s_rod[g % 2].wait(e, 16 * (g // 2))
                        s_rst.inc(e.tensor_copy(
                            rstage[0:1, (g % 2) * 512:(g % 2 + 1) * 512],
                            pav[64:65, :]), 1)
                        s_ae.inc(e.tensor_copy(
                            avT[(hh % 2) * 64:(hh % 2) * 64 + 64,
                                (hh // 2) * 2048 + tc * 512:(hh // 2) * 2048 + (tc + 1) * 512],
                            pav[0:64, :]), 1)

            @blk.gpsimd
            def _(e):
                prime(e, s_rod, s_rst)
                for g in range(64):
                    s_rst.wait(e, g + 1)
                    s_rod[g % 2].inc(e.dma_start(
                        rs_h.ap()[sq, g:g + 1, :],
                        rstage[0:1, (g % 2) * 512:(g % 2 + 1) * 512]), 16)
                for sl in range(2):
                    s_rod[sl].wait(e, 16 * 32)
        sc_a.close()

        sc_p4 = contextlib.ExitStack()
        wob = ctx.sbuf(f"l0f_wob{sq}", [128, 8 * 1024], BF16, sc_p4)
        wstg4 = ctx.sbuf(f"l0f_wstg4{sq}", [128, 2 * 1024], F32, sc_p4)
        rows = ctx.sbuf(f"l0f_rows{sq}", [64, 512], BF16, sc_p4)
        gm32 = ctx.sbuf(f"l0f_gm32{sq}", [32, 512], F32, sc_p4)
        mk32 = ctx.sbuf(f"l0f_mk32{sq}", [32, 512], F32, sc_p4)
        scl = ctx.sbuf(f"l0f_scl{sq}", [64, 512], F32, sc_p4)
        mrec = ctx.sbuf(f"l0f_mrec{sq}", [32, 512], F32, sc_p4)
        selb2 = ctx.sbuf(f"l0f_selb{sq}", [64, 32 * 128], BF16, sc_p4)
        sclb = ctx.sbuf(f"l0f_sclb{sq}", [64, 512], BF16, sc_p4)
        outb = ctx.sbuf(f"l0f_outb{sq}", [128, 8 * 512], BF16, sc_p4)
        s_oev = ctx.sem(scope=sc_seq)
        sc_nrm = contextlib.ExitStack()
        with nc.Block() as blk:
            pbc = [ctx.psum(f"l0f_pbc{sq}{j}", [128, 512], F32, sc_nrm) for j in range(2)]
            s_rl = ctx.sem()
            s_rc = ctx.sem(scope=sc_nrm)
            s_bc = ctx.sem(scope=sc_nrm)
            s_nm = ctx.sem(scope=sc_nrm)
            s_wl = [ctx.sem(scope=sc_p4) for _ in range(2)]
            s_wc = ctx.sem(scope=sc_p4)

            @blk.sync
            def _(e):
                prime(e, s_wl, s_wc, s_rl)
                s_rl.inc(e.dma_start(rows[:, :], rs_h.ap()[sq]), 16)
                s_rl.wait(e, 16)
                s_rl.inc(e.dma_start(gm32[:, :],
                                     bass.AP(gm_h, sq * 2 * 8 * 2048,
                                             [[2048, 8], [512, 4], [1, 512]])), 16)
                s_rl.wait(e, 32)
                s_rl.inc(e.dma_start(mk32[:, :],
                                     bass.AP(gm_h, sq * 2 * 8 * 2048 + 8 * 2048,
                                             [[2048, 8], [512, 4], [1, 512]])), 16)
                s_rl.wait(e, 48)
                s_rl.inc(e.dma_start(selb2[:, :], ctx.cdram["self"].ap()), 16)
                for br in range(2):
                    for ec4 in range(4):
                        ec = br * 4 + ec4
                        if ec >= 2:
                            s_wc.wait(e, ec - 1)
                        s_wl[ec % 2].inc(e.dma_start(
                            wstg4[:, (ec % 2) * 1024:(ec % 2 + 1) * 1024],
                            wo_in[br].ap()[ec4 * 128:(ec4 + 1) * 128, :]), 16)

            @blk.vector
            def _(e):
                prime(e, s_wl, s_wc, s_rc, s_bc, s_nm)
                for ec in range(8):
                    s_wl[ec % 2].wait(e, 16 * (ec // 2 + 1))
                    s_wc.inc(e.tensor_copy(wob[:, ec * 1024:(ec + 1) * 1024],
                                           wstg4[:, (ec % 2) * 1024:(ec % 2 + 1) * 1024]), 1)
                s_rl.wait(e, 48)
                v = 0
                s_rc.inc(e.tensor_scalar(scl[0:32, :], rows[0:32, :], 1.0, None,
                                         OP.add), 1); v += 1
                s_rc.wait(e, v)
                s_rc.inc(e.tensor_sub(scl[0:32, :], scl[0:32, :], mk32[:, :]), 1); v += 1
                s_rc.wait(e, v)
                s_rc.inc(e.reciprocal(mrec[:, :], scl[0:32, :]), 1); v += 1
                s_rc.wait(e, v)
                s_rc.inc(e.tensor_mul(scl[0:32, :], gm32[:, :], mrec[:, :]), 1); v += 1
                s_rc.wait(e, v)
                s_rc.inc(e.reciprocal(scl[32:64, :], rows[32:64, :]), 1); v += 1
                s_rc.wait(e, v)
                s_rc.inc(e.tensor_copy(sclb[:, :], scl[:, :]), 1); v += 1
                for j in range(32):
                    br, rem = divmod(j, 16)
                    p, tc = divmod(rem, 4)
                    s_bc.wait(e, j + 1)
                    s_nm.inc(e.tensor_tensor(
                        avT[:, (br * 4 + p) * 2048 + tc * 512:
                            (br * 4 + p) * 2048 + (tc + 1) * 512],
                        avT[:, (br * 4 + p) * 2048 + tc * 512:
                            (br * 4 + p) * 2048 + (tc + 1) * 512],
                        pbc[j % 2][:, :], OP.mult), 1)

            @blk.tensor
            def _(e):
                prime(e, s_rc, s_bc, s_nm)
                s_rc.wait(e, 6)
                s_rl.wait(e, 64)
                for j in range(32):
                    if j >= 2:
                        s_nm.wait(e, j - 1)
                    s_bc.inc(e.matmul(pbc[j % 2][:, :],
                                      selb2[:, j * 128:(j + 1) * 128],
                                      sclb[:, :], start=True, stop=True), 1)
        sc_nrm.close()
        for tc in range(4):
            wsc2 = contextlib.ExitStack()
            with nc.Block() as blk:
                pw = [ctx.psum(f"l0f_pw{sq}{tc}{j}", [128, 512], F32, wsc2) for j in range(8)]
                s_wm = ctx.sem(scope=wsc2)

                @blk.tensor
                def _(e):
                    prime(e, s_wm, s_wc)
                    s_wc.wait(e, 8)
                    for eci in range(8):
                        for hd8 in range(8):
                            br, hdc = divmod(hd8, 4)
                            ins = e.matmul(
                                pw[eci][:, :],
                                wob[:, (br * 4 + hdc) * 1024 + eci * 128:
                                    (br * 4 + hdc) * 1024 + (eci + 1) * 128],
                                avT[:, (br * 4 + hdc) * 2048 + tc * 512:
                                    (br * 4 + hdc) * 2048 + (tc + 1) * 512],
                                start=(hd8 == 0), stop=(hd8 == 7))
                        s_wm.inc(ins, 1)

                @blk.scalar
                def _(e):
                    prime(e, s_oev, s_wm)
                    for eci in range(8):
                        s_wm.wait(e, eci + 1)
                        s_oev.inc(e.activation(outb[:, eci * 512:(eci + 1) * 512],
                                               pw[eci][:, :], AF.Copy), 1)
            wsc2.close()
            emit_out_ln_add(ctx, f"l0f_o{sq}_{tc}", outb, 512, sq * 2048 + tc * 512)
        sc_p4.close()
        sc_seq.close()


def build_program(upto="all", taps=()):
    stack = contextlib.ExitStack()
    nc = bass.Bass("TRN2", target_bir_lowering=False)
    ctx = Ctx(nc, stack)
    ctx.tap_names = tuple(taps)

    x_in = nc.dram_tensor("x", [T_TOK, E], F32, kind="ExternalInput")
    y_out = nc.dram_tensor("y", [T_TOK, E], F32, kind="ExternalOutput")
    cdefs = host_constants()
    cdram = {}
    for k, v in cdefs.items():
        dt = BF16 if v.dtype == BF else F32
        cdram[k] = nc.dram_tensor(k, list(v.shape), dt, kind="ExternalInput")

    ctx.cdram = cdram
    for k, v in cdefs.items():
        if k in ("seln", "self", "caus0", "caus1", "caus2", "caus3"):
            continue
        dt = BF16 if v.dtype == BF else F32
        ctx.cb[k] = ctx.sbuf("c_" + k, list(v.shape), dt)
    ctx.b["idn"] = ctx.cb["idn"]
    xres = ctx.sbuf("xres", [128, N_TILE * 1024], BF16)
    ctx.sbuf("g_lnscr", [128, 2 * 1024], BF16)
    sc_ph0 = contextlib.ExitStack()
    stage = ctx.sbuf("stage", [128, 4 * 1024], F32, sc_ph0)

    s_xd = [ctx.sem(f"xd{j}") for j in range(4)]
    s_xc = ctx.sem("xc")
    s_cd = ctx.sem("cd")
    ncst = len(cdefs)
    with nc.Block() as blk:
        @blk.sync
        def _(e):
            for i in range(N_TILE):
                if i >= 4:
                    s_xc.wait(e, i - 3)
                s_xd[i % 4].inc(e.dma_start(
                    stage[:, (i % 4) * 1024:(i % 4 + 1) * 1024],
                    x_in.ap()[i * 128:(i + 1) * 128, :]), 16)

        @blk.gpsimd
        def _(e):
            for k in cdefs:
                if k in ("seln", "self", "caus0", "caus1", "caus2", "caus3"):
                    continue
                s_cd.inc(e.dma_start(ctx.cb[k][:, :], cdram[k].ap()), 16)
            s_cd.wait(e, 16 * (ncst - 6))

        @blk.vector
        def _(e):
            for i in range(N_TILE):
                s_xd[i % 4].wait(e, 16 * (i // 4 + 1))
                s_xc.inc(e.tensor_copy(
                    tile(xres, i),
                    stage[:, (i % 4) * 1024:(i % 4 + 1) * 1024]), 1)

    sc_ph0.close()
    if upto != "copy":
        order = ["l0a", "l0f", "l1a", "l1f"]
        emits = {"l0a": emit_l0_attn, "l0f": emit_l0_ffn,
                 "l1a": emit_l1_attn, "l1f": emit_l1_ffn}
        sel = order if upto == "all" else upto.split(",")
        for s in sel:
            emits[s](ctx)

    s_oc = ctx.sem("oc")
    s_od = [ctx.sem("od0"), ctx.sem("od1")]
    sc_out = contextlib.ExitStack()
    stage = ctx.sbuf("ostage", [128, 2 * 1024], F32, sc_out)
    with nc.Block() as blk:
        @blk.scalar
        def _(e):
            for i in range(N_TILE):
                if i >= 2:
                    s_od[i % 2].wait(e, 16 * (i // 2))
                s_oc.inc(e.activation(stage[:, (i % 2) * 1024:(i % 2 + 1) * 1024],
                                      tile(xres, i), AF.Copy), 1)

        @blk.sync
        def _(e):
            for i in range(N_TILE):
                s_oc.wait(e, i + 1)
                s_od[i % 2].inc(e.dma_start(
                    y_out.ap()[i * 128:(i + 1) * 128, :],
                    stage[:, (i % 2) * 1024:(i % 2 + 1) * 1024]), 16)
            s_od[0].wait(e, 16 * 16)
            s_od[1].wait(e, 16 * 16)
    sc_out.close()

    nc._ctx_keepalive = stack
    return nc, ctx


def make_in_maps(inputs, upto="all"):
    cdefs = host_constants()
    x = np.ascontiguousarray(np.asarray(inputs["L0_attn_x"], dtype=np.float32))
    sel = ["l0a", "l0f", "l1a", "l1f"] if upto == "all" else upto.split(",")
    shared = dict(cdefs)
    if "l0a" in sel:
        shared["l0a_wq"] = np.ascontiguousarray(np.asarray(inputs["L0_attn_Wq"], dtype=np.float32))
        shared["l0a_wk"] = np.ascontiguousarray(np.asarray(inputs["L0_attn_Wk"], dtype=np.float32))
        shared["l0a_wv"] = np.ascontiguousarray(np.asarray(inputs["L0_attn_Wv"], dtype=np.float32))
        shared["l0a_wo"] = np.ascontiguousarray(np.asarray(inputs["L0_attn_Wo"], dtype=np.float32))
    if "l0f" in sel:
        shared["l0f_rw"] = np.ascontiguousarray(np.asarray(inputs["L0_ffn_mosa_router_w"], dtype=np.float32))
        shared["l0f_mqkv"] = np.ascontiguousarray(np.asarray(inputs["L0_ffn_mosa_wqkv"], dtype=np.float32).reshape(1024, 1536))
        shared["l0f_dqkv"] = np.ascontiguousarray(np.asarray(inputs["L0_ffn_dense_wqkv"], dtype=np.float32).reshape(1024, 1536))
        shared["l0f_mwo"] = np.ascontiguousarray(np.asarray(inputs["L0_ffn_mosa_wo"], dtype=np.float32))
        shared["l0f_dwo"] = np.ascontiguousarray(np.asarray(inputs["L0_ffn_dense_wo"], dtype=np.float32))
    if "l1a" in sel:
        shared["l1a_k"] = np.ascontiguousarray(np.asarray(inputs["L1_attn_k"], dtype=np.float32))
        shared["l1a_v"] = np.ascontiguousarray(np.asarray(inputs["L1_attn_v"], dtype=np.float32))
    if "l1f" in sel:
        UTr, Vr = host_l1f_factors(inputs["L1_ffn_G1_R"], inputs["L1_ffn_G2_R"])
        UTi, Vi = host_l1f_factors(inputs["L1_ffn_G1_I"], inputs["L1_ffn_G2_I"])
        shared["l1f_ut"] = np.ascontiguousarray(np.stack([UTr, UTi]))
        shared["l1f_v"] = np.ascontiguousarray(np.stack([Vr, Vi]))
        shared["l1f_br"] = np.ascontiguousarray(np.asarray(inputs["L1_ffn_bias_real"], dtype=np.float32))
        shared["l1f_bi"] = np.ascontiguousarray(np.asarray(inputs["L1_ffn_bias_imag"], dtype=np.float32))
    in_maps = []
    for c in range(8):
        m = {"x": np.ascontiguousarray(x[2 * c:2 * c + 2].reshape(T_TOK, E))}
        m.update(shared)
        in_maps.append(m)
    return in_maps


def run(inputs, upto="all", taps=()):
    nc, ctx = build_program(upto=upto, taps=taps)
    in_maps = make_in_maps(inputs, upto=upto)
    res = run_bass_kernel_spmd(nc, in_maps, core_ids=list(range(8)))
    return res


def kernel(**inputs):
    res = run(inputs, upto="all")
    out = np.concatenate([r["y"].reshape(2, S_LEN, E) for r in res.results], axis=0)
    return out.astype(np.float32)
